# version 12 (final)
# baseline (speedup 1.0000x reference)
import numpy as np
import ml_dtypes

import concourse.bass as bass
import concourse.mybir as mybir

F32 = mybir.dt.float32
BF16 = mybir.dt.bfloat16
AX = mybir.AxisListType.X
ALU = mybir.AluOpType
ACTF = mybir.ActivationFunctionType

P = 128
S = 2048
D = 1024
NH = 16
HD = 64
NDC = D // P
NSC = S // P
NSB = S // 512
E = 8
H0 = 2048
NHC0 = H0 // P
G = 16
LD = 256
LHN = 4
NLAT = 64
LCH = 32
H1 = 64
EPS = 1e-5
BF = ml_dtypes.bfloat16


def _host_consts():
    c = {}
    inv_freq = 1.0 / (10000.0 ** (np.arange(0, HD, 2, dtype=np.float64) / HD))
    pos = np.arange(S, dtype=np.float64)
    emb = np.repeat(pos[:, None] * inv_freq[None, :], 2, axis=1)
    cos64 = np.cos(emb).T
    sin64 = np.sin(emb).T
    c["rope_cos"] = np.concatenate([cos64, cos64], 0).astype(BF)
    c["rope_sin"] = np.concatenate([sin64, sin64], 0).astype(BF)
    c["eye"] = np.eye(P, dtype=BF)
    rot = np.zeros((P, P), dtype=np.float32)
    for h in range(2):
        for i in range(HD // 2):
            rot[64 * h + 2 * i + 1, 64 * h + 2 * i] = -1.0
            rot[64 * h + 2 * i, 64 * h + 2 * i + 1] = 1.0
    c["rotm"] = rot.astype(BF)
    kk, qq = np.meshgrid(np.arange(P), np.arange(P), indexing="ij")
    c["m_causal"] = (kk <= qq).astype(BF)
    c["m_upper"] = (qq > kk).astype(BF)
    c["m_lowereq"] = (qq <= kk).astype(BF)
    ll, ss = np.meshgrid(np.arange(NLAT), np.arange(S), indexing="ij")
    c["m_lat"] = (ll < (ss // LCH)).astype(BF)
    bo = np.zeros((P, 2), dtype=np.float32)
    bo[0:64, 0] = 1.0
    bo[64:128, 1] = 1.0
    c["blockones"] = bo.astype(BF)
    e8 = np.zeros((E, E * H1), dtype=np.float32)
    for e in range(E):
        e8[e, 64 * e:64 * e + 64] = 1.0
    c["exp8"] = e8.astype(BF)
    pa = np.zeros((P, 4), dtype=np.float32)
    for j in range(4):
        pa[32 * j:32 * j + 32, j] = 1.0 / LCH
    c["poolavg"] = pa.astype(BF)
    sel = np.zeros((E, E * P), dtype=np.float32)
    for e in range(E):
        sel[e, e * P:(e + 1) * P] = 1.0
    c["sel8"] = sel.astype(BF)
    s16 = np.zeros((G, G * HD), dtype=np.float32)
    for r in range(G):
        s16[r, r * HD:(r + 1) * HD] = 1.0
    c["sel16"] = s16.astype(BF)
    c["eye32"] = np.eye(P, dtype=np.float32)
    return c


def _blocks_T(w):
    di, do = w.shape
    return np.ascontiguousarray(
        w.reshape(di // P, P, do // P, P).transpose(2, 1, 0, 3).reshape(
            do // P, P, di)).astype(np.float32)


def _prep_inputs(ins):
    d = {}
    f32 = np.float32
    d["wqT"] = _blocks_T(np.ascontiguousarray(np.asarray(ins["L0_attn_Wq"]).T))
    d["wkT"] = _blocks_T(np.ascontiguousarray(np.asarray(ins["L0_attn_Wk"]).T))
    d["wvT"] = np.ascontiguousarray(np.asarray(ins["L0_attn_Wv"]).T).astype(f32)
    d["woT"] = np.ascontiguousarray(np.asarray(ins["L0_attn_Wo"]).T).astype(f32)
    d["w_red"] = np.ascontiguousarray(ins["L0_ffn_w_red"]).astype(f32)
    d["wg"] = np.ascontiguousarray(ins["L0_ffn_wg"]).astype(f32)
    w1 = np.asarray(ins["L0_ffn_W1"])
    d["w1"] = np.ascontiguousarray(
        w1.reshape(E, NDC, P, NHC0, P).transpose(0, 3, 2, 1, 4).reshape(
            E, NHC0, P, D)).astype(f32)
    w2 = np.asarray(ins["L0_ffn_W2"])
    d["w2"] = np.ascontiguousarray(
        w2.reshape(E, NHC0, P, NDC, P).transpose(0, 3, 2, 1, 4).reshape(
            E, NDC, P, H0)).astype(f32)
    d["b1"] = np.ascontiguousarray(ins["L0_ffn_b1"]).astype(f32)
    d["b2"] = np.ascontiguousarray(ins["L0_ffn_b2"]).astype(f32)
    d["qkv_w"] = _blocks_T(np.asarray(ins["L1_attn_qkv_w"]))
    d["qkv_n"] = np.ascontiguousarray(ins["L1_attn_qkv_w"]).astype(f32)
    d["qkv_b"] = np.ascontiguousarray(ins["L1_attn_qkv_b"]).astype(f32)
    d["rq_w"] = _blocks_T(np.asarray(ins["L1_attn_rq_w"]))
    d["rq_b"] = np.ascontiguousarray(ins["L1_attn_rq_b"]).astype(f32)
    d["rkv_w"] = np.ascontiguousarray(ins["L1_attn_rkv_w"]).astype(f32)
    d["rkv_b"] = np.ascontiguousarray(ins["L1_attn_rkv_b"]).astype(f32)
    d["rout_w"] = _blocks_T(np.asarray(ins["L1_attn_rout_w"]))
    d["rout_wT"] = _blocks_T(np.ascontiguousarray(np.asarray(ins["L1_attn_rout_w"]).T))
    d["rout_b"] = np.ascontiguousarray(ins["L1_attn_rout_b"]).astype(f32)
    d["out_w"] = np.ascontiguousarray(ins["L1_attn_out_w"]).astype(f32)
    d["out_b"] = np.ascontiguousarray(ins["L1_attn_out_b"]).astype(f32)
    d["gl_w"] = np.ascontiguousarray(ins["L1_attn_gl_w"]).astype(f32)
    d["gr_w"] = np.ascontiguousarray(ins["L1_attn_gr_w"]).astype(f32)
    d["ln_g"] = np.ascontiguousarray(ins["L1_ffn_ln_g"]).astype(f32)
    d["ln_b"] = np.ascontiguousarray(ins["L1_ffn_ln_b"]).astype(f32)
    d["bn_g"] = np.ascontiguousarray(ins["L1_ffn_bn_g"]).astype(f32)
    d["bn_b"] = np.ascontiguousarray(ins["L1_ffn_bn_b"]).astype(f32)
    d["bn_mean"] = np.ascontiguousarray(ins["L1_ffn_bn_mean"]).astype(f32)
    d["bn_var"] = np.ascontiguousarray(ins["L1_ffn_bn_var"]).astype(f32)
    d["wr"] = np.ascontiguousarray(ins["L1_ffn_Wr"]).astype(f32)
    d["br"] = np.ascontiguousarray(ins["L1_ffn_br"]).astype(f32)
    d["wd"] = np.ascontiguousarray(
        np.asarray(ins["L1_ffn_W_down"]).transpose(1, 0, 2).reshape(D, E * H1)).astype(f32)
    d["bd"] = np.ascontiguousarray(
        np.asarray(ins["L1_ffn_b_down"]).reshape(E * H1)).astype(f32)
    d["wu"] = np.ascontiguousarray(
        np.asarray(ins["L1_ffn_W_up"]).reshape(E * H1, D)).astype(f32)
    d["bu"] = np.ascontiguousarray(ins["L1_ffn_b_up"]).astype(f32)
    return d


_WSHAPES = {
    "wqT": [NDC, P, D], "wkT": [NDC, P, D],
    "wvT": [D, D], "woT": [D, D],
    "w_red": [D, G], "wg": [E, G],
    "w1": [E, NHC0, P, D], "w2": [E, NDC, P, H0],
    "b1": [E, H0], "b2": [E, D],
    "qkv_w": [24, P, D], "qkv_n": [D, 3 * D], "qkv_b": [3 * D],
    "rq_w": [2, P, D], "rq_b": [LD],
    "rkv_w": [D, 2 * LD], "rkv_b": [2 * LD],
    "rout_w": [NDC, P, 2 * P], "rout_wT": [2, P, D], "rout_b": [D],
    "out_w": [D, D], "out_b": [D],
    "gl_w": [HD, 1], "gr_w": [HD, 1],
    "ln_g": [D], "ln_b": [D], "bn_g": [D], "bn_b": [D],
    "bn_mean": [D], "bn_var": [D],
    "wr": [D, E], "br": [E],
    "wd": [D, E * H1], "bd": [E * H1], "wu": [E * H1, D], "bu": [E, D],
}

_DBG_NAMES = ["d_att0", "d_x1", "d_x2", "d_att1", "d_x3", "d_x4"]


class Prog:
    def __init__(self, consts, debug=False):
        nc = bass.Bass("TRN2", target_bir_lowering=False)
        self.nc = nc
        self.debug = debug
        self.x_h = nc.dram_tensor("x", [S, D], F32, kind="ExternalInput").ap()
        self.out_h = nc.dram_tensor("out", [S, D], F32, kind="ExternalOutput").ap()
        self.W = {n: nc.dram_tensor(n, shp, F32, kind="ExternalInput").ap()
                  for n, shp in _WSHAPES.items()}
        self.C = {n: nc.inline_tensor(np.asarray(v), name="c_" + n).ap()
                  for n, v in consts.items()}
        self.dbg = {}
        if debug:
            for n in _DBG_NAMES:
                self.dbg[n] = nc.dram_tensor(n, [S, D], F32,
                                             kind="ExternalOutput").ap()

        A = nc.alloc_sbuf_tensor
        self.xs = A("xs", [P, NSC * D], F32)
        self.xT = A("xT", [P, NDC * S], BF16)
        self.bufB = A("bufB", [P, NDC * S], BF16)
        self.bufC = A("bufC", [P, 8448], BF16)
        self.bufD = A("bufD", [P, 4096], BF16)
        self.qga = A("qga", [65, S], BF16)
        self.qgb = A("qgb", [65, S], BF16)
        self.kga = A("kga", [65, S], BF16)
        self.kgb = A("kgb", [65, S], BF16)
        self.wst = A("wst", [P, 2048], F32)
        self.wbf = A("wbf", [P, 4096], BF16)
        self.smf = A("smf", [P, 1024], F32)
        self.smb = A("smb", [P, 2048], BF16)
        self.stat = A("stat", [P, 96], F32)
        self.rows = A("rows", [8, S], BF16)
        self.kc = {}
        for n, v in consts.items():
            arr = np.asarray(v)
            if n in ("m_lat", "sel8", "sel16"):
                continue
            self.kc[n] = A("k_" + n, list(arr.shape),
                           F32 if n == "eye32" else BF16)
        self.mlat = self.kc["rope_cos"]
        self.PS = [nc.alloc_psum_tensor("ps%d" % i, [P, 512], F32)
                   for i in range(8)]

        self.sem = {}
        self.ctr = {}
        names = ["t", "v", "s", "g", "dq", "dg", "o"]
        names += ["dq%d" % i for i in range(6)]
        names += ["dg%d" % i for i in range(3)]
        for n in names:
            self.sem[n] = nc.alloc_semaphore("sem_" + n)
            self.ctr[n] = 0
        self._dqrr = 0
        self._dgrr = 0
        self.marks = {}
        self.blk = None

    def inc(self, instr, sem):
        step = 1 if sem in ("t", "v", "s", "g") else 16
        self.ctr[sem] += step
        instr.then_inc(self.sem[sem], step)
        return self.ctr[sem]

    def mark(self, label, sem=None):
        if sem is None:
            self.marks[label] = {n: self.ctr[n] for n in self.ctr}
        else:
            self.marks[label] = {sem: self.ctr[sem]}

    def wait(self, eng, label):
        for sem, val in self.marks[label].items():
            if val > 0:
                eng.wait_ge(self.sem[sem], val)

    def wv(self, eng, sem, val):
        if val > 0:
            eng.wait_ge(self.sem[sem], val)

    def psb(self, i, n=P):
        return self.PS[i][:, :].bitcast(BF16)[:, 0:n]

    def dma_barrier(self, eng, sem):
        if self.ctr[sem] > 0:
            eng.wait_ge(self.sem[sem], self.ctr[sem])

    def body(self, engine, fn):
        getattr(self.blk, engine)(fn)

    def load_cast(self, hbm_ap, n, wst_off, wbf_off, label, wait=None,
                  wait_cast=None, blocks=None, queue="sync", stage=None):
        k = self
        if queue == "sync":
            ch = "dq%d" % (self._dqrr % 6)
            self._dqrr += 1
        else:
            ch = "dg%d" % (self._dgrr % 3)
            self._dgrr += 1

        stg = k.wst if stage is None else stage

        def qb(q):
            k.dma_barrier(q, ch)
            if wait is not None:
                k.wait(q, wait)
            dst = stg[:, wst_off:wst_off + n]
            if blocks is not None:
                dst = dst.rearrange("p (o j) -> p o j", o=blocks)
            k.inc(q.dma_start(out=dst, in_=hbm_ap), ch)
        self.body("sync" if queue == "sync" else "gpsimd", qb)
        self.mark(label + "_dma", ch)

        def vb(v):
            k.wait(v, label + "_dma")
            if wait_cast is not None:
                k.wait(v, wait_cast)
            k.inc(v.tensor_copy(out=k.wbf[:, wbf_off:wbf_off + n],
                                in_=stg[:, wst_off:wst_off + n]), "v")
        self.body("vector", vb)
        self.mark(label, "v")
        return label

    def dump(self, name, region_fn, wait_label):
        if not self.debug:
            return
        k = self
        dst = self.dbg[name]

        def gb(g):
            k.dma_barrier(g, "dg")
            k.wait(g, wait_label)
            for t in range(NSC):
                k.inc(g.dma_start(out=dst[t * P:(t + 1) * P, :],
                                  in_=region_fn(t)), "dg")
        self.body("gpsimd", gb)



def ph_input(p):
    k = p

    def qb(q):
        for t in range(NSC):
            k.inc(q.dma_start(out=p.xs[:, t * D:(t + 1) * D],
                              in_=p.x_h[t * P:(t + 1) * P, :]), "dq")
        for n, ap in p.C.items():
            if n in ("m_lat", "sel8", "sel16"):
                continue
            k.inc(q.dma_start(out=p.kc[n][:, :], in_=ap), "dq")
    p.body("sync", qb)
    k.mark("input", "dq")

    def vb(v):
        k.wait(v, "input")
        k.inc(v.memset(p.kga[64:65, :], 1.0), "v")
        k.inc(v.memset(p.kgb[64:65, :], 1.0), "v")
    p.body("vector", vb)
    k.mark("aug_ones", "v")


def ph_transpose(p, tag, do_cast=True, tile_marks=None):
    k = p
    tbase = k.ctr["t"]
    vbase = k.ctr["v"]

    def tb(t):
        if tile_marks is None:
            k.wait(t, tag + "_src")
        i = 0
        for st in range(NSC):
            if tile_marks is not None:
                k.wait(t, tile_marks(st))
            for dc in range(NDC):
                if i >= 4:
                    k.wv(t, "v", vbase + i - 3)
                if do_cast:
                    k.inc(t.transpose(
                        p.PS[4 + i % 4][:, 0:P],
                        p.xs[:, st * D + dc * P: st * D + dc * P + P],
                        p.kc["eye32"][:, :]), "t")
                else:
                    k.inc(t.transpose(
                        p.psb(4 + i % 4, P),
                        p.bufB[:, st * D + dc * P: st * D + dc * P + P],
                        p.kc["eye"][:, :]), "t")
                i += 1
    p.body("tensor", tb)

    def vb(v):
        i = 0
        for st in range(NSC):
            for dc in range(NDC):
                k.wv(v, "t", tbase + i + 1)
                srcap = (p.PS[4 + i % 4][:, 0:P] if do_cast
                         else p.psb(4 + i % 4, P))
                k.inc(v.tensor_copy(
                    out=p.xT[:, dc * S + st * P: dc * S + st * P + P],
                    in_=srcap), "v")
                i += 1
    p.body("vector", vb)
    k.mark(tag, "v")


def ph_l0_attn(p):
    k = p
    KC = p.kc

    for grp in range(2):
        for dc in range(NDC):
            src = p.W["wvT"][dc * P:(dc + 1) * P, 512 * grp:512 * grp + 512]
            if dc >= 4:
                gw = "g%d_wv%d" % (grp, dc - 4)
            elif grp == 1:
                gw = "qk_done3"
            else:
                gw = None
            gwc = ("qk_done3" if grp == 1 else None)
            p.load_cast(src, 512, (dc % 4) * 512, dc * 512,
                        "g%d_wv%d" % (grp, dc), wait=gw, wait_cast=gwc)

        def vones(v, grp=grp):
            if grp == 1:
                k.wait(v, "l0a_vaug_used0")
            for st in range(NSC):
                k.inc(v.memset(p.bufC[:, st * 528: st * 528 + 528], 1.0), "v")
        p.body("vector", vones)
        k.mark("g%d_vones" % grp, "v")

        tb = k.ctr["t"]
        vb0 = k.ctr["v"]

        def tmm(t, grp=grp, tb=tb, vb0=vb0):
            k.wait(t, "g%d_wv%d" % (grp, NDC - 1))
            k.wait(t, "g%d_vones" % grp)
            k.wait(t, "l0a_xT")
            for st in range(NSC):
                if st >= 2:
                    k.wv(t, "v", vb0 + st - 1)
                for dc in range(NDC):
                    k.inc(t.matmul(
                        p.PS[st % 2][:, 0:512],
                        p.xT[:, dc * S + st * P: dc * S + st * P + P],
                        p.wbf[:, dc * 512:(dc + 1) * 512],
                        start=(dc == 0), stop=(dc == NDC - 1)), "t")
        p.body("tensor", tmm)

        def vevac(v, grp=grp, tb=tb):
            for st in range(NSC):
                k.wv(v, "t", tb + (st + 1) * NDC)
                dst = p.bufC[:, st * 528: st * 528 + 528]
                dst3 = dst.rearrange("p (h c) -> p h c", c=66)[:, 0:8, 0:64]
                src3 = p.PS[st % 2][:, 0:512].rearrange("p (h c) -> p h c", c=64)
                k.inc(v.tensor_copy(out=dst3, in_=src3), "v")
        p.body("vector", vevac)
        k.mark("g%d_v" % grp, "v")

        for cpi in range(4):
            cp = grp * 4 + cpi
            srcq = p.W["wqT"][cp]
            srck = p.W["wkT"][cp]
            guard = ("qk_done%d" % (cp - 1)) if cpi > 0 else ("g%d_v" % grp)
            wq = p.load_cast(srcq, 1024, 0, 0, "wq%d" % cp,
                             wait=guard, wait_cast=guard)
            wk = p.load_cast(srck, 1024, 1024, 1024, "wk%d" % cp,
                             wait=guard, wait_cast=guard)

            for which, wboff in (("k", 1024), ("q", 0)):
                scale = 0.125 if which == "q" else 1.0
                wlab = wq if which == "q" else wk
                s_base = k.ctr["s"]

                def tproj(t, wboff=wboff, wlab=wlab, grp=grp, which=which,
                          cp=cp, s_base=s_base):
                    k.wait(t, wlab)
                    k.wait(t, "g%d_v" % grp)
                    if "normmm_prev" in k.marks:
                        k.wait(t, "normmm_prev")
                    for sb in range(NSB):
                        if sb >= 2:
                            k.wv(t, "s", s_base + sb - 1)
                        for dc in range(NDC):
                            k.inc(t.matmul(
                                p.PS[4 + (sb % 2)][:, 0:512],
                                p.wbf[:, wboff + dc * P: wboff + dc * P + P],
                                p.xT[:, dc * S + sb * 512: dc * S + (sb + 1) * 512],
                                start=(dc == 0), stop=(dc == NDC - 1)), "t")
                p.body("tensor", tproj)
                tb2 = k.ctr["t"]

                def sevac(s, tb2=tb2, scale=scale, cp=cp):
                    if "normmm_prev" in k.marks:
                        k.wait(s, "normmm_prev")
                    if cp > 0:
                        k.wait(s, "heads%d" % (cp - 1))
                    if "bufD_free" in k.marks:
                        k.wait(s, "bufD_free")
                    for sb in range(NSB):
                        k.wv(s, "t", tb2 - NSB * NDC + (sb + 1) * NDC)
                        k.inc(s.activation(
                            out=p.smb[:, sb * 512:(sb + 1) * 512],
                            in_=p.PS[4 + (sb % 2)][:, 0:512],
                            func=ACTF.Copy, scale=scale), "s")
                p.body("scalar", sevac)
                k.mark("raw_%s%d" % (which, cp), "s")

                tb3 = k.ctr["t"]
                v_base = k.ctr["v"]

                def trot(t, which=which, cp=cp, s_base=s_base, v_base=v_base):
                    for sb in range(NSB):
                        k.wv(t, "s", s_base + sb + 1)
                        if sb >= 2:
                            k.wv(t, "v", v_base + 4 * (sb - 1))
                        k.inc(t.matmul(p.PS[6 + (sb % 2)][:, 0:512],
                                       KC["rotm"][:, :],
                                       p.smb[:, sb * 512:(sb + 1) * 512],
                                       start=True, stop=True), "t")
                p.body("tensor", trot)

                def vrope(v, tb3=tb3, which=which, cp=cp):
                    if "bufD_free" in k.marks:
                        k.wait(v, "bufD_free")
                    for sb in range(NSB):
                        k.wv(v, "t", tb3 + sb + 1)
                        sl = slice(sb * 512, (sb + 1) * 512)
                        sl2 = slice(2048 + sb * 512, 2048 + (sb + 1) * 512)
                        k.inc(v.tensor_tensor(
                            out=p.bufD[:, sl2], in0=p.smb[:, sl],
                            in1=KC["rope_cos"][:, sl], op=ALU.mult), "v")
                        k.inc(v.tensor_tensor(
                            out=p.bufD[:, sl], in0=p.PS[6 + (sb % 2)][:, 0:512],
                            in1=KC["rope_sin"][:, sl], op=ALU.mult), "v")
                        c1 = k.ctr["v"]
                        v.wait_ge(p.sem["v"], c1)
                        k.inc(v.tensor_tensor(
                            out=p.bufD[:, sl], in0=p.bufD[:, sl],
                            in1=p.bufD[:, sl2], op=ALU.add), "v")
                        c2 = k.ctr["v"]
                        v.wait_ge(p.sem["v"], c2)
                        k.inc(v.tensor_tensor(
                            out=p.smb[:, sl], in0=p.bufD[:, sl],
                            in1=p.bufD[:, sl], op=ALU.mult), "v")
                p.body("vector", vrope)
                k.mark("rot_%s%d" % (which, cp), "v")
                k.mark("sq_%s%d" % (which, cp), "v")

                for sb in range(NSB):
                    tb4 = k.ctr["t"]

                    def tns(t, sb=sb, which=which, cp=cp, v_base=v_base):
                        k.wv(t, "v", v_base + 4 * sb + 4)
                        if "ns_evac" in k.marks:
                            k.wait(t, "ns_evac")
                        if cp > 0 and sb == 0 and which == "k":
                            k.wait(t, "hbc%d_3" % (2 * cp - 1))
                        k.inc(t.matmul(p.PS[3][0:2, 0:512],
                                       KC["blockones"][:, :],
                                       p.smb[:, sb * 512:(sb + 1) * 512],
                                       start=True, stop=True), "t")
                    p.body("tensor", tns)

                    def vns(v, tb4=tb4, sb=sb, which=which, cp=cp):
                        k.wv(v, "t", tb4 + 1)
                        if which == "q" and sb == 0:
                            k.wait(v, "kmax%d" % cp)
                        with p.nc.allow_low_precision(
                                reason="norm bound feeds bf16 stabilizer"):
                            k.inc(v.tensor_copy(
                                out=p.rows[0:2, sb * 512:(sb + 1) * 512],
                                in_=p.PS[3][0:2, 0:512]), "v")
                    p.body("vector", vns)
                    k.mark("ns_evac", "v")
                k.mark("norm_%s%d" % (which, cp), "v")
                k.mark("normmm_prev", "t")

                if which == "k":
                    def vkm(v, cp=cp):
                        k.wait(v, "norm_k%d" % cp)
                        k.inc(v.reduce_max(out=p.stat[0:2, 0:1],
                                           in_=p.rows[0:2, :], axis=AX), "v")
                    p.body("vector", vkm)
                    k.mark("kmax%d" % cp, "v")

                def vaug(v, which=which, cp=cp):
                    k.wait(v, "rot_%s%d" % (which, cp))
                    if cp > 0:
                        k.wait(v, "heads%d" % (cp - 1))
                    dst = p.qga if which == "q" else p.kga
                    k.inc(v.tensor_copy(out=dst[0:64, :], in_=p.bufD[0:64, 0:2048]), "v")
                p.body("vector", vaug)
                k.mark("augA_%s%d" % (which, cp), "v")

                def gaug(g, which=which, cp=cp):
                    k.dma_barrier(g, "dg")
                    k.wait(g, "rot_%s%d" % (which, cp))
                    if cp > 0:
                        k.wait(g, "heads%d" % (cp - 1))
                    dst = p.qgb if which == "q" else p.kgb
                    k.inc(g.dma_start(out=dst[0:64, :], in_=p.bufD[64:128, 0:2048]), "dg")
                p.body("gpsimd", gaug)
                k.mark("augB_%s%d" % (which, cp), "dg")
                k.mark("bufD_free")

            def vcrow1(v, cp=cp):
                k.wait(v, "kmax%d" % cp)
                k.wait(v, "norm_q%d" % cp)
                with p.nc.allow_low_precision(reason="stabilizer bound"):
                    k.inc(v.tensor_scalar(
                        out=p.rows[0:2, :], in0=p.rows[0:2, :],
                        scalar1=p.stat[0:2, 0:1], scalar2=None,
                        op0=ALU.mult), "v")
            p.body("vector", vcrow1)
            k.mark("crow1_%d" % cp, "v")

            def scrow(s, cp=cp):
                k.wait(s, "crow1_%d" % cp)
                k.inc(s.activation(out=p.rows[0:2, :], in_=p.rows[0:2, :],
                                   func=ACTF.Sqrt), "s")
            p.body("scalar", scrow)
            k.mark("crow2_%d" % cp, "s")

            def vcrow(v, cp=cp):
                k.wait(v, "crow2_%d" % cp)
                with p.nc.allow_low_precision(reason="stabilizer bound"):
                    k.inc(v.tensor_scalar(
                        out=p.rows[0:2, :], in0=p.rows[0:2, :],
                        scalar1=-1.0, scalar2=-1.0,
                        op0=ALU.mult, op1=ALU.add), "v")
            p.body("vector", vcrow)
            k.mark("crow%d" % cp, "v")

            def gcrow(g, cp=cp):
                k.dma_barrier(g, "dg")
                k.wait(g, "crow%d" % cp)
                k.wait(g, "augA_q%d" % cp)
                k.wait(g, "augB_q%d" % cp)
                k.inc(g.dma_start(out=p.qga[64:65, :], in_=p.rows[0:1, :]), "dg")
                k.inc(g.dma_start(out=p.qgb[64:65, :], in_=p.rows[1:2, :]), "dg")
            p.body("gpsimd", gcrow)
            k.mark("aug%d" % cp, "dg")
            k.mark("qk_done%d" % cp)

            for hh in range(2):
                h = 2 * cp + hh
                qa = p.qga if hh == 0 else p.qgb
                ka = p.kga if hh == 0 else p.kgb
                hslot = h % 8
                tb5 = k.ctr["t"]
                sb5 = k.ctr["s"]
                vb5 = k.ctr["v"]

                st_sched = []
                for kc in range(NSC):
                    for qb in range(kc // 4, NSB):
                        st_sched.append((kc, qb))
                n_t = len(st_sched)
                silast = {}
                for si, (kc, qb) in enumerate(st_sched):
                    silast[qb] = si
                tops = []
                for si in range(n_t):
                    tops.append(("st", si))
                    if si >= 2:
                        av = si - 2
                        tops.append(("av", av))
                        if av == silast[1]:
                            tops.append(("bc", 0))
                        elif av == silast[2]:
                            tops.append(("bc", 1))
                tops.append(("av", n_t - 2))
                tops.append(("av", n_t - 1))
                tops.append(("bc", 2))
                tops.append(("bc", 3))
                tc = {}
                for pos, opx in enumerate(tops):
                    tc[opx] = tb5 + pos + 1
                vcum = [0] * n_t
                vrec = {}
                vbcp = {}
                vc = vb5
                for si, (kc, qb) in enumerate(st_sched):
                    if kc >= 4 * qb:
                        vc += 2 if kc == 4 * qb + 3 else 1
                    vcum[si] = vc
                    if si == silast[0]:
                        vc += 1; vrec[0] = vc
                    elif si == silast[1]:
                        vc += 1; vrec[1] = vc
                        vc += 2; vbcp[0] = vc
                    elif si == silast[2]:
                        vc += 1; vrec[2] = vc
                        vc += 2; vbcp[1] = vc
                vc += 1; vrec[3] = vc
                vc += 2; vbcp[2] = vc
                vc += 2; vbcp[3] = vc

                def thead(t, tb5=tb5, sb5=sb5, qa=qa, ka=ka, hslot=hslot,
                          st_sched=st_sched, tops=tops, tc=tc, vcum=vcum,
                          vrec=vrec, vbcp=vbcp, cp=cp, h=h):
                    k.wait(t, "aug%d" % cp)
                    k.wait(t, "augA_k%d" % cp)
                    k.wait(t, "augB_k%d" % cp)
                    for opx in tops:
                        kind, xi = opx
                        if kind == "st":
                            si = xi
                            kc, qb = st_sched[si]
                            lo = max(0, kc - 4 * qb) * P
                            if si >= 3:
                                k.wv(t, "s", sb5 + si - 2)
                            k.inc(t.matmul(
                                p.PS[si % 3][:, lo:512],
                                ka[0:65, kc * P:(kc + 1) * P],
                                qa[0:65, qb * 512 + lo:(qb + 1) * 512],
                                start=True, stop=True), "t")
                        elif kind == "av":
                            si = xi
                            kc, qb = st_sched[si]
                            avlo = max(0, kc - 4 * qb) * P
                            if avlo >= 384:
                                avlo = 0
                            if kc == 0 and h > 0:
                                k.wait(t, "hbc%d_%d" % (h - 1, qb))
                            k.wv(t, "s", sb5 + si + 1)
                            k.wv(t, "v", vcum[si])
                            k.inc(t.matmul(
                                p.PS[4 + qb][0:65, avlo:512],
                                p.bufC[:, kc * 528 + hslot * 66:
                                       kc * 528 + hslot * 66 + 65],
                                p.smb[:, (si % 3) * 512 + avlo:
                                       (si % 3) * 512 + 512],
                                start=(kc == 0), stop=(kc == 4 * qb + 3)), "t")
                        else:
                            qb = xi
                            k.wv(t, "v", vrec[qb])
                            if qb > 0:
                                k.wv(t, "v", vbcp[qb - 1] - 1)
                            elif h > 0:
                                k.wait(t, "hbc%d_3" % (h - 1))
                            k.inc(t.matmul(
                                p.PS[3][0:64, 0:512],
                                p.kc["m_causal"][0:1, 0:64],
                                p.rows[0:1, qb * 512:(qb + 1) * 512],
                                start=True, stop=True), "t")
                p.body("tensor", thead)

                def sexp(s, sb5=sb5, st_sched=st_sched, tc=tc, vcum=vcum):
                    for si in range(len(st_sched)):
                        kc, qb = st_sched[si]
                        lo = max(0, kc - 4 * qb) * P
                        tw = tc[("st", si)]
                        if si >= 3:
                            tw = max(tw, tc[("av", si - 3)])
                            k.wv(s, "v", vcum[si - 3])
                        k.wv(s, "t", tw)
                        k.inc(s.activation(
                            out=p.smb[:, (si % 3) * 512 + lo:
                                     (si % 3) * 512 + 512],
                            in_=p.PS[si % 3][:, lo:512], func=ACTF.Exp), "s")
                p.body("scalar", sexp)

                scr = p.smf[0:64, 512:768].bitcast(BF16)

                def vnorm(v, qb, tc=tc, silast=silast, cp=cp):
                    k.wv(v, "t", tc[("av", silast[qb])])
                    if qb == 0:
                        k.wait(v, "aug%d" % cp)
                    with p.nc.allow_low_precision(
                            reason="1/l feeds a bf16 matmul operand; rounding"
                                   " happens at the cast either way"):
                        k.inc(v.reciprocal(
                            out=p.rows[0:1, qb * 512:(qb + 1) * 512],
                            in_=p.PS[4 + qb][64:65, 0:512]), "v")

                def vbc(v, qb, tc=tc, hh=hh, h=h, scr=scr):
                    k.wv(v, "t", tc[("bc", qb)])
                    if hh == 1 and qb > 0:
                        k.wait(v, "hodd%d_%d" % (h, qb - 1))
                    c0 = k.ctr["v"]
                    v.wait_ge(p.sem["v"], c0)
                    with p.nc.allow_low_precision(
                            reason="1/l broadcast feeds bf16 scale"):
                        k.inc(v.tensor_copy(out=scr,
                                            in_=p.PS[3][0:64, 0:512]), "v")
                    c1 = k.ctr["v"]
                    v.wait_ge(p.sem["v"], c1)
                    if hh == 0:
                        dst = p.bufB[0:64, (h // 2) * S + qb * 512:
                                     (h // 2) * S + (qb + 1) * 512]
                    else:
                        dst = p.smb[0:64, 1536:2048]
                    k.inc(v.tensor_tensor(
                        out=dst, in0=p.PS[4 + qb][0:64, 0:512],
                        in1=scr, op=ALU.mult), "v")

                def vseg(v, los, his, recq, bcq, sb5=sb5, st_sched=st_sched):
                    for si in range(los, his):
                        kc, qb = st_sched[si]
                        base = (si % 3) * 512
                        npre = kc - 4 * qb
                        if npre < 0:
                            continue
                        k.wv(v, "s", sb5 + si + 1)
                        if npre == 3:
                            k.inc(v.memset(p.smb[:, base:base + npre * P],
                                           0.0), "v")
                        off = base + npre * P
                        k.inc(v.tensor_tensor(
                            out=p.smb[:, off:off + P],
                            in0=p.smb[:, off:off + P],
                            in1=p.kc["m_causal"][:, :], op=ALU.mult), "v")
                    if recq is not None:
                        vnorm(v, recq)
                        assert k.ctr["v"] == vrec[recq], \
                            ("vrec", recq, k.ctr["v"], vrec[recq])
                    if bcq is not None:
                        vbc(v, bcq)
                        assert k.ctr["v"] == vbcp[bcq], \
                            ("vbcp", bcq, k.ctr["v"], vbcp[bcq])

                segs = [(0, silast[0] + 1, 0, None),
                        (silast[0] + 1, silast[1] + 1, 1, 0),
                        (silast[1] + 1, silast[2] + 1, 2, 1),
                        (silast[2] + 1, n_t, 3, 2),
                        (n_t, n_t, None, 3)]
                for (los, his, recq, bcq) in segs:
                    p.body("vector",
                           lambda v, los=los, his=his, recq=recq, bcq=bcq:
                           vseg(v, los, his, recq, bcq))
                    if bcq is not None:
                        k.mark("hbc%d_%d" % (h, bcq), "v")
                        if hh == 1:
                            def gsc(g, qb=bcq, h=h):
                                k.dma_barrier(g, "dg")
                                k.wait(g, "hbc%d_%d" % (h, qb))
                                k.inc(g.dma_start(
                                    out=p.bufB[64:128,
                                               (h // 2) * S + qb * 512:
                                               (h // 2) * S + (qb + 1) * 512],
                                    in_=p.smb[0:64, 1536:2048]), "dg")
                            p.body("gpsimd", gsc)
                            k.mark("hodd%d_%d" % (h, bcq), "dg")
                k.mark("hnorm%d" % h)
            k.mark("heads%d" % cp)
        if grp == 0:
            k.mark("l0a_vaug_used0")
    k.mark("l0a_attnT")

    if p.debug:
        dbg_bufB = p.nc.dram_tensor("d_bufB", [128, 16384], BF16,
                                    kind="ExternalOutput").ap()

        def gdb(g):
            k.dma_barrier(g, "dg")
            k.wait(g, "l0a_attnT")
            k.inc(g.dma_start(out=dbg_bufB, in_=p.bufB[:, :]), "dg")
        p.body("gpsimd", gdb)
        k.mark("l0a_dbgB", "dg")

    for ic in range(NDC):
        src = p.W["woT"][ic * P:(ic + 1) * P, :]
        p.load_cast(src, 1024, (ic % 2) * 1024, 0, "wo_st%d" % ic,
                    wait=("wo_mv%d" % (ic - 2)) if ic >= 2 else "l0a_attnT",
                    wait_cast=("wo_mv%d" % (ic - 1)) if ic >= 1 else "l0a_attnT")
        def vmv(v, ic=ic):
            k.wait(v, "wo_st%d" % ic)
            k.inc(v.tensor_copy(out=p.bufC[:, ic * 1024:(ic + 1) * 1024],
                                in_=p.wbf[:, 0:1024]), "v")
        p.body("vector", vmv)
        k.mark("wo_mv%d" % ic, "v")
    k.mark("wo_res", "v")

    ph_proj_res_ln(p, "l0a", lambda ic: p.bufB[:, ic * S: (ic + 1) * S],
                   lambda ic: p.bufC[:, ic * 1024:(ic + 1) * 1024],
                   "wo_res", "l0a_attnT", dbg_pre="d_att0")



def emit_ln_phase(p, tag, apply_out=None, radd_mark=None, src_mark=None,
                  dbg_name=None):
    k = p

    def vsum(v):
        for sc in range(NSC):
            if radd_mark is not None:
                k.wait(v, radd_mark + "_radd%d" % sc)
            elif sc == 0 and src_mark is not None:
                k.wait(v, src_mark)
            k.inc(v.reduce_sum(out=p.stat[:, 32 + sc:33 + sc],
                               in_=p.xs[:, sc * D:(sc + 1) * D], axis=AX), "v")
    p.body("vector", vsum)
    k.mark(tag + "_sums", "v")

    def ssq(s):
        for sc in range(NSC):
            if radd_mark is not None:
                k.wait(s, radd_mark + "_radd%d" % sc)
            elif sc == 0 and src_mark is not None:
                k.wait(s, src_mark)
            if sc > 0:
                s.wait_ge(p.sem["s"], k.ctr["s"])
            k.inc(s.activation(out=p.smf[:, 0:1024],
                               in_=p.xs[:, sc * D:(sc + 1) * D],
                               func=ACTF.Square, accum_out=p.stat[:, 48 + sc:49 + sc]),
                  "s")
    p.body("scalar", ssq)
    k.mark(tag + "_sqs", "s")

    def vstats(v):
        k.wait(v, tag + "_sqs")
        k.wait(v, tag + "_sums")
        k.inc(v.tensor_scalar(out=p.stat[:, 32:48], in0=p.stat[:, 32:48],
                              scalar1=1.0 / D, scalar2=None, op0=ALU.mult), "v")
        k.inc(v.tensor_scalar(out=p.stat[:, 48:64], in0=p.stat[:, 48:64],
                              scalar1=1.0 / D, scalar2=None, op0=ALU.mult), "v")
        c1 = k.ctr["v"]
        v.wait_ge(p.sem["v"], c1)
        k.inc(v.tensor_tensor(out=p.stat[:, 64:80], in0=p.stat[:, 32:48],
                              in1=p.stat[:, 32:48], op=ALU.mult), "v")
        c2 = k.ctr["v"]
        v.wait_ge(p.sem["v"], c2)
        k.inc(v.tensor_tensor(out=p.stat[:, 48:64], in0=p.stat[:, 48:64],
                              in1=p.stat[:, 64:80], op=ALU.subtract), "v")
        c3 = k.ctr["v"]
        v.wait_ge(p.sem["v"], c3)
        k.inc(v.tensor_scalar(out=p.stat[:, 48:64], in0=p.stat[:, 48:64],
                              scalar1=EPS, scalar2=None, op0=ALU.add), "v")
    p.body("vector", vstats)
    k.mark(tag + "_vst", "v")

    def ssqrt(s):
        k.wait(s, tag + "_vst")
        k.inc(s.activation(out=p.stat[:, 64:80], in_=p.stat[:, 48:64],
                           func=ACTF.Sqrt), "s")
    p.body("scalar", ssqrt)
    k.mark(tag + "_sqrt", "s")

    def vapply(v):
        k.wait(v, tag + "_sqrt")
        k.inc(v.reciprocal(out=p.stat[:, 64:80], in_=p.stat[:, 64:80]), "v")
        c1 = k.ctr["v"]
        v.wait_ge(p.sem["v"], c1)
        for sc in range(NSC):
            if dbg_name is not None and p.debug:
                k.wait(v, tag + "_dbg%d" % sc)
            dst = (p.xs[:, sc * D:(sc + 1) * D] if apply_out is None
                   else apply_out(sc))
            if apply_out is None:
                k.inc(v.tensor_scalar(
                    out=dst, in0=p.xs[:, sc * D:(sc + 1) * D],
                    scalar1=p.stat[:, 32 + sc:33 + sc],
                    scalar2=p.stat[:, 64 + sc:65 + sc],
                    op0=ALU.subtract, op1=ALU.mult), "v")
            else:
                with p.nc.allow_low_precision(reason="bf16 matmul operand"):
                    k.inc(v.tensor_scalar(
                        out=dst, in0=p.xs[:, sc * D:(sc + 1) * D],
                        scalar1=p.stat[:, 32 + sc:33 + sc],
                        scalar2=p.stat[:, 64 + sc:65 + sc],
                        op0=ALU.subtract, op1=ALU.mult), "v")
            k.mark(tag + "_ln%d" % sc, "v")
    p.body("vector", vapply)
    k.mark(tag + "_ln", "v")


def emit_dbg_dumps(p, tag, dbg_name):
    if not p.debug:
        return
    k = p

    for sc in range(NSC):
        def gd(g, sc=sc):
            k.dma_barrier(g, "dg")
            k.wait(g, tag + "_radd%d" % sc)
            k.inc(g.dma_start(out=p.dbg[dbg_name][sc * P:(sc + 1) * P, :],
                              in_=p.xs[:, sc * D:(sc + 1) * D]), "dg")
        p.body("gpsimd", gd)
        k.mark(tag + "_dbg%d" % sc, "dg")


def ph_proj_res_ln(p, tag, lhsT_fn, rhs_fn, w_ready, act_ready, dbg_pre=None,
                   extra_bias_row=None, scale_col=None):
    k = p
    for sc in range(NSC):
        def tmm(t, sc=sc):
            k.wait(t, w_ready)
            k.wait(t, act_ready)
            if sc >= 1:
                k.wait(t, tag + "_radd%d" % (sc - 1))
            for half in range(2):
                last = extra_bias_row is None
                for ic in range(NDC):
                    k.inc(t.matmul(
                        p.PS[half][:, 0:512],
                        lhsT_fn(ic)[:, sc * P:(sc + 1) * P],
                        rhs_fn(ic)[:, half * 512:(half + 1) * 512],
                        start=(ic == 0),
                        stop=(last and ic == NDC - 1)), "t")
                if extra_bias_row is not None:
                    k.inc(t.matmul(
                        p.PS[half][:, 0:512],
                        p.kc["m_causal"][0:1, 0:128],
                        extra_bias_row[0:1, half * 512:(half + 1) * 512],
                        start=False, stop=True), "t")
        p.body("tensor", tmm)
        mm_done = k.ctr["t"]

        def vradd(v, sc=sc, mm_done=mm_done):
            k.wv(v, "t", mm_done)
            for half in range(2):
                sl = slice(sc * D + half * 512, sc * D + (half + 1) * 512)
                if scale_col is None:
                    k.inc(v.tensor_tensor(
                        out=p.xs[:, sl], in0=p.PS[half][:, 0:512],
                        in1=p.xs[:, sl], op=ALU.add), "v")
                else:
                    k.inc(v.scalar_tensor_tensor(
                        out=p.xs[:, sl], in0=p.PS[half][:, 0:512],
                        scalar=scale_col, in1=p.xs[:, sl],
                        op0=ALU.mult, op1=ALU.add), "v")
        p.body("vector", vradd)
        k.mark(tag + "_radd%d" % sc, "v")
    if dbg_pre is not None:
        emit_dbg_dumps(p, tag, dbg_pre)
    emit_ln_phase(p, tag, radd_mark=tag, dbg_name=dbg_pre)


def ph_l0_ffn(p):
    k = p
    KC = p.kc

    def qsel(q):
        k.dma_barrier(q, "dq")
        k.wait(q, "l0a_attnT")
        k.inc(q.dma_start(out=p.kc["rope_sin"][0:E, 0:E * P], in_=p.C["sel8"]),
              "dq")
    p.body("sync", qsel)
    k.mark("sel8", "dq")

    src_wr = p.W["w_red"].rearrange("(c p) g -> p c g", p=P)
    p.load_cast(src_wr, NDC * G, 0, 0, "wred", wait="l0a_ln",
                wait_cast="l0a_ln", blocks=NDC)

    tb = k.ctr["t"]

    def txr(t):
        k.wait(t, "wred")
        k.wait(t, "l0f_xT")
        for sb in range(NSB):
            if sb >= 2:
                k.wv(t, "v", xr_vb + sb - 1)
            for dc in range(NDC):
                k.inc(t.matmul(p.PS[sb % 2][0:G, 0:512],
                               p.wbf[:, dc * G:(dc + 1) * G],
                               p.xT[:, dc * S + sb * 512: dc * S + (sb + 1) * 512],
                               start=(dc == 0), stop=(dc == NDC - 1)), "t")
    xr_vb = k.ctr["v"]
    p.body("tensor", txr)

    def vxr(v, tb=tb):
        for sb in range(NSB):
            k.wv(v, "t", tb + (sb + 1) * NDC)
            k.inc(v.tensor_copy(out=p.bufD[0:G, sb * 512:(sb + 1) * 512],
                                in_=p.PS[sb % 2][0:G, 0:512]), "v")
    p.body("vector", vxr)
    k.mark("xrT", "v")

    def qwg(q):
        k.dma_barrier(q, "dq")
        k.inc(q.dma_start(out=p.stat[0:E, 48:64], in_=p.W["wg"]), "dq")
    p.body("sync", qwg)
    k.mark("wg_dma", "dq")

    def swg(s):
        k.wait(s, "wg_dma")
        k.inc(s.activation(out=p.stat[0:E, 64:80], in_=p.stat[0:E, 48:64],
                           func=ACTF.Square), "s")
    p.body("scalar", swg)
    k.mark("wg_sq", "s")

    def vwg(v):
        k.wait(v, "wg_sq")
        k.inc(v.reduce_sum(out=p.stat[0:E, 80:81], in_=p.stat[0:E, 64:80],
                           axis=AX), "v")
    p.body("vector", vwg)
    k.mark("wg_ss", "v")

    def swg2(s):
        k.wait(s, "wg_ss")
        k.inc(s.activation(out=p.stat[0:E, 81:82], in_=p.stat[0:E, 80:81],
                           func=ACTF.Sqrt), "s")
    p.body("scalar", swg2)
    k.mark("wg_sqrt", "s")

    def vwg2(v):
        k.wait(v, "wg_sqrt")
        k.inc(v.tensor_scalar_max(out=p.stat[0:E, 82:83],
                                  in0=p.stat[0:E, 81:82], scalar1=1e-4), "v")
        c1 = k.ctr["v"]
        v.wait_ge(p.sem["v"], c1)
        k.inc(v.reciprocal(out=p.stat[0:E, 83:84], in_=p.stat[0:E, 82:83]), "v")
        c2 = k.ctr["v"]
        v.wait_ge(p.sem["v"], c2)
        k.inc(v.tensor_scalar(out=p.smb[0:E, 0:G], in0=p.stat[0:E, 48:64],
                              scalar1=p.stat[0:E, 83:84], scalar2=None,
                              op0=ALU.mult), "v")
    p.body("vector", vwg2)
    k.mark("wgn", "v")

    def twgn(t):
        k.wait(t, "wgn")
        k.inc(t.transpose(p.psb(2, G)[0:G, 0:E], p.smb[0:E, 0:G],
                          KC["eye"][0:E, 0:E]), "t")
    p.body("tensor", twgn)
    k.mark("wgnT_mm", "t")

    def vwgn(v):
        k.wait(v, "wgnT_mm")
        k.inc(v.tensor_copy(out=p.smb[0:G, 16:24], in_=p.psb(2, G)[0:G, 0:E]), "v")
    p.body("vector", vwgn)
    k.mark("wgnT", "v")

    for sc in range(NSC):
        tb2 = k.ctr["t"]

        def tlg(t, sc=sc):
            k.wait(t, "wgnT")
            k.wait(t, "xrT")
            if sc >= 1:
                k.wait(t, "gate%d" % (sc - 1))
            k.inc(t.matmul(p.PS[3][:, 0:E],
                           p.bufD[0:G, sc * P:(sc + 1) * P],
                           p.smb[0:G, 16:24], start=True, stop=True), "t")
        p.body("tensor", tlg)
        lg_done = k.ctr["t"]

        def vlg(v, sc=sc, lg_done=lg_done):
            k.wv(v, "t", lg_done)
            k.inc(v.reduce_max(out=p.stat[:, 0:1], in_=p.PS[3][:, 0:E],
                               axis=AX, negate=True), "v")
            k.inc(v.reduce_max(out=p.stat[:, 1:2], in_=p.PS[3][:, 0:E],
                               axis=AX), "v")
        p.body("vector", vlg)
        k.mark("lgm%d" % sc, "v")

        def slg(s, sc=sc):
            k.wait(s, "lgm%d" % sc)
            k.inc(s.activation(out=p.smf[:, 0:E], in_=p.PS[3][:, 0:E],
                               func=ACTF.Exp, bias=p.stat[:, 0:1],
                               accum_out=p.stat[:, 2:3]), "s")
        p.body("scalar", slg)
        k.mark("lge%d" % sc, "s")

        def vlg2(v, sc=sc):
            k.wait(v, "lge%d" % sc)
            k.inc(v.reciprocal(out=p.stat[:, 16 + sc:17 + sc],
                               in_=p.stat[:, 2:3]), "v")
            k.inc(v.tensor_scalar(out=p.smb[:, 128 + E * sc:128 + E * sc + E],
                                  in0=p.PS[3][:, 0:E],
                                  scalar1=p.stat[:, 1:2], scalar2=None,
                                  op0=ALU.is_equal), "v")
        p.body("vector", vlg2)
        k.mark("gate%d" % sc, "v")
    k.mark("gates")

    for sc in range(NSC):
        tb3 = k.ctr["t"]

        def tmt(t, sc=sc):
            k.wait(t, "gates")
            if sc >= 2:
                k.wait(t, "mt%d" % (sc - 2))
            k.inc(t.transpose(p.psb(4 + sc % 2, P)[0:E, 0:P],
                              p.smb[:, 128 + E * sc:128 + E * sc + E],
                              KC["eye"][:, :]), "t")
        p.body("tensor", tmt)
        mt_done = k.ctr["t"]

        def vmt(v, sc=sc, mt_done=mt_done):
            k.wv(v, "t", mt_done)
            k.inc(v.tensor_copy(out=p.bufD[0:E, sc * P:(sc + 1) * P],
                                in_=p.psb(4 + sc % 2, P)[0:E, 0:P]), "v")
        p.body("vector", vmt)
        k.mark("mt%d" % sc, "v")
    k.mark("maskT")

    def qb2(q):
        k.dma_barrier(q, "dq")
        k.wait(q, "wred")
        k.inc(q.dma_start(out=p.wst[0:E, 0:D], in_=p.W["b2"]), "dq")
    p.body("sync", qb2)
    k.mark("b2_dma", "dq")

    def vb2(v):
        k.wait(v, "b2_dma")
        k.inc(v.tensor_copy(out=p.smb[0:E, 1024:2048], in_=p.wst[0:E, 0:D]), "v")
    p.body("vector", vb2)
    k.mark("b2b", "v")

    def vz(v):
        k.wait(v, "l0f_xT")
        if "l0a_dbgB" in k.marks:
            k.wait(v, "l0a_dbgB")
        for i in range(4):
            k.inc(v.memset(p.bufB[:, i * 4096:(i + 1) * 4096], 0.0), "v")
    p.body("vector", vz)
    k.mark("oacc0", "v")

    for e in range(E):
        def qb1(q, e=e):
            k.dma_barrier(q, "dq")
            if e > 0:
                k.wait(q, "eW1_%d_done" % (e - 1))
            with p.nc.allow_non_contiguous_dma(reason="tiny 8KB bias gather"):
                k.inc(q.dma_start(out=p.stat[:, 32:48],
                                  in_=p.W["b1"][e].rearrange("(hc p) -> p hc", p=P)),
                      "dq")
        p.body("sync", qb1)
        k.mark("b1_%d" % e, "dq")

        tmb_b = k.ctr["t"]
        vmb_b = k.ctr["v"]

        def tmb(t, e=e, vmb_b=vmb_b):
            k.wait(t, "maskT")
            if e > 0:
                k.wait(t, "emask_%d_done" % (e - 1))
            for j in range(4):
                if j >= 2:
                    k.wv(t, "v", vmb_b + j - 1)
                if e == 0:
                    k.wait(t, "sel8")
                k.inc(t.matmul(
                    p.PS[6 + j % 2][:, 0:512],
                    KC["rope_sin"][0:E, e * P:(e + 1) * P],
                    p.bufD[0:E, j * 512:(j + 1) * 512],
                    start=True, stop=True), "t")
        p.body("tensor", tmb)

        def vmb(v, e=e, tmb_b=tmb_b):
            for j in range(4):
                k.wv(v, "t", tmb_b + j + 1)
                k.inc(v.tensor_copy(
                    out=p.bufD[:, 2048 + j * 512:2048 + (j + 1) * 512],
                    in_=p.PS[6 + j % 2][:, 0:512]), "v")
        p.body("vector", vmb)
        k.mark("mb_%d" % e, "v")

        for hh in range(2):
            for sbh in range(2):
                W1BANKS = [4, 5, 0, 1]
                s_base = k.ctr["s"]
                tb4s = []
                for hci in range(8):
                    hc = hh * 8 + hci
                    srcw = p.W["w1"][e, hc]
                    lab = "w1_%d_%d_%d_%d" % (e, hh, sbh, hci)
                    slot3 = hci % 3
                    slot = hci % 2
                    stg_off1 = 0 if slot3 == 2 else slot3 * 1024
                    stg1 = p.smf if slot3 == 2 else None
                    gw1 = "wst_slot%d" % slot3
                    if gw1 not in k.marks:
                        gw1 = "gates" if slot3 == 2 else "b2b"
                    p.load_cast(srcw, 1024, stg_off1, slot * 1024,
                                lab, wait=gw1,
                                wait_cast=("w1mm_prev%d" % slot)
                                if ("w1mm_prev%d" % slot) in k.marks else "xrT",
                                stage=stg1)
                    k.mark("wst_slot%d" % slot3, "v")
                    tb4s.append(k.ctr["t"])

                    def tw1(t, e=e, hh=hh, sbh=sbh, hci=hci, lab=lab,
                            s_base=s_base):
                        k.wait(t, lab)
                        k.wait(t, "l0f_xT")
                        if (e, hh, sbh, hci) == (0, 0, 0, 0):
                            k.wait(t, "maskT")
                            k.wait(t, "gates")
                        for half in range(2):
                            idx = 2 * hci + half
                            if idx >= 4:
                                k.wv(t, "s", s_base + idx - 4 + 1)
                            elif (e, hh, sbh) != (0, 0, 0):
                                pass
                            bank = W1BANKS[idx % 4]
                            soff = sbh * 1024 + half * 512
                            for dc in range(NDC):
                                k.inc(t.matmul(
                                    p.PS[bank][:, 0:512],
                                    p.wbf[:, (hci % 2) * 1024 + dc * P:
                                           (hci % 2) * 1024 + dc * P + P],
                                    p.xT[:, dc * S + soff: dc * S + soff + 512],
                                    start=(dc == 0), stop=(dc == NDC - 1)), "t")
                    p.body("tensor", tw1)
                    k.mark("w1mm_prev%d" % (hci % 2), "t")

                def sglall(s, e=e, hh=hh, sbh=sbh, tb4s=list(tb4s)):
                    k.wait(s, "b1_%d" % e)
                    if "w2round" in k.marks:
                        k.wait(s, "w2round")
                    for hci in range(8):
                        hc = hh * 8 + hci
                        for half in range(2):
                            bank = W1BANKS[(2 * hci + half) % 4]
                            k.wv(s, "t", tb4s[hci] + (half + 1) * NDC)
                            k.inc(s.activation(
                                out=p.bufC[:, hci * 1024 + half * 512:
                                          hci * 1024 + half * 512 + 512],
                                in_=p.PS[bank][:, 0:512],
                                func=ACTF.Gelu, bias=p.stat[:, 32 + hc:33 + hc]),
                                "s")
                p.body("scalar", sglall)
                k.mark("gelus_%d_%d_%d" % (e, hh, sbh), "s")
                gel_base = s_base

                def vgmall(v, e=e, sbh=sbh, gel_base=gel_base):
                    k.wait(v, "mb_%d" % e)
                    for hci in range(8):
                        for half in range(2):
                            k.wv(v, "s", gel_base + 2 * hci + half + 1)
                            off = hci * 1024 + half * 512
                            moff = 2048 + sbh * 1024 + half * 512
                            k.inc(v.tensor_tensor(
                                out=p.bufC[:, off:off + 512],
                                in0=p.bufC[:, off:off + 512],
                                in1=p.bufD[:, moff:moff + 512], op=ALU.mult), "v")
                p.body("vector", vgmall)
                k.mark("gready_%d_%d_%d" % (e, hh, sbh))
                if sbh == 1:
                    k.mark("eW1_%d_done" % e)

                OTBANKS = [6, 7, 2, 3]
                for cc in range(NDC):
                    srcw2 = p.W["w2"][e, cc][:, hh * 1024:(hh + 1) * 1024]
                    lab2 = "w2_%d_%d_%d_%d" % (e, hh, sbh, cc)
                    slot2 = cc % 3
                    wslot = cc % 2
                    stg_off = 0 if slot2 == 2 else slot2 * 1024
                    stg = p.smf if slot2 == 2 else None
                    gw2 = "wst_slot%d" % slot2
                    if gw2 not in k.marks:
                        gw2 = "gates"
                    p.load_cast(srcw2, 1024, stg_off,
                                2048 + wslot * 1024, lab2,
                                wait=gw2,
                                wait_cast=("w2mm_prev%d" % wslot)
                                if ("w2mm_prev%d" % wslot) in k.marks else None,
                                stage=stg)
                    k.mark("wst_slot%d" % slot2, "v")

                    for half in range(2):
                        bank = OTBANKS[(2 * cc + half) % 4]
                        tb5 = k.ctr["t"]

                        def tw2(t, e=e, hh=hh, sbh=sbh, cc=cc, half=half,
                                bank=bank, lab2=lab2):
                            k.wait(t, lab2)
                            if (e, hh, sbh) == (0, 0, 0):
                                k.wait(t, "gates")
                            k.wait(t, "gready_%d_%d_%d" % (e, hh, sbh))
                            ob = "oadd_%d" % bank
                            if ob in k.marks:
                                k.wait(t, ob)
                            nmm = 8 + (1 if (e == 0 and hh == 0) else 0)
                            for hci in range(8):
                                k.inc(t.matmul(
                                    p.PS[bank][:, 0:512],
                                    p.wbf[:, 2048 + (cc % 2) * 1024 + hci * P:
                                           2048 + (cc % 2) * 1024 + hci * P + P],
                                    p.bufC[:, hci * 1024 + half * 512:
                                           hci * 1024 + half * 512 + 512],
                                    start=(hci == 0),
                                    stop=(hci == 7 and nmm == 8)), "t")
                            if nmm == 9:
                                soff = sbh * 1024 + half * 512
                                k.inc(t.matmul(
                                    p.PS[bank][:, 0:512],
                                    p.smb[0:E, 1024 + cc * P:1024 + cc * P + P],
                                    p.bufD[0:E, soff:soff + 512],
                                    start=False, stop=True), "t")
                        p.body("tensor", tw2)
                        k.mark("w2mm_prev%d" % (cc % 2), "t")
                        w2_done = k.ctr["t"]

                        def voadd(v, e=e, hh=hh, sbh=sbh, cc=cc, half=half,
                                  bank=bank, w2_done=w2_done):
                            k.wv(v, "t", w2_done)
                            if e == 0 and hh == 0:
                                k.wait(v, "b2b")
                                k.wait(v, "oacc0")
                            off = cc * S + sbh * 1024 + half * 512
                            k.inc(v.tensor_tensor(
                                out=p.bufB[:, off:off + 512],
                                in0=p.PS[bank][:, 0:512],
                                in1=p.bufB[:, off:off + 512], op=ALU.add), "v")
                        p.body("vector", voadd)
                        k.mark("oadd_%d" % bank, "v")
                k.mark("w2round", "t")
        k.mark("emask_%d_done" % e)
    k.mark("moe_oT")

    for sc in range(NSC):
        tbt = k.ctr["t"]
        vbt = k.ctr["v"]

        def ttr(t, sc=sc, vbt=vbt):
            k.wait(t, "moe_oT")
            if sc >= 1:
                k.wait(t, "l0f_radd%d" % (sc - 1))
            for cc in range(NDC):
                if cc >= 4:
                    k.wv(t, "v", vbt + cc - 3)
                k.inc(t.transpose(
                    p.psb(cc % 4, P),
                    p.bufB[:, cc * S + sc * P: cc * S + sc * P + P],
                    KC["eye"][:, :]), "t")
        p.body("tensor", ttr)

        def vtr(v, sc=sc, tbt=tbt):
            if sc == 0:
                k.wait(v, "gates")
            for cc in range(NDC):
                k.wv(v, "t", tbt + cc + 1)
                sl = slice(sc * D + cc * P, sc * D + (cc + 1) * P)
                k.inc(v.scalar_tensor_tensor(
                    out=p.xs[:, sl],
                    in0=p.psb(cc % 4, P),
                    scalar=p.stat[:, 16 + sc:17 + sc],
                    in1=p.xs[:, sl],
                    op0=ALU.mult, op1=ALU.add), "v")
        p.body("vector", vtr)
        k.mark("l0f_radd%d" % sc, "v")
    emit_dbg_dumps(p, "l0f", "d_x2")
    emit_ln_phase(p, "l0f", radd_mark="l0f", dbg_name="d_x2" if p.debug else None)


def ph_l1_attn(p):
    k = p
    KC = p.kc

    def qov(q):
        k.dma_barrier(q, "dq")
        k.wait(q, "l0f_ln")
        k.inc(q.dma_start(out=p.kc["rope_cos"][0:NLAT, :], in_=p.C["m_lat"]), "dq")
        k.inc(q.dma_start(out=p.kc["rope_sin"][0:G, P * E:P * E + G * HD],
                          in_=p.C["sel16"]), "dq")
    p.body("sync", qov)
    k.mark("l1consts", "dq")

    for grp in range(2):
        for dc in range(NDC):
            src_ = p.W["qkv_n"][dc * P:(dc + 1) * P,
                                2 * D + 512 * grp:2 * D + 512 * (grp + 1)]
            if dc >= 4:
                gw = "h%d_wv%d" % (grp, dc - 4)
            else:
                gw = "l1qk_done3" if grp == 1 else "l0f_ln"
            p.load_cast(src_, 512, (dc % 4) * 512, dc * 512,
                        "h%d_wv%d" % (grp, dc), wait=gw,
                        wait_cast=("l1qk_done3" if grp == 1 else "l0f_ln"))
        def qvb(q, grp=grp):
            k.dma_barrier(q, "dq")
            k.wait(q, "l0f_ln" if grp == 0 else "l1vb0")
            k.inc(q.dma_start(out=p.smf[0:1, 0:512],
                              in_=p.W["qkv_b"][2 * D + 512 * grp:
                                               2 * D + 512 * (grp + 1)].unsqueeze(0)),
                  "dq")
        p.body("sync", qvb)
        k.mark("l1vb_dma%d" % grp, "dq")

        def vvb(v, grp=grp):
            k.wait(v, "l1vb_dma%d" % grp)
            if grp == 1:
                k.wait(v, "l1heads3")
            k.inc(v.tensor_copy(out=p.smb[0:1, 1536:2048],
                                in_=p.smf[0:1, 0:512]), "v")
        p.body("vector", vvb)
        k.mark("l1vb%d" % grp, "v")

        def vones(v, grp=grp):
            if grp == 1:
                k.wait(v, "l1a_vaug_used0")
            else:
                k.wait(v, "l0f_ln")
            for st in range(NSC):
                k.inc(v.memset(p.bufC[:, st * 528: st * 528 + 528], 1.0), "v")
        p.body("vector", vones)
        k.mark("h%d_vones" % grp, "v")

        tb = k.ctr["t"]
        vb0 = k.ctr["v"]

        def tmm(t, grp=grp, tb=tb, vb0=vb0):
            k.wait(t, "h%d_wv%d" % (grp, NDC - 1))
            k.wait(t, "h%d_vones" % grp)
            k.wait(t, "l1vb%d" % grp)
            k.wait(t, "l1a_xT")
            for st in range(NSC):
                if st >= 2:
                    k.wv(t, "v", vb0 + st - 1)
                for dc in range(NDC):
                    k.inc(t.matmul(
                        p.PS[st % 2][:, 0:512],
                        p.xT[:, dc * S + st * P: dc * S + st * P + P],
                        p.wbf[:, dc * 512:(dc + 1) * 512],
                        start=(dc == 0), stop=False), "t")
                k.inc(t.matmul(
                    p.PS[st % 2][:, 0:512],
                    KC["m_causal"][0:1, 0:P],
                    p.smb[0:1, 1536:2048],
                    start=False, stop=True), "t")
        p.body("tensor", tmm)

        def vevac(v, grp=grp, tb=tb):
            for st in range(NSC):
                k.wv(v, "t", tb + (st + 1) * (NDC + 1))
                dst = p.bufC[:, st * 528: st * 528 + 528]
                dst3 = dst.rearrange("p (h c) -> p h c", c=66)[:, 0:8, 0:64]
                src3 = p.PS[st % 2][:, 0:512].rearrange("p (h c) -> p h c", c=64)
                k.inc(v.tensor_copy(out=dst3, in_=src3), "v")
        p.body("vector", vevac)
        k.mark("h%d_v" % grp, "v")
        if grp == 0:
            k.mark("l1vb_used0")

        for cpi in range(4):
            cp = grp * 4 + cpi
            srcq = p.W["qkv_w"][cp]
            srck = p.W["qkv_w"][8 + cp]
            guard = ("l1qk_done%d" % (cp - 1)) if cpi > 0 else ("h%d_v" % grp)
            wq = p.load_cast(srcq, 1024, 0, 0, "l1wq%d" % cp,
                             wait=guard, wait_cast=guard)
            wk = p.load_cast(srck, 1024, 1024, 1024, "l1wk%d" % cp,
                             wait=guard, wait_cast=guard)
            k.mark("l1_qk_free", "v")
            def qbc(q, cp=cp):
                k.dma_barrier(q, "dq")
                if "l1bc_used" in k.marks:
                    k.wait(q, "l1bc_used")
                k.inc(q.dma_start(out=p.stat[:, 84:85],
                                  in_=p.W["qkv_b"][128 * cp:128 * cp + 128].unsqueeze(1)),
                      "dq")
                k.inc(q.dma_start(out=p.stat[:, 85:86],
                                  in_=p.W["qkv_b"][D + 128 * cp:D + 128 * cp + 128].unsqueeze(1)),
                      "dq")
            p.body("sync", qbc)
            k.mark("l1bc_dma%d" % cp, "dq")

            def vbc0(v, cp=cp):
                k.wait(v, "l1bc_dma%d" % cp)
                k.inc(v.tensor_scalar(out=p.stat[:, 86:87], in0=p.stat[:, 84:85],
                                      scalar1=0.125, scalar2=None,
                                      op0=ALU.mult), "v")
            p.body("vector", vbc0)
            k.mark("l1bc%d" % cp, "v")
            k.mark("l1bc_used", "v")

            for which, wboff in (("k", 1024), ("q", 0)):
                scale = 0.125 if which == "q" else 1.0
                bcol = p.stat[:, 86:87] if which == "q" else p.stat[:, 85:86]
                wlab = ("l1wq%d" % cp) if which == "q" else ("l1wk%d" % cp)
                s_base = k.ctr["s"]

                def tproj(t, wboff=wboff, wlab=wlab, grp=grp, which=which,
                          cp=cp, s_base=s_base):
                    k.wait(t, wlab)
                    k.wait(t, "h%d_v" % grp)
                    k.wait(t, "l1bc%d" % cp)
                    for sb in range(NSB):
                        if sb >= 2:
                            k.wv(t, "s", s_base + sb - 1)
                        for dc in range(NDC):
                            k.inc(t.matmul(
                                p.PS[4 + (sb % 2)][:, 0:512],
                                p.wbf[:, wboff + dc * P: wboff + dc * P + P],
                                p.xT[:, dc * S + sb * 512: dc * S + (sb + 1) * 512],
                                start=(dc == 0), stop=(dc == NDC - 1)), "t")
                p.body("tensor", tproj)
                tb2 = k.ctr["t"]

                def sevac(s, tb2=tb2, scale=scale, cp=cp, bcol=bcol):
                    if cp > 0:
                        k.wait(s, "l1heads%d" % (cp - 1))
                    if "l1bufD_free" in k.marks:
                        k.wait(s, "l1bufD_free")
                    for sb in range(NSB):
                        k.wv(s, "t", tb2 - NSB * NDC + (sb + 1) * NDC)
                        k.inc(s.activation(
                            out=p.bufD[:, sb * 512:(sb + 1) * 512],
                            in_=p.PS[4 + (sb % 2)][:, 0:512],
                            func=ACTF.Identity, scale=scale, bias=bcol), "s")
                p.body("scalar", sevac)
                k.mark("l1raw_%s%d" % (which, cp), "s")

                def vsq(v, which=which, cp=cp):
                    k.wait(v, "l1raw_%s%d" % (which, cp))
                    for sb in range(NSB):
                        sl = slice(sb * 512, (sb + 1) * 512)
                        k.inc(v.tensor_tensor(out=p.smb[:, sl], in0=p.bufD[:, sl],
                                              in1=p.bufD[:, sl], op=ALU.mult), "v")
                p.body("vector", vsq)
                k.mark("l1sq_%s%d" % (which, cp), "v")

                for sb in range(NSB):
                    tb4 = k.ctr["t"]

                    def tns(t, sb=sb, which=which, cp=cp):
                        k.wait(t, "l1sq_%s%d" % (which, cp))
                        if "l1ns_evac" in k.marks:
                            k.wait(t, "l1ns_evac")
                        if cp > 0 and sb == 0 and which == "k":
                            k.wait(t, "l1hbc%d_3" % (2 * cp - 1))
                        k.inc(t.matmul(p.PS[3][0:2, 0:512],
                                       KC["blockones"][:, :],
                                       p.smb[:, sb * 512:(sb + 1) * 512],
                                       start=True, stop=True), "t")
                    p.body("tensor", tns)

                    def vns(v, tb4=tb4, sb=sb, which=which, cp=cp):
                        k.wv(v, "t", tb4 + 1)
                        if which == "q" and sb == 0:
                            k.wait(v, "l1kmax%d" % cp)
                        with p.nc.allow_low_precision(
                                reason="norm bound feeds bf16 stabilizer"):
                            k.inc(v.tensor_copy(
                                out=p.rows[0:2, sb * 512:(sb + 1) * 512],
                                in_=p.PS[3][0:2, 0:512]), "v")
                    p.body("vector", vns)
                    k.mark("l1ns_evac", "v")
                k.mark("l1norm_%s%d" % (which, cp), "v")
                k.mark("l1normmm_prev", "t")

                if which == "k":
                    def vkm(v, cp=cp):
                        k.wait(v, "l1norm_k%d" % cp)
                        k.inc(v.reduce_max(out=p.stat[0:2, 0:1],
                                           in_=p.rows[0:2, :], axis=AX), "v")
                    p.body("vector", vkm)
                    k.mark("l1kmax%d" % cp, "v")

                def vaug(v, which=which, cp=cp):
                    k.wait(v, "l1sq_%s%d" % (which, cp))
                    if cp > 0:
                        k.wait(v, "l1heads%d" % (cp - 1))
                    dst = p.qga if which == "q" else p.kga
                    k.inc(v.tensor_copy(out=dst[0:64, :], in_=p.bufD[0:64, 0:2048]), "v")
                p.body("vector", vaug)
                k.mark("l1augA_%s%d" % (which, cp), "v")

                def gaug(g, which=which, cp=cp):
                    k.dma_barrier(g, "dg")
                    k.wait(g, "l1sq_%s%d" % (which, cp))
                    if cp > 0:
                        k.wait(g, "l1heads%d" % (cp - 1))
                    dst = p.qgb if which == "q" else p.kgb
                    k.inc(g.dma_start(out=dst[0:64, :], in_=p.bufD[64:128, 0:2048]), "dg")
                p.body("gpsimd", gaug)
                k.mark("l1augB_%s%d" % (which, cp), "dg")
                k.mark("l1bufD_free")

            def vcrow1(v, cp=cp):
                k.wait(v, "l1kmax%d" % cp)
                k.wait(v, "l1norm_q%d" % cp)
                with p.nc.allow_low_precision(reason="stabilizer bound"):
                    k.inc(v.tensor_scalar(
                        out=p.rows[0:2, :], in0=p.rows[0:2, :],
                        scalar1=p.stat[0:2, 0:1], scalar2=None,
                        op0=ALU.mult), "v")
            p.body("vector", vcrow1)
            k.mark("l1crow1_%d" % cp, "v")

            def scrow(s, cp=cp):
                k.wait(s, "l1crow1_%d" % cp)
                k.inc(s.activation(out=p.rows[0:2, :], in_=p.rows[0:2, :],
                                   func=ACTF.Sqrt), "s")
            p.body("scalar", scrow)
            k.mark("l1crow2_%d" % cp, "s")

            def vcrow(v, cp=cp):
                k.wait(v, "l1crow2_%d" % cp)
                with p.nc.allow_low_precision(reason="stabilizer bound"):
                    k.inc(v.tensor_scalar(
                        out=p.rows[0:2, :], in0=p.rows[0:2, :],
                        scalar1=-1.0, scalar2=-1.0,
                        op0=ALU.mult, op1=ALU.add), "v")
            p.body("vector", vcrow)
            k.mark("l1crow%d" % cp, "v")

            def gcrow(g, cp=cp):
                k.dma_barrier(g, "dg")
                k.wait(g, "l1crow%d" % cp)
                k.wait(g, "l1augA_q%d" % cp)
                k.wait(g, "l1augB_q%d" % cp)
                k.inc(g.dma_start(out=p.qga[64:65, :], in_=p.rows[0:1, :]), "dg")
                k.inc(g.dma_start(out=p.qgb[64:65, :], in_=p.rows[1:2, :]), "dg")
            p.body("gpsimd", gcrow)
            k.mark("l1aug%d" % cp, "dg")
            k.mark("l1qk_done%d" % cp)

            for hh in range(2):
                h = 2 * cp + hh
                qa = p.qga if hh == 0 else p.qgb
                ka = p.kga if hh == 0 else p.kgb
                hslot = h % 8
                tb5 = k.ctr["t"]
                sb5 = k.ctr["s"]
                vb5 = k.ctr["v"]

                st_sched = []
                for kc in range(NSC):
                    for qb in range(NSB):
                        if 4 * qb - 1 <= kc <= 4 * qb + 3:
                            st_sched.append((kc, qb))
                n_t = len(st_sched)
                silast = {}
                for si, (kc, qb) in enumerate(st_sched):
                    silast[qb] = si

                def live_qcs(kc, qb):
                    out = []
                    for qc in (kc, kc + 1):
                        if 4 * qb <= qc < 4 * qb + 4 and qc < NSC:
                            out.append(qc)
                    return out

                tops = []
                for si in range(n_t):
                    tops.append(("st", si))
                    if si >= 2:
                        av = si - 2
                        tops.append(("av", av))
                        if av == silast[1]:
                            tops.append(("bc", 0))
                        elif av == silast[2]:
                            tops.append(("bc", 1))
                tops.append(("av", n_t - 2))
                tops.append(("av", n_t - 1))
                tops.append(("bc", 2))
                tops.append(("bc", 3))
                tc = {}
                tcnt = tb5
                for opx in tops:
                    kind, xi = opx
                    if kind == "av":
                        tcnt += len(live_qcs(*st_sched[xi]))
                    else:
                        tcnt += 1
                    tc[opx] = tcnt
                vcum = [0] * n_t
                vrec = {}
                vbcp = {}
                vc = vb5
                for si, (kc, qb) in enumerate(st_sched):
                    vc += len(live_qcs(kc, qb))
                    vcum[si] = vc
                    if si == silast[0]:
                        vc += 2; vrec[0] = vc
                    elif si == silast[1]:
                        vc += 2; vrec[1] = vc
                        vc += 2; vbcp[0] = vc
                    elif si == silast[2]:
                        vc += 2; vrec[2] = vc
                        vc += 2; vbcp[1] = vc
                vc += 2; vrec[3] = vc
                vc += 2; vbcp[2] = vc
                vc += 2; vbcp[3] = vc

                def thead(t, tb5=tb5, sb5=sb5, qa=qa, ka=ka, hslot=hslot,
                          st_sched=st_sched, tops=tops, tc=tc, vcum=vcum,
                          vrec=vrec, vbcp=vbcp, cp=cp, h=h,
                          live_qcs=live_qcs):
                    k.wait(t, "l1aug%d" % cp)
                    k.wait(t, "l1augA_k%d" % cp)
                    k.wait(t, "l1augB_k%d" % cp)
                    for opx in tops:
                        kind, xi = opx
                        if kind == "st":
                            si = xi
                            kc, qb = st_sched[si]
                            qcs = live_qcs(kc, qb)
                            lo = (qcs[0] - 4 * qb) * P
                            hi = (qcs[-1] - 4 * qb + 1) * P
                            if si >= 3:
                                k.wv(t, "s", sb5 + si - 2)
                            k.inc(t.matmul(
                                p.PS[si % 3][:, lo:hi],
                                ka[0:65, kc * P:(kc + 1) * P],
                                qa[0:65, qb * 512 + lo:qb * 512 + hi],
                                start=True, stop=True), "t")
                        elif kind == "av":
                            si = xi
                            kc, qb = st_sched[si]
                            first_kc = max(0, 4 * qb - 1)
                            if kc == first_kc and h > 0:
                                k.wait(t, "l1hbc%d_%d" % (h - 1, qb))
                            k.wv(t, "s", sb5 + si + 1)
                            k.wv(t, "v", vcum[si])
                            for qc in live_qcs(kc, qb):
                                colo = (qc - 4 * qb) * P
                                stf = (qc == kc + 1) or (kc == 0 and qc == 0)
                                spf = (qc == kc)
                                k.inc(t.matmul(
                                    p.PS[4 + qb][0:65, colo:colo + P],
                                    p.bufC[:, kc * 528 + hslot * 66:
                                           kc * 528 + hslot * 66 + 65],
                                    p.smb[:, (si % 3) * 512 + colo:
                                           (si % 3) * 512 + colo + P],
                                    start=stf, stop=spf), "t")
                        else:
                            qb = xi
                            k.wv(t, "v", vrec[qb])
                            if qb > 0:
                                k.wv(t, "v", vbcp[qb - 1] - 1)
                            elif h > 0:
                                k.wait(t, "l1hbc%d_3" % (h - 1))
                            k.inc(t.matmul(
                                p.PS[3][0:64, 0:512],
                                p.kc["m_causal"][0:1, 0:64],
                                p.rows[0:1, qb * 512:(qb + 1) * 512],
                                start=True, stop=True), "t")
                p.body("tensor", thead)

                def sexp(s, sb5=sb5, st_sched=st_sched, tc=tc, vcum=vcum,
                         live_qcs=live_qcs):
                    for si in range(len(st_sched)):
                        kc, qb = st_sched[si]
                        qcs = live_qcs(kc, qb)
                        lo = (qcs[0] - 4 * qb) * P
                        hi = (qcs[-1] - 4 * qb + 1) * P
                        tw = tc[("st", si)]
                        if si >= 3:
                            tw = max(tw, tc[("av", si - 3)])
                            k.wv(s, "v", vcum[si - 3])
                        k.wv(s, "t", tw)
                        k.inc(s.activation(
                            out=p.smb[:, (si % 3) * 512 + lo:
                                     (si % 3) * 512 + hi],
                            in_=p.PS[si % 3][:, lo:hi], func=ACTF.Exp), "s")
                p.body("scalar", sexp)

                scr = p.smf[0:64, 512:768].bitcast(BF16)

                def vnorm(v, qb, tc=tc, silast=silast, cp=cp):
                    k.wv(v, "t", tc[("av", silast[qb])])
                    if qb == 0:
                        k.wait(v, "l1aug%d" % cp)
                    with p.nc.allow_low_precision(
                            reason="1/l feeds a bf16 matmul operand"):
                        k.inc(v.tensor_scalar(
                            out=p.rows[0:1, qb * 512:(qb + 1) * 512],
                            in0=p.PS[4 + qb][64:65, 0:512],
                            scalar1=1e-30, scalar2=None, op0=ALU.add), "v")
                        c1 = k.ctr["v"]
                        v.wait_ge(p.sem["v"], c1)
                        k.inc(v.reciprocal(
                            out=p.rows[0:1, qb * 512:(qb + 1) * 512],
                            in_=p.rows[0:1, qb * 512:(qb + 1) * 512]), "v")

                def vbc(v, qb, tc=tc, hh=hh, h=h, scr=scr):
                    k.wv(v, "t", tc[("bc", qb)])
                    if hh == 1 and qb > 0:
                        k.wait(v, "l1hodd%d_%d" % (h, qb - 1))
                    c0 = k.ctr["v"]
                    v.wait_ge(p.sem["v"], c0)
                    with p.nc.allow_low_precision(
                            reason="1/l broadcast feeds bf16 scale"):
                        k.inc(v.tensor_copy(out=scr,
                                            in_=p.PS[3][0:64, 0:512]), "v")
                    c1 = k.ctr["v"]
                    v.wait_ge(p.sem["v"], c1)
                    if hh == 0:
                        dst = p.bufB[0:64, (h // 2) * S + qb * 512:
                                     (h // 2) * S + (qb + 1) * 512]
                    else:
                        dst = p.smb[0:64, 1536:2048]
                    k.inc(v.tensor_tensor(
                        out=dst, in0=p.PS[4 + qb][0:64, 0:512],
                        in1=scr, op=ALU.mult), "v")

                def vseg(v, los, his, recq, bcq, sb5=sb5, st_sched=st_sched,
                         live_qcs=live_qcs):
                    for si in range(los, his):
                        kc, qb = st_sched[si]
                        base = (si % 3) * 512
                        qcs = live_qcs(kc, qb)
                        if qcs:
                            k.wv(v, "s", sb5 + si + 1)
                        for qc in qcs:
                            rel = qc - kc
                            off = base + (qc - 4 * qb) * P
                            mt = p.kc["m_upper"] if rel == 0 else \
                                p.kc["m_lowereq"]
                            k.inc(v.tensor_tensor(
                                out=p.smb[:, off:off + P],
                                in0=p.smb[:, off:off + P],
                                in1=mt[:, :], op=ALU.mult), "v")
                    if recq is not None:
                        vnorm(v, recq)
                        assert k.ctr["v"] == vrec[recq], \
                            ("l1vrec", recq, k.ctr["v"], vrec[recq])
                    if bcq is not None:
                        vbc(v, bcq)
                        assert k.ctr["v"] == vbcp[bcq], \
                            ("l1vbcp", bcq, k.ctr["v"], vbcp[bcq])

                segs = [(0, silast[0] + 1, 0, None),
                        (silast[0] + 1, silast[1] + 1, 1, 0),
                        (silast[1] + 1, silast[2] + 1, 2, 1),
                        (silast[2] + 1, n_t, 3, 2),
                        (n_t, n_t, None, 3)]
                for (los, his, recq, bcq) in segs:
                    p.body("vector",
                           lambda v, los=los, his=his, recq=recq, bcq=bcq:
                           vseg(v, los, his, recq, bcq))
                    if bcq is not None:
                        k.mark("l1hbc%d_%d" % (h, bcq), "v")
                        if hh == 1:
                            def gsc(g, qb=bcq, h=h):
                                k.dma_barrier(g, "dg")
                                k.wait(g, "l1hbc%d_%d" % (h, qb))
                                k.inc(g.dma_start(
                                    out=p.bufB[64:128,
                                               (h // 2) * S + qb * 512:
                                               (h // 2) * S + (qb + 1) * 512],
                                    in_=p.smb[0:64, 1536:2048]), "dg")
                            p.body("gpsimd", gsc)
                            k.mark("l1hodd%d_%d" % (h, bcq), "dg")
                k.mark("l1hnorm%d" % h)
            k.mark("l1heads%d" % cp)
        if grp == 0:
            k.mark("l1a_vaug_used0")
    k.mark("l1a_localT")




def ph_l1_latmix(p):
    k = p
    KC = p.kc
    MLAT = p.kc["rope_cos"]

    tpl = k.ctr["t"]
    for st in range(NSC):
        def vc(v, st=st):
            if st == 0:
                k.wait(v, "l1a_localT")
            if st >= 2:
                k.wait(v, "poolmm%d" % (st - 2))
            k.inc(v.tensor_copy(
                out=p.smb[:, (st % 2) * 1024:(st % 2) * 1024 + 1024],
                in_=p.xs[:, st * D:(st + 1) * D]), "v")
        p.body("vector", vc)
        k.mark("poolc%d" % st, "v")

        def tp(t, st=st):
            k.wait(t, "poolc%d" % st)
            if st == 0:
                k.wait(t, "l1a_localT")
            for dc in range(NDC):
                k.inc(t.matmul(
                    p.PS[6][:, 0:512][:, dc * 64 + 4 * st: dc * 64 + 4 * st + 4],
                    p.smb[:, (st % 2) * 1024 + dc * P:(st % 2) * 1024 + dc * P + P],
                    KC["poolavg"][:, :],
                    start=True, stop=True), "t")
        p.body("tensor", tp)
        k.mark("poolmm%d" % st, "t")
    k.mark("pool_mm", "t")

    def vpool(v):
        k.wait(v, "pool_mm")
        k.inc(v.tensor_copy(out=p.bufD[:, 2048:2560], in_=p.PS[6][:, 0:512]), "v")
    p.body("vector", vpool)
    k.mark("poolT", "v")

    def qrb(q):
        k.dma_barrier(q, "dq")
        k.wait(q, "l1a_localT")
        with p.nc.allow_non_contiguous_dma(reason="tiny bias gathers"):
            k.inc(q.dma_start(out=p.stat[:, 88:92],
                              in_=p.W["rkv_b"].rearrange("(c p) -> p c", p=P)), "dq")
            k.inc(q.dma_start(out=p.stat[:, 92:94],
                              in_=p.W["rq_b"].rearrange("(c p) -> p c", p=P)), "dq")
        k.inc(q.dma_start(out=p.smf[0:1, 0:512], in_=p.W["rkv_b"].unsqueeze(0)),
              "dq")
    p.body("sync", qrb)
    k.mark("l1rb_dma", "dq")

    def vrb(v):
        k.wait(v, "l1rb_dma")
        k.inc(v.tensor_copy(out=p.rows[0:1, 0:512], in_=p.smf[0:1, 0:512]), "v")
        k.inc(v.tensor_scalar(out=p.stat[:, 94:96], in0=p.stat[:, 92:94],
                              scalar1=0.125, scalar2=None, op0=ALU.mult), "v")
    p.body("vector", vrb)
    k.mark("l1rb", "v")

    for fc in range(2):
        src_ = p.W["rkv_w"].rearrange("(c p) f -> p c f", p=P)[:, :, fc * P:(fc + 1) * P]
        p.load_cast(src_, 1024, 0, 0, "rkw%d" % fc,
                    wait=("rkmm%d" % (fc - 1)) if fc else "l1a_localT",
                    wait_cast=("rkmm%d" % (fc - 1)) if fc else "l1heads7",
                    blocks=NDC)
        tb = k.ctr["t"]

        def trk(t, fc=fc):
            k.wait(t, "rkw%d" % fc)
            k.wait(t, "poolT")
            if fc > 0:
                k.wait(t, "rkT%d" % (fc - 1))
            for dc in range(NDC):
                k.inc(t.matmul(p.PS[0][:, 0:64],
                               p.wbf[:, dc * P:(dc + 1) * P],
                               p.bufD[:, 2048 + dc * 64:2048 + dc * 64 + 64],
                               start=(dc == 0), stop=(dc == NDC - 1)), "t")
        p.body("tensor", trk)
        k.mark("rkmm%d" % fc, "t")

        def srk(s, fc=fc):
            k.wait(s, "rkmm%d" % fc)
            k.wait(s, "l1rb")
            k.inc(s.activation(out=p.bufD[:, 2560 + fc * 64:2560 + fc * 64 + 64],
                               in_=p.PS[0][:, 0:64], func=ACTF.Identity,
                               bias=p.stat[:, 88 + fc:89 + fc]), "s")
        p.body("scalar", srk)
        k.mark("rkT%d" % fc, "s")
    k.mark("rkT", "s")

    src_ = p.W["rkv_w"].rearrange("(c p) f -> p c f", p=P)[:, :, 2 * P:4 * P]
    p.load_cast(src_, 2048, 0, 2048, "rvw", wait="rkmm1", wait_cast="rkmm1",
                blocks=NDC)
    tbv = k.ctr["t"]

    def trv(t):
        k.wait(t, "rvw")
        k.wait(t, "rkT")
        k.wait(t, "l1rb")
        for dc in range(NDC):
            k.inc(t.matmul(p.PS[1][0:64, 0:256],
                           p.bufD[:, 2048 + dc * 64:2048 + dc * 64 + 64],
                           p.wbf[:, 2048 + dc * 256:2048 + (dc + 1) * 256],
                           start=(dc == 0), stop=False), "t")
        k.inc(t.matmul(p.PS[1][0:64, 0:256],
                       KC["m_causal"][0:1, 0:64],
                       p.rows[0:1, 256:512],
                       start=False, stop=True), "t")
    p.body("tensor", trv)
    k.mark("rvmm", "t")

    def vrv(v):
        k.wait(v, "rvmm")
        k.inc(v.tensor_copy(out=p.bufD[0:64, 2688:2944], in_=p.PS[1][0:64, 0:256]), "v")
    p.body("vector", vrv)
    k.mark("rvtm", "v")

    def vrva(v):
        k.wait(v, "rvtm")
        k.wait(v, "l1a_localT")
        k.inc(v.memset(p.bufC[0:64, 4096:4360], 1.0), "v")
        c1 = k.ctr["v"]
        v.wait_ge(p.sem["v"], c1)
        dst = p.bufC[0:64, 4096:4360].rearrange("p (l c) -> p l c", c=66)[:, 0:4, 0:64]
        src3 = p.bufD[0:64, 2688:2944].rearrange("p (l c) -> p l c", c=64)
        k.inc(v.tensor_copy(out=dst, in_=src3), "v")
    p.body("vector", vrva)
    k.mark("rvaug", "v")

    for fc in range(2):
        src_ = p.W["rq_w"][fc]
        p.load_cast(src_, 1024, 0, 0, "rqw%d" % fc,
                    wait="rvw" if fc == 0 else "lheads%d" % (fc - 1),
                    wait_cast="rvw" if fc == 0 else "lheads%d" % (fc - 1))
        tbq = k.ctr["t"]
        sq_base = k.ctr["s"]

        def trq(t, fc=fc, sq_base=sq_base):
            k.wait(t, "rqw%d" % fc)
            k.wait(t, "l1a_localT")
            if fc > 0:
                k.wait(t, "lheads%d" % (fc - 1))
            for sb in range(NSB):
                if sb >= 1:
                    k.wv(t, "s", sq_base + sb)
                for dc in range(NDC):
                    k.inc(t.matmul(
                        p.PS[2][:, 0:512],
                        p.wbf[:, dc * P:(dc + 1) * P],
                        p.xT[:, dc * S + sb * 512: dc * S + (sb + 1) * 512],
                        start=(dc == 0), stop=(dc == NDC - 1)), "t")
        p.body("tensor", trq)

        def srq(s, fc=fc, tbq=tbq):
            k.wait(s, "l1rb")
            k.wait(s, "l1a_localT" if fc == 0 else "lheads%d" % (fc - 1))
            for sb in range(NSB):
                k.wv(s, "t", tbq + (sb + 1) * NDC)
                k.inc(s.activation(
                    out=p.bufD[:, sb * 512:(sb + 1) * 512],
                    in_=p.PS[2][:, 0:512], func=ACTF.Identity, scale=0.125,
                    bias=p.stat[:, 94 + fc:95 + fc]), "s")
        p.body("scalar", srq)
        k.mark("rqT%d" % fc, "s")

        def vsq(v, fc=fc):
            k.wait(v, "rqT%d" % fc)
            for sb in range(NSB):
                sl = slice(sb * 512, (sb + 1) * 512)
                k.inc(v.tensor_tensor(out=p.smb[:, sl], in0=p.bufD[:, sl],
                                      in1=p.bufD[:, sl], op=ALU.mult), "v")
        p.body("vector", vsq)
        k.mark("lsq%d" % fc, "v")

        def vsqk(v, fc=fc):
            k.wait(v, "rkT")
            if fc > 0:
                k.wait(v, "lheads%d" % (fc - 1))
            k.inc(v.tensor_tensor(
                out=p.bufD[:, 2944:3008],
                in0=p.bufD[:, 2560 + fc * 64:2560 + fc * 64 + 64],
                in1=p.bufD[:, 2560 + fc * 64:2560 + fc * 64 + 64],
                op=ALU.mult), "v")
        p.body("vector", vsqk)
        k.mark("lsqk%d" % fc, "v")

        tkn = k.ctr["t"]

        def tkn_f(t, fc=fc):
            k.wait(t, "lsqk%d" % fc)
            if "lkn_ev" in k.marks:
                k.wait(t, "lkn_ev")
            k.inc(t.matmul(p.PS[3][0:2, 0:64], KC["blockones"][:, :],
                           p.bufD[:, 2944:3008], start=True, stop=True), "t")
        p.body("tensor", tkn_f)

        def vknev(v, fc=fc, tkn=tkn):
            k.wv(v, "t", tkn + 1)
            with p.nc.allow_low_precision(reason="norm bound feeds stabilizer"):
                k.inc(v.tensor_copy(out=p.rows[0:2, 0:64],
                                    in_=p.PS[3][0:2, 0:64]), "v")
        p.body("vector", vknev)
        k.mark("lkn_ev", "v")
        k.mark("lkn%d" % fc, "v")

        def vkm(v, fc=fc):
            k.wait(v, "lkn%d" % fc)
            k.inc(v.reduce_max(out=p.stat[0:2, 0:1], in_=p.rows[0:2, 0:64],
                               axis=AX), "v")
        p.body("vector", vkm)
        k.mark("lkmax%d" % fc, "v")

        for sb in range(NSB):
            tqn = k.ctr["t"]

            def tqn_f(t, fc=fc, sb=sb):
                k.wait(t, "lsq%d" % fc)
                k.wait(t, "lkn_ev")
                if "lqn_ev" in k.marks:
                    k.wait(t, "lqn_ev")
                k.inc(t.matmul(p.PS[3][0:2, 0:512], KC["blockones"][:, :],
                               p.smb[:, sb * 512:(sb + 1) * 512],
                               start=True, stop=True), "t")
            p.body("tensor", tqn_f)

            def vqn(v, fc=fc, sb=sb, tqn=tqn):
                k.wv(v, "t", tqn + 1)
                if sb == 0:
                    k.wait(v, "lkmax%d" % fc)
                with p.nc.allow_low_precision(
                        reason="norm bound feeds stabilizer"):
                    k.inc(v.tensor_copy(
                        out=p.rows[0:2, sb * 512:(sb + 1) * 512],
                        in_=p.PS[3][0:2, 0:512]), "v")
            p.body("vector", vqn)
            k.mark("lqn_ev", "v")
        k.mark("lqn%d" % fc, "v")

        def vcrow1(v, fc=fc):
            k.wait(v, "lkmax%d" % fc)
            k.wait(v, "lqn%d" % fc)
            with p.nc.allow_low_precision(reason="stabilizer bound"):
                k.inc(v.tensor_scalar(
                    out=p.rows[0:2, :], in0=p.rows[0:2, :],
                    scalar1=p.stat[0:2, 0:1], scalar2=None,
                    op0=ALU.mult), "v")
        p.body("vector", vcrow1)
        k.mark("lcrow1_%d" % fc, "v")

        def scrowl(s, fc=fc):
            k.wait(s, "lcrow1_%d" % fc)
            k.inc(s.activation(out=p.rows[0:2, :], in_=p.rows[0:2, :],
                               func=ACTF.Sqrt), "s")
        p.body("scalar", scrowl)
        k.mark("lcrow2_%d" % fc, "s")

        def vcrow(v, fc=fc):
            k.wait(v, "lcrow2_%d" % fc)
            with p.nc.allow_low_precision(reason="stabilizer bound"):
                k.inc(v.tensor_scalar(
                    out=p.rows[0:2, :], in0=p.rows[0:2, :],
                    scalar1=-1.0, scalar2=-1.0,
                    op0=ALU.mult, op1=ALU.add), "v")
        p.body("vector", vcrow)
        k.mark("lcrow%d" % fc, "v")

        def vaugl(v, fc=fc):
            k.wait(v, "lsq%d" % fc)
            if fc > 0:
                k.wait(v, "lheads%d" % (fc - 1))
            k.inc(v.tensor_copy(out=p.qga[0:64, :], in_=p.bufD[0:64, 0:2048]), "v")
            k.inc(v.tensor_copy(out=p.kga[0:64, 0:64],
                                in_=p.bufD[0:64, 2560 + fc * 64:2560 + fc * 64 + 64]),
                  "v")
        p.body("vector", vaugl)
        k.mark("laugA%d" % fc, "v")

        def gaugl(g, fc=fc):
            k.dma_barrier(g, "dg")
            k.wait(g, "lsq%d" % fc)
            k.wait(g, "rkT")
            if fc > 0:
                k.wait(g, "lheads%d" % (fc - 1))
            k.inc(g.dma_start(out=p.qgb[0:64, :], in_=p.bufD[64:128, 0:2048]), "dg")
            k.inc(g.dma_start(out=p.kgb[0:64, 0:64],
                              in_=p.bufD[64:128, 2560 + fc * 64:2560 + fc * 64 + 64]),
                  "dg")
        p.body("gpsimd", gaugl)
        k.mark("laugB%d" % fc, "dg")

        def gcrowl(g, fc=fc):
            k.dma_barrier(g, "dg")
            k.wait(g, "lcrow%d" % fc)
            k.wait(g, "laugA%d" % fc)
            k.wait(g, "laugB%d" % fc)
            k.inc(g.dma_start(out=p.qga[64:65, :], in_=p.rows[0:1, :]), "dg")
            k.inc(g.dma_start(out=p.qgb[64:65, :], in_=p.rows[1:2, :]), "dg")
        p.body("gpsimd", gcrowl)
        k.mark("laug%d" % fc, "dg")

        for lhh in range(2):
            lh = 2 * fc + lhh
            qa = p.qga if lhh == 0 else p.qgb
            ka = p.kga if lhh == 0 else p.kgb
            tb5 = k.ctr["t"]
            sb5 = k.ctr["s"]
            vb5 = k.ctr["v"]

            def tlath(t, fc=fc, lh=lh, lhh=lhh, qa=qa, ka=ka, sb5=sb5, vb5=vb5):
                k.wait(t, "laug%d" % fc)
                k.wait(t, "laugB%d" % fc)
                if lh > 0:
                    k.wait(t, "lhnorm%d" % (lh - 1))
                k.wait(t, "rvaug")
                for sb in range(NSB):
                    if sb >= 3:
                        k.wv(t, "s", sb5 + sb - 2)
                    k.inc(t.matmul(
                        p.PS[sb % 3][0:64, 0:512],
                        ka[0:65, 0:64],
                        qa[0:65, sb * 512:(sb + 1) * 512],
                        start=True, stop=True), "t")
                for sb in range(NSB):
                    k.wv(t, "s", sb5 + sb + 1)
                    k.wv(t, "v", vb5 + sb + 1)
                    k.inc(t.matmul(
                        p.PS[4 + sb][0:65, 0:512],
                        p.bufC[0:64, 4096 + lh * 66:4096 + lh * 66 + 65],
                        p.smb[:, (sb % 3) * 512:(sb % 3) * 512 + 512][0:64, :],
                        start=True, stop=True), "t")
            p.body("tensor", tlath)
            tlat_end = k.ctr["t"]

            def slexp(s, fc=fc, lh=lh, tb5=tb5, vb5=vb5):
                for sb in range(NSB):
                    tw = tb5 + sb + 1
                    if sb >= 3:
                        tw = max(tw, tb5 + NSB + (sb - 3) + 1)
                        k.wv(s, "v", vb5 + sb - 2)
                    k.wv(s, "t", tw)
                    k.inc(s.activation(
                        out=p.smb[:, (sb % 3) * 512:(sb % 3) * 512 + 512][0:64, :],
                        in_=p.PS[sb % 3][0:64, 0:512], func=ACTF.Exp), "s")
            p.body("scalar", slexp)

            def vlmask(v, fc=fc, lh=lh, sb5=sb5):
                for sb in range(NSB):
                    k.wv(v, "s", sb5 + sb + 1)
                    k.inc(v.tensor_tensor(
                        out=p.smb[:, (sb % 3) * 512:(sb % 3) * 512 + 512][0:64, :],
                        in0=p.smb[:, (sb % 3) * 512:(sb % 3) * 512 + 512][0:64, :],
                        in1=MLAT[0:64, sb * 512:(sb + 1) * 512], op=ALU.mult), "v")
            p.body("vector", vlmask)
            k.mark("lav%d" % lh, "t")

            def vlnorm(v, lh=lh, fc=fc, tlat_end=tlat_end):
                k.wv(v, "t", tlat_end)
                k.wait(v, "laug%d" % fc)
                with p.nc.allow_low_precision(reason="1/l bf16 operand"):
                    for sb in range(NSB):
                        k.inc(v.tensor_scalar(
                            out=p.rows[0:1, sb * 512:(sb + 1) * 512],
                            in0=p.PS[4 + sb][64:65, 0:512],
                            scalar1=1e-30, scalar2=None, op0=ALU.add), "v")
                    c1 = k.ctr["v"]
                    v.wait_ge(p.sem["v"], c1)
                    for sb in range(NSB):
                        k.inc(v.reciprocal(
                            out=p.rows[0:1, sb * 512:(sb + 1) * 512],
                            in_=p.rows[0:1, sb * 512:(sb + 1) * 512]), "v")
            p.body("vector", vlnorm)
            k.mark("lhrec%d" % lh, "v")
            for sb in range(NSB):
                tb6 = k.ctr["t"]

                def tlbc(t, lh=lh, sb=sb):
                    k.wait(t, "lhrec%d" % lh)
                    if sb > 0:
                        k.wait(t, "lhbc%d_%d" % (lh, sb - 1))
                    k.inc(t.matmul(p.PS[3][0:64, 0:512],
                                   p.kc["m_causal"][0:1, 0:64],
                                   p.rows[0:1, sb * 512:(sb + 1) * 512],
                                   start=True, stop=True), "t")
                p.body("tensor", tlbc)
                tbc_done = k.ctr["t"]

                def vlbc(v, tbc_done=tbc_done, lh=lh, sb=sb, lhh=lhh, fc=fc):
                    k.wv(v, "t", tbc_done)
                    if lhh == 1 and sb > 0:
                        k.wait(v, "lhodd%d_%d" % (lh, sb - 1))
                    k.inc(v.tensor_copy(out=p.smb[0:64, 1024:1536],
                                        in_=p.PS[3][0:64, 0:512]), "v")
                    c1 = k.ctr["v"]
                    v.wait_ge(p.sem["v"], c1)
                    if lhh == 0:
                        dst = p.bufC[0:64, fc * S + sb * 512:fc * S + (sb + 1) * 512]
                    else:
                        dst = p.smb[0:64, 1536:2048]
                    k.inc(v.tensor_tensor(
                        out=dst, in0=p.PS[4 + sb][0:64, 0:512],
                        in1=p.smb[0:64, 1024:1536], op=ALU.mult), "v")
                p.body("vector", vlbc)
                k.mark("lhbc%d_%d" % (lh, sb), "v")

                if lhh == 1:
                    def glsc(g, sb=sb, lh=lh, fc=fc):
                        k.dma_barrier(g, "dg")
                        k.wait(g, "lhbc%d_%d" % (lh, sb))
                        k.inc(g.dma_start(
                            out=p.bufC[64:128, fc * S + sb * 512:
                                       fc * S + (sb + 1) * 512],
                            in_=p.smb[0:64, 1536:2048]), "dg")
                    p.body("gpsimd", glsc)
                    k.mark("lhodd%d_%d" % (lh, sb), "dg")
            k.mark("lhnorm%d" % lh)
        k.mark("lheads%d" % fc)
    k.mark("lat_outT")


    def qgw(q):
        k.dma_barrier(q, "dq")
        k.wait(q, "lat_outT")
        k.inc(q.dma_start(out=p.stat[0:64, 84:85], in_=p.W["gl_w"]), "dq")
        k.inc(q.dma_start(out=p.stat[64:128, 84:85], in_=p.W["gl_w"]), "dq")
        k.inc(q.dma_start(out=p.stat[0:64, 85:86], in_=p.W["gr_w"]), "dq")
        k.inc(q.dma_start(out=p.stat[64:128, 85:86], in_=p.W["gr_w"]), "dq")
        with p.nc.allow_non_contiguous_dma(reason="tiny rout_b gather"):
            k.inc(q.dma_start(out=p.stat[0:64, 0:16],
                              in_=p.W["rout_b"].rearrange("(h j) -> j h", j=HD)),
                  "dq")
        with p.nc.allow_non_contiguous_dma(reason="tiny rout_b col gather"):
            k.inc(q.dma_start(out=p.stat[:, 16:24],
                              in_=p.W["rout_b"].rearrange("(c p) -> p c", p=P)),
                  "dq")
        k.inc(q.dma_start(out=p.smf[0:1, 0:D], in_=p.W["out_b"].unsqueeze(0)), "dq")
    p.body("sync", qgw)
    k.mark("gw_dma", "dq")

    def vgw(v):
        k.wait(v, "gw_dma")
        k.inc(v.tensor_copy(out=p.smb[:, 1568:1569], in_=p.stat[:, 84:85]), "v")
        k.inc(v.tensor_copy(out=p.smb[:, 1569:1570], in_=p.stat[:, 85:86]), "v")
        k.inc(v.memset(p.smb[:, 1586:1590], 0.0), "v")
        c1 = k.ctr["v"]
        v.wait_ge(p.sem["v"], c1)
        k.inc(v.tensor_copy(out=p.smb[0:64, 1586:1587], in_=p.smb[0:64, 1568:1569]), "v")
        k.inc(v.tensor_copy(out=p.smb[64:128, 1587:1588], in_=p.smb[64:128, 1568:1569]), "v")
        k.inc(v.tensor_copy(out=p.smb[0:64, 1588:1589], in_=p.smb[0:64, 1569:1570]), "v")
        k.inc(v.tensor_copy(out=p.smb[64:128, 1589:1590], in_=p.smb[64:128, 1569:1570]), "v")
        k.inc(v.tensor_scalar(out=p.smb[0:64, 1570:1586], in0=p.stat[0:64, 0:16],
                              scalar1=p.stat[0:64, 85:86], scalar2=None,
                              op0=ALU.mult), "v")
        k.inc(v.tensor_copy(out=p.rows[0:1, 0:1024], in_=p.smf[0:1, 0:D]), "v")
    p.body("vector", vgw)
    k.mark("gw", "v")

    def trbg(t):
        k.wait(t, "gw")
        k.inc(t.matmul(p.PS[0][0:1, 0:16], KC["m_causal"][0:64, 127:128],
                       p.smb[0:64, 1570:1586], start=True, stop=True), "t")
    p.body("tensor", trbg)
    k.mark("rbg_mm", "t")

    def vrbg(v):
        k.wait(v, "rbg_mm")
        k.inc(v.tensor_copy(out=p.rows[0:1, 1024:1040], in_=p.PS[0][0:1, 0:16]), "v")
    p.body("vector", vrbg)
    k.mark("rbg", "v")

    for fc in range(2):
        src_ = p.W["rout_wT"][fc]
        p.load_cast(src_, 1024, 0, 0, "rwt%d" % fc,
                    wait="lat_outT" if fc == 0 else "rwt0",
                    wait_cast="lat_outT" if fc == 0 else "wrmm0")
        tbw = k.ctr["t"]

        def twr(t, fc=fc):
            k.wait(t, "rwt%d" % fc)
            k.wait(t, "gw")
            for h in range(NH):
                cc = h // 2
                off = (h % 2) * 64
                k.inc(t.matmul(
                    p.PS[1][:, fc * NH + h: fc * NH + h + 1],
                    p.wbf[off:off + 64, cc * P:(cc + 1) * P],
                    p.smb[off:off + 64, 1569:1570],
                    start=True, stop=True), "t")
        p.body("tensor", twr)
        k.mark("wrmm%d" % fc, "t")

    def vwr(v):
        k.wait(v, "wrmm1")
        k.inc(v.tensor_copy(out=p.smb[:, 1536:1568], in_=p.PS[1][:, 0:32]), "v")
    p.body("vector", vwr)
    k.mark("WR", "v")

    for sc in range(NSC):
        tbg = k.ctr["t"]

        def tgp(t, sc=sc):
            k.wait(t, "WR")
            k.wait(t, "rbg")
            k.wait(t, "l1a_localT")
            if sc >= 2:
                k.wait(t, "gsig%d" % (sc - 2))
            bank = 2 + sc % 2
            for c in range(NDC):
                k.inc(t.matmul(p.PS[bank][:, 2 * c:2 * c + 2],
                               p.bufB[:, c * S + sc * P: c * S + sc * P + P],
                               p.smb[:, 1586:1588],
                               start=(c == 0), stop=False), "t")
            for fc in range(2):
                k.inc(t.matmul(p.PS[bank][:, 0:16],
                               p.bufC[:, fc * S + sc * P: fc * S + sc * P + P],
                               p.smb[:, 1536 + fc * NH:1536 + fc * NH + NH],
                               start=False, stop=False), "t")
            k.inc(t.matmul(p.PS[bank][:, 0:16],
                           KC["m_causal"][0:1, 0:P],
                           p.rows[0:1, 1024:1040],
                           start=False, stop=True), "t")
        p.body("tensor", tgp)
        k.mark("gpre%d" % sc, "t")

        def sgs(s, sc=sc):
            k.wait(s, "gpre%d" % sc)
            bank = 2 + sc % 2
            k.inc(s.activation(out=p.smb[:, 128 + sc * NH:128 + sc * NH + NH],
                               in_=p.PS[bank][:, 0:16], func=ACTF.Sigmoid), "s")
        p.body("scalar", sgs)
        k.mark("gsig%d" % sc, "s")
    k.mark("gates_tm")


    for sc in range(NSC):
        tbt = k.ctr["t"]

        def tgt(t, sc=sc):
            k.wait(t, "gsig%d" % sc)
            if sc >= 2:
                k.wait(t, "gtT%d" % (sc - 2))
            k.inc(t.transpose(p.psb(4 + sc % 2, P)[0:NH, 0:P],
                              p.smb[:, 128 + sc * NH:128 + sc * NH + NH],
                              KC["eye"][:, :]), "t")
        p.body("tensor", tgt)
        gt_done = k.ctr["t"]

        def vgt(v, sc=sc, gt_done=gt_done):
            k.wv(v, "t", gt_done)
            k.inc(v.tensor_copy(out=p.bufD[0:NH, sc * P:(sc + 1) * P],
                                in_=p.psb(4 + sc % 2, P)[0:NH, 0:P]), "v")
        p.body("vector", vgt)
        k.mark("gtT%d" % sc, "v")
    k.mark("gateT")

    for c in range(NDC):
        src_ = p.W["rout_w"][c]
        p.load_cast(src_, 256, 0, 0, "row%d" % c,
                    wait=("mix%d" % (c - 1)) if c else "gateT",
                    wait_cast=("mix%d" % (c - 1)) if c else "gateT")
        for sb in range(NSB):
            tbr = k.ctr["t"]

            def tre(t, c=c, sb=sb):
                k.wait(t, "row%d" % c)
                if sb == 0 and c == 0:
                    k.wait(t, "gateT")
                if "mixv%d_%d" % (c, sb - 1) in k.marks:
                    k.wait(t, "mixv%d_%d" % (c, sb - 1))
                elif c > 0:
                    k.wait(t, "mixv%d_%d" % (c - 1, NSB - 1))
                for fc in range(2):
                    k.inc(t.matmul(p.PS[0][:, 0:512],
                                   p.wbf[:, fc * P:(fc + 1) * P],
                                   p.bufC[:, fc * S + sb * 512: fc * S + (sb + 1) * 512],
                                   start=(fc == 0), stop=(fc == 1)), "t")
                SEL = p.kc["rope_sin"]
                k.inc(t.matmul(p.PS[1][0:64, 0:512],
                               SEL[0:G, 1024 + (2 * c) * 64:1024 + (2 * c) * 64 + 64],
                               p.bufD[0:NH, sb * 512:(sb + 1) * 512],
                               start=True, stop=True), "t")
                k.inc(t.matmul(p.PS[1][64:128, 0:512],
                               SEL[0:G, 1024 + (2 * c + 1) * 64:1024 + (2 * c + 1) * 64 + 64],
                               p.bufD[0:NH, sb * 512:(sb + 1) * 512],
                               start=True, stop=True, tile_position=(0, 64)), "t")
            p.body("tensor", tre)
            k.mark("rem%d_%d" % (c, sb), "t")

            def sre(s, c=c, sb=sb):
                k.wait(s, "rem%d_%d" % (c, sb))
                k.inc(s.activation(out=p.smb[:, 0:512], in_=p.PS[0][:, 0:512],
                                   func=ACTF.Identity, bias=p.stat[:, 16 + c:17 + c]),
                      "s")
            p.body("scalar", sre)
            k.mark("remb%d_%d" % (c, sb), "s")

            def vmix(v, c=c, sb=sb):
                k.wait(v, "remb%d_%d" % (c, sb))
                k.inc(v.tensor_tensor(
                    out=p.smb[:, 512:1024],
                    in0=p.bufB[:, c * S + sb * 512: c * S + (sb + 1) * 512],
                    in1=p.smb[:, 0:512], op=ALU.subtract), "v")
                c1 = k.ctr["v"]
                v.wait_ge(p.sem["v"], c1)
                k.inc(v.tensor_tensor(
                    out=p.smb[:, 1024:1536], in0=p.PS[1][:, 0:512],
                    in1=p.smb[:, 512:1024], op=ALU.mult), "v")
                c2 = k.ctr["v"]
                v.wait_ge(p.sem["v"], c2)
                k.inc(v.tensor_tensor(
                    out=p.xT[:, c * S + sb * 512: c * S + (sb + 1) * 512],
                    in0=p.smb[:, 0:512], in1=p.smb[:, 1024:1536], op=ALU.add), "v")
            p.body("vector", vmix)
            k.mark("mixv%d_%d" % (c, sb), "v")
        k.mark("mix%d" % c, "v")
    k.mark("mixedT")

    for ic in range(NDC):
        src_ = p.W["out_w"][ic * P:(ic + 1) * P, :]
        p.load_cast(src_, 1024, (ic % 2) * 1024, 0, "ow_st%d" % ic,
                    wait=("ow_mv%d" % (ic - 2)) if ic >= 2 else "mixedT",
                    wait_cast=("ow_mv%d" % (ic - 1)) if ic >= 1 else "mixedT")

        def vmv(v, ic=ic):
            k.wait(v, "ow_st%d" % ic)
            if ic == 0:
                k.wait(v, "lat_outT")
                k.wait(v, "mixedT")
            k.inc(v.tensor_copy(out=p.bufC[:, ic * 1024:(ic + 1) * 1024],
                                in_=p.wbf[:, 0:1024]), "v")
        p.body("vector", vmv)
        k.mark("ow_mv%d" % ic, "v")
    k.mark("ow_res", "v")

    ph_proj_res_ln(p, "l1a", lambda ic: p.xT[:, ic * S:(ic + 1) * S],
                   lambda ic: p.bufC[:, ic * 1024:(ic + 1) * 1024],
                   "ow_res", "mixedT", dbg_pre="d_x3",
                   extra_bias_row=p.rows[0:1, 0:1024])




def ph_l1_ffn(p):
    k = p
    KC = p.kc

    emit_ln_phase(p, "zn",
                  apply_out=lambda sc: p.bufB[:, sc * D:(sc + 1) * D],
                  src_mark="l1a_ln")
    k.mark("zn_done", "v")

    p.marks["l1f_xT_src"] = dict(p.marks["zn_done"])
    ph_transpose(p, "l1f_xT", do_cast=False)

    def vzs(v):
        k.wait(v, "l1f_xT")
        for dc in range(NDC):
            k.inc(v.reduce_sum(out=p.stat[:, 32 + dc:33 + dc],
                               in_=p.xT[:, dc * S:(dc + 1) * S], axis=AX), "v")
    p.body("vector", vzs)
    k.mark("zsent", "v")

    def qpar(q):
        k.dma_barrier(q, "dq")
        k.wait(q, "zn_done")
        with p.nc.allow_non_contiguous_dma(reason="tiny param gathers"):
            k.inc(q.dma_start(out=p.stat[:, 40:48],
                              in_=p.W["ln_g"].rearrange("(c p) -> p c", p=P)), "dq")
            k.inc(q.dma_start(out=p.stat[:, 48:56],
                              in_=p.W["ln_b"].rearrange("(c p) -> p c", p=P)), "dq")
            k.inc(q.dma_start(out=p.stat[:, 56:64],
                              in_=p.W["bn_g"].rearrange("(c p) -> p c", p=P)), "dq")
            k.inc(q.dma_start(out=p.stat[:, 64:72],
                              in_=p.W["bn_b"].rearrange("(c p) -> p c", p=P)), "dq")
            k.inc(q.dma_start(out=p.stat[:, 72:80],
                              in_=p.W["bn_mean"].rearrange("(c p) -> p c", p=P)), "dq")
            k.inc(q.dma_start(out=p.stat[:, 80:88],
                              in_=p.W["bn_var"].rearrange("(c p) -> p c", p=P)), "dq")
            k.inc(q.dma_start(out=p.stat[:, 16:20],
                              in_=p.W["bd"].rearrange("(c p) -> p c", p=P)), "dq")
        k.inc(q.dma_start(out=p.smf[0:1, 0:E], in_=p.W["br"].unsqueeze(0)), "dq")
        k.inc(q.dma_start(out=p.wst[0:E, 0:D], in_=p.W["bu"]), "dq")
    p.body("sync", qpar)
    k.mark("l1f_par", "dq")

    def vpar(v):
        k.wait(v, "l1f_par")
        k.inc(v.tensor_copy(out=p.smb[:, 16:24], in_=p.stat[:, 48:56]), "v")
        k.inc(v.tensor_copy(out=p.smb[:, 24:32], in_=p.stat[:, 40:48]), "v")
        k.inc(v.tensor_copy(out=p.rows[0:1, 1040:1048], in_=p.smf[0:1, 0:E]), "v")
        k.inc(v.tensor_copy(out=p.smb[0:E, 1024:2048], in_=p.wst[0:E, 0:D]), "v")
    p.body("vector", vpar)
    k.mark("l1f_parb", "v")

    def vzb(v):
        k.wait(v, "zsent")
        k.wait(v, "l1f_parb")
        k.inc(v.tensor_scalar(out=p.stat[:, 0:8], in0=p.stat[:, 32:40],
                              scalar1=1.0 / S, scalar2=None, op0=ALU.mult), "v")
        c1 = k.ctr["v"]
        v.wait_ge(p.sem["v"], c1)
        k.inc(v.tensor_tensor(out=p.stat[:, 0:8], in0=p.stat[:, 0:8],
                              in1=p.stat[:, 40:48], op=ALU.mult), "v")
        c2 = k.ctr["v"]
        v.wait_ge(p.sem["v"], c2)
        k.inc(v.tensor_tensor(out=p.stat[:, 0:8], in0=p.stat[:, 0:8],
                              in1=p.stat[:, 48:56], op=ALU.add), "v")
        k.inc(v.tensor_scalar(out=p.stat[:, 8:16], in0=p.stat[:, 80:88],
                              scalar1=EPS, scalar2=None, op0=ALU.add), "v")
    p.body("vector", vzb)
    k.mark("zb1", "v")

    def szb(s):
        k.wait(s, "zb1")
        k.inc(s.activation(out=p.stat[:, 8:16], in_=p.stat[:, 8:16],
                           func=ACTF.Sqrt), "s")
    p.body("scalar", szb)
    k.mark("zb2", "s")

    def vzb2(v):
        k.wait(v, "zb2")
        k.inc(v.reciprocal(out=p.stat[:, 8:16], in_=p.stat[:, 8:16]), "v")
        c1 = k.ctr["v"]
        v.wait_ge(p.sem["v"], c1)
        k.inc(v.tensor_tensor(out=p.stat[:, 0:8], in0=p.stat[:, 0:8],
                              in1=p.stat[:, 72:80], op=ALU.subtract), "v")
        c2 = k.ctr["v"]
        v.wait_ge(p.sem["v"], c2)
        k.inc(v.tensor_tensor(out=p.stat[:, 0:8], in0=p.stat[:, 0:8],
                              in1=p.stat[:, 8:16], op=ALU.mult), "v")
        c3 = k.ctr["v"]
        v.wait_ge(p.sem["v"], c3)
        k.inc(v.tensor_tensor(out=p.stat[:, 0:8], in0=p.stat[:, 0:8],
                              in1=p.stat[:, 56:64], op=ALU.mult), "v")
        c4 = k.ctr["v"]
        v.wait_ge(p.sem["v"], c4)
        k.inc(v.tensor_tensor(out=p.stat[:, 0:8], in0=p.stat[:, 0:8],
                              in1=p.stat[:, 64:72], op=ALU.add), "v")
        c5 = k.ctr["v"]
        v.wait_ge(p.sem["v"], c5)
        with p.nc.allow_low_precision(reason="bf16 matmul operand"):
            k.inc(v.tensor_copy(out=p.smb[:, 32:40], in_=p.stat[:, 0:8]), "v")
    p.body("vector", vzb2)
    k.mark("zbn", "v")

    src_ = p.W["wr"].rearrange("(c p) e -> p c e", p=P)
    p.load_cast(src_, NDC * E, 0, 1024, "wrw", wait="l1f_parb",
                wait_cast="l1f_parb", blocks=NDC)

    def tlg(t):
        k.wait(t, "wrw")
        k.wait(t, "zbn")
        k.wait(t, "l1f_xT")
        for dc in range(NDC):
            k.inc(t.matmul(p.PS[0][0:1, 0:E],
                           p.smb[:, 32 + dc:33 + dc],
                           p.wbf[:, 1024 + dc * E:1024 + (dc + 1) * E],
                           start=(dc == 0), stop=False), "t")
        k.inc(t.matmul(p.PS[0][0:1, 0:E],
                       KC["m_causal"][0:1, 0:1],
                       p.rows[0:1, 1040:1048],
                       start=False, stop=True), "t")
    p.body("tensor", tlg)
    k.mark("l1f_lg", "t")

    def vlg(v):
        k.wait(v, "l1f_lg")
        k.inc(v.reduce_max(out=p.stat[0:1, 88:89], in_=p.PS[0][0:1, 0:E],
                           axis=AX, negate=True), "v")
        k.inc(v.reduce_max(out=p.stat[0:1, 89:90], in_=p.PS[0][0:1, 0:E],
                           axis=AX), "v")
    p.body("vector", vlg)
    k.mark("l1f_lgm", "v")

    def slg(s):
        k.wait(s, "l1f_lgm")
        k.inc(s.activation(out=p.smf[0:1, 0:E], in_=p.PS[0][0:1, 0:E],
                           func=ACTF.Exp, bias=p.stat[0:1, 88:89],
                           accum_out=p.stat[0:1, 90:91]), "s")
    p.body("scalar", slg)
    k.mark("l1f_lge", "s")

    def vlg2(v):
        k.wait(v, "l1f_lge")
        k.inc(v.reciprocal(out=p.stat[0:1, 91:92], in_=p.stat[0:1, 90:91]), "v")
        k.inc(v.tensor_scalar(out=p.smb[0:1, 0:E], in0=p.PS[0][0:1, 0:E],
                              scalar1=p.stat[0:1, 89:90], scalar2=None,
                              op0=ALU.is_equal), "v")
        c1 = k.ctr["v"]
        v.wait_ge(p.sem["v"], c1)
        with p.nc.allow_low_precision(reason="gate bf16 bcast operand"):
            k.inc(v.tensor_copy(out=p.smb[0:1, 40:41], in_=p.stat[0:1, 91:92]), "v")
    p.body("vector", vlg2)
    k.mark("l1f_gate", "v")

    def tmt(t):
        k.wait(t, "l1f_gate")
        k.inc(t.transpose(p.psb(1, 1)[0:E, 0:1], p.smb[0:1, 0:E],
                          KC["eye"][0:1, 0:1]), "t")
    p.body("tensor", tmt)
    k.mark("l1f_mtT", "t")

    def vmt(v):
        k.wait(v, "l1f_mtT")
        k.inc(v.tensor_copy(out=p.smb[0:E, 8:9], in_=p.psb(1, 1)[0:E, 0:1]), "v")
    p.body("vector", vmt)
    k.mark("l1f_mt", "v")

    tmc_b = k.ctr["t"]

    def tmc(t):
        k.wait(t, "l1f_mt")
        for j in range(4):
            k.inc(t.matmul(p.PS[2][:, j:j + 1],
                           KC["exp8"][0:E, j * P:(j + 1) * P],
                           p.smb[0:E, 8:9], start=True, stop=True), "t")
    p.body("tensor", tmc)
    k.mark("l1f_mc_mm", "t")

    def vmc(v):
        k.wait(v, "l1f_mc_mm")
        k.inc(v.tensor_copy(out=p.stat[:, 32:36], in_=p.PS[2][:, 0:4]), "v")
    p.body("vector", vmc)
    k.mark("l1f_mc", "v")

    def tgb(t):
        k.wait(t, "l1f_gate")
        k.wait(t, "l1f_mc_mm")
        k.inc(t.matmul(p.PS[3][:, 0:1], KC["m_causal"][0:1, 0:P],
                       p.smb[0:1, 40:41], start=True, stop=True), "t")
    p.body("tensor", tgb)
    k.mark("l1f_gb_mm", "t")

    def vgb(v):
        k.wait(v, "l1f_gb_mm")
        k.inc(v.tensor_copy(out=p.stat[:, 95:96], in_=p.PS[3][:, 0:1]), "v")
    p.body("vector", vgb)
    k.mark("l1f_gb", "v")

    for dc in range(NDC):
        src_ = p.W["wd"][dc * P:(dc + 1) * P, :]
        p.load_cast(src_, 512, (dc % 4) * 512, dc * 512, "wdl%d" % dc,
                    wait=("wdl%d" % (dc - 4)) if dc >= 4 else "wrw",
                    wait_cast="wrw")
    for hj in range(4):
        tbx = k.ctr["t"]

        def tbx_f(t, hj=hj):
            k.wait(t, "wdl%d" % (NDC - 1))
            k.wait(t, "l1f_parb")
            k.wait(t, "l1f_mc")
            for dc in range(NDC):
                k.inc(t.matmul(p.PS[2][:, hj:hj + 1],
                               p.wbf[:, dc * 512 + hj * P: dc * 512 + hj * P + P],
                               p.smb[:, 16 + dc:17 + dc],
                               start=(dc == 0), stop=(dc == NDC - 1)), "t")
        p.body("tensor", tbx_f)
    k.mark("l1f_bdx_mm", "t")

    def vbdx(v):
        k.wait(v, "l1f_bdx_mm")
        k.inc(v.tensor_tensor(out=p.stat[:, 16:20], in0=p.PS[2][:, 0:4],
                              in1=p.stat[:, 16:20], op=ALU.add), "v")
    p.body("vector", vbdx)
    k.mark("l1f_bdtot", "v")

    def vwds(v):
        k.wait(v, "wdl%d" % (NDC - 1))
        k.wait(v, "l1f_bdx_mm")
        k.wait(v, "l1f_parb")
        for dc in range(NDC):
            k.inc(v.tensor_scalar(out=p.wbf[:, dc * 512:(dc + 1) * 512],
                                  in0=p.wbf[:, dc * 512:(dc + 1) * 512],
                                  scalar1=p.stat[:, 40 + dc:41 + dc],
                                  scalar2=None, op0=ALU.mult), "v")
    p.body("vector", vwds)
    k.mark("l1f_wdp", "v")

    for hj in range(4):
        for sb in range(NSB):
            tbh = k.ctr["t"]

            def th(t, hj=hj, sb=sb):
                k.wait(t, "l1f_wdp")
                k.wait(t, "l1f_xT")
                ob = "l1f_hm%d_%d" % (hj, sb - 1) if sb > 0 else (
                    "l1f_hm%d_%d" % (hj - 1, NSB - 1) if hj > 0 else None)
                if ob is not None and ob in k.marks:
                    pass
                bank = 4 + (sb % 2)
                if "l1f_hrl%d" % bank in k.marks:
                    k.wait(t, "l1f_hrl%d" % bank)
                for dc in range(NDC):
                    k.inc(t.matmul(p.PS[bank][:, 0:512],
                                   p.wbf[:, dc * 512 + hj * P: dc * 512 + hj * P + P],
                                   p.xT[:, dc * S + sb * 512: dc * S + (sb + 1) * 512],
                                   start=(dc == 0), stop=(dc == NDC - 1)), "t")
            p.body("tensor", th)
            th_done = k.ctr["t"]

            def sh(s, hj=hj, sb=sb, th_done=th_done):
                k.wv(s, "t", th_done)
                k.wait(s, "l1f_bdtot")
                bank = 4 + (sb % 2)
                k.inc(s.activation(
                    out=p.bufC[:, hj * S + sb * 512: hj * S + (sb + 1) * 512],
                    in_=p.PS[bank][:, 0:512], func=ACTF.Relu,
                    bias=p.stat[:, 16 + hj:17 + hj]), "s")
            p.body("scalar", sh)
            k.mark("l1f_hr%d_%d" % (hj, sb), "s")
            k.mark("l1f_hrl%d" % (4 + sb % 2), "s")

            def vh(v, hj=hj, sb=sb):
                k.wait(v, "l1f_hr%d_%d" % (hj, sb))
                k.wait(v, "l1f_mc")
                k.inc(v.tensor_scalar(
                    out=p.bufC[:, hj * S + sb * 512: hj * S + (sb + 1) * 512],
                    in0=p.bufC[:, hj * S + sb * 512: hj * S + (sb + 1) * 512],
                    scalar1=p.stat[:, 32 + hj:33 + hj], scalar2=None,
                    op0=ALU.mult), "v")
            p.body("vector", vh)
            k.mark("l1f_hm%d_%d" % (hj, sb), "v")
    k.mark("l1f_hm")

    for hj in range(4):
        src_ = p.W["wu"][hj * P:(hj + 1) * P, :]
        p.load_cast(src_, 1024, (hj % 2) * 1024, hj * 1024, "wul%d" % hj,
                    wait=("wul%d" % (hj - 2)) if hj >= 2 else "l1f_hm",
                    wait_cast="l1f_hm" if hj == 0 else None)
    tbu = k.ctr["t"]

    def tbu_f(t):
        k.wait(t, "l1f_mt")
        k.wait(t, "l1f_parb")
        k.wait(t, "l1f_gb_mm")
        for half in range(2):
            k.inc(t.matmul(p.PS[6 + half][0:1, 0:512],
                           p.smb[0:E, 8:9],
                           p.smb[0:E, 1024 + half * 512:1024 + (half + 1) * 512],
                           start=True, stop=True), "t")
    p.body("tensor", tbu_f)
    k.mark("l1f_busel_mm", "t")

    def vbu(v, tbu=tbu):
        k.wait(v, "l1f_busel_mm")
        k.inc(v.tensor_copy(out=p.rows[0:1, 1024:1536], in_=p.PS[6][0:1, 0:512]), "v")
        k.inc(v.tensor_copy(out=p.rows[0:1, 1536:2048], in_=p.PS[7][0:1, 0:512]), "v")
    p.body("vector", vbu)
    k.mark("l1f_busel", "v")

    for sc in range(NSC):
        tbo = k.ctr["t"]

        def to_f(t, sc=sc):
            k.wait(t, "wul3")
            k.wait(t, "l1f_hm")
            k.wait(t, "l1f_busel")
            if sc >= 1:
                k.wait(t, "l1f_radd%d" % (sc - 1))
            for half in range(2):
                for hj in range(4):
                    k.inc(t.matmul(
                        p.PS[half][:, 0:512],
                        p.bufC[:, hj * S + sc * P: hj * S + sc * P + P],
                        p.wbf[:, hj * 1024 + half * 512: hj * 1024 + (half + 1) * 512],
                        start=(hj == 0), stop=False), "t")
                k.inc(t.matmul(
                    p.PS[half][:, 0:512],
                    KC["m_causal"][0:1, 0:P],
                    p.rows[0:1, 1024 + half * 512:1024 + (half + 1) * 512],
                    start=False, stop=True), "t")
        p.body("tensor", to_f)
        mm_done = k.ctr["t"]

        def vro(v, sc=sc, mm_done=mm_done):
            k.wv(v, "t", mm_done)
            k.wait(v, "l1f_gb")
            for half in range(2):
                sl = slice(sc * D + half * 512, sc * D + (half + 1) * 512)
                k.inc(v.scalar_tensor_tensor(
                    out=p.xs[:, sl],
                    in0=p.PS[half][:, 0:512],
                    scalar=p.stat[:, 95:96],
                    in1=p.xs[:, sl],
                    op0=ALU.mult, op1=ALU.add), "v")
        p.body("vector", vro)
        k.mark("l1f_radd%d" % sc, "v")
    emit_dbg_dumps(p, "l1f", "d_x4")
    emit_ln_phase(p, "l1f", radd_mark="l1f", dbg_name="d_x4" if p.debug else None)


def build_program(consts, debug=False):
    p = Prog(consts, debug=debug)
    import contextlib

    def scope(name):
        return p.nc.named_scope(name)

    with p.nc.Block() as blk:
        p.blk = blk
        with scope("ph_input"):
            ph_input(p)
            p.mark("l0a_xT_src", "dq")
            p.marks["l0a_xT_src"] = p.marks["input"]
            ph_transpose(p, "l0a_xT")
        with scope("ph_l0_attn"):
            ph_l0_attn(p)
        if debug:
            p.dump("d_x1", lambda t: p.xs[:, t * D:(t + 1) * D], "l0a_ln")
        with scope("ph_l0_ffn"):
            p.marks["l0f_xT_src"] = dict(p.marks["l0a_ln"])
            if debug:
                p.marks["l0f_xT_src"]["dg"] = p.ctr["dg"]
            if debug:
                ph_transpose(p, "l0f_xT")
            else:
                ph_transpose(p, "l0f_xT",
                             tile_marks=lambda t: "l0a_ln%d" % t)
            ph_l0_ffn(p)
        with scope("ph_l1_attn"):
            p.marks["l1a_xT_src"] = dict(p.marks["l0f_ln"])
            p.marks["l1a_xT_src"]["dg"] = p.ctr["dg"]
            if debug:
                ph_transpose(p, "l1a_xT")
            else:
                ph_transpose(p, "l1a_xT",
                             tile_marks=lambda t: "l0f_ln%d" % t)
            ph_l1_attn(p)
        with scope("ph_l1_latmix"):
            ph_l1_latmix(p)
        with scope("ph_l1_ffn"):
            ph_l1_ffn(p)

        def gout(g):
            p.dma_barrier(g, "dg")
            p.wait(g, "l1f_ln")
            for t in range(NSC):
                p.inc(g.dma_start(out=p.out_h[t * P:(t + 1) * P, :],
                                  in_=p.xs[:, t * D:(t + 1) * D]), "o")
            g.wait_ge(p.sem["o"], p.ctr["o"])
        p.body("gpsimd", gout)
    return p


def kernel(**inputs):
    from concourse.bass_utils import run_bass_kernel_spmd
    consts = _host_consts()
    w = _prep_inputs(inputs)
    p = build_program(consts, debug=False)
    x = np.ascontiguousarray(np.asarray(inputs["L0_attn_x"], dtype=np.float32))
    in_maps = []
    for b in range(8):
        m = {"x": np.ascontiguousarray(x[b])}
        m.update(w)
        in_maps.append(m)
    res = run_bass_kernel_spmd(p.nc, in_maps, core_ids=list(range(8)))
    return np.stack([r["out"] for r in res.results]).astype(np.float32)


# version 13
# speedup vs baseline: 1.0149x; 1.0149x over previous
import numpy as np
import ml_dtypes

import concourse.bass as bass
import concourse.mybir as mybir

F32 = mybir.dt.float32
BF16 = mybir.dt.bfloat16
AX = mybir.AxisListType.X
ALU = mybir.AluOpType
ACTF = mybir.ActivationFunctionType

P = 128
S = 2048
D = 1024
NH = 16
HD = 64
NDC = D // P
NSC = S // P
NSB = S // 512
E = 8
H0 = 2048
NHC0 = H0 // P
G = 16
LD = 256
LHN = 4
NLAT = 64
LCH = 32
H1 = 64
EPS = 1e-5
BF = ml_dtypes.bfloat16


def _host_consts():
    c = {}
    inv_freq = 1.0 / (10000.0 ** (np.arange(0, HD, 2, dtype=np.float64) / HD))
    pos = np.arange(S, dtype=np.float64)
    emb = np.repeat(pos[:, None] * inv_freq[None, :], 2, axis=1)
    cos64 = np.cos(emb).T
    sin64 = np.sin(emb).T
    c["rope_cos"] = np.concatenate([cos64, cos64], 0).astype(BF)
    c["rope_sin"] = np.concatenate([sin64, sin64], 0).astype(BF)
    c["eye"] = np.eye(P, dtype=BF)
    rot = np.zeros((P, P), dtype=np.float32)
    for h in range(2):
        for i in range(HD // 2):
            rot[64 * h + 2 * i + 1, 64 * h + 2 * i] = -1.0
            rot[64 * h + 2 * i, 64 * h + 2 * i + 1] = 1.0
    c["rotm"] = rot.astype(BF)
    kk, qq = np.meshgrid(np.arange(P), np.arange(P), indexing="ij")
    c["m_causal"] = (kk <= qq).astype(BF)
    c["m_upper"] = (qq > kk).astype(BF)
    c["m_lowereq"] = (qq <= kk).astype(BF)
    ll, ss = np.meshgrid(np.arange(NLAT), np.arange(S), indexing="ij")
    c["m_lat"] = (ll < (ss // LCH)).astype(BF)
    bo = np.zeros((P, 2), dtype=np.float32)
    bo[0:64, 0] = 1.0
    bo[64:128, 1] = 1.0
    c["blockones"] = bo.astype(BF)
    e8 = np.zeros((E, E * H1), dtype=np.float32)
    for e in range(E):
        e8[e, 64 * e:64 * e + 64] = 1.0
    c["exp8"] = e8.astype(BF)
    pa = np.zeros((P, 4), dtype=np.float32)
    for j in range(4):
        pa[32 * j:32 * j + 32, j] = 1.0 / LCH
    c["poolavg"] = pa.astype(BF)
    sel = np.zeros((E, E * P), dtype=np.float32)
    for e in range(E):
        sel[e, e * P:(e + 1) * P] = 1.0
    c["sel8"] = sel.astype(BF)
    s16 = np.zeros((G, G * HD), dtype=np.float32)
    for r in range(G):
        s16[r, r * HD:(r + 1) * HD] = 1.0
    c["sel16"] = s16.astype(BF)
    c["eye32"] = np.eye(P, dtype=np.float32)
    return c


def _blocks_T(w):
    di, do = w.shape
    return np.ascontiguousarray(
        w.reshape(di // P, P, do // P, P).transpose(2, 1, 0, 3).reshape(
            do // P, P, di)).astype(np.float32)


def _prep_inputs(ins):
    d = {}
    f32 = np.float32
    d["wqT"] = _blocks_T(np.ascontiguousarray(np.asarray(ins["L0_attn_Wq"]).T))
    d["wkT"] = _blocks_T(np.ascontiguousarray(np.asarray(ins["L0_attn_Wk"]).T))
    d["wvT"] = np.ascontiguousarray(np.asarray(ins["L0_attn_Wv"]).T).astype(f32)
    d["woT"] = np.ascontiguousarray(np.asarray(ins["L0_attn_Wo"]).T).astype(f32)
    d["w_red"] = np.ascontiguousarray(ins["L0_ffn_w_red"]).astype(f32)
    d["wg"] = np.ascontiguousarray(ins["L0_ffn_wg"]).astype(f32)
    w1 = np.asarray(ins["L0_ffn_W1"])
    d["w1"] = np.ascontiguousarray(
        w1.reshape(E, NDC, P, NHC0, P).transpose(0, 3, 2, 1, 4).reshape(
            E, NHC0, P, D)).astype(f32)
    w2 = np.asarray(ins["L0_ffn_W2"])
    d["w2"] = np.ascontiguousarray(
        w2.reshape(E, NHC0, P, NDC, P).transpose(0, 3, 2, 1, 4).reshape(
            E, NDC, P, H0)).astype(f32)
    d["b1"] = np.ascontiguousarray(ins["L0_ffn_b1"]).astype(f32)
    d["b2"] = np.ascontiguousarray(ins["L0_ffn_b2"]).astype(f32)
    d["qkv_w"] = _blocks_T(np.asarray(ins["L1_attn_qkv_w"]))
    d["qkv_n"] = np.ascontiguousarray(ins["L1_attn_qkv_w"]).astype(f32)
    d["qkv_b"] = np.ascontiguousarray(ins["L1_attn_qkv_b"]).astype(f32)
    d["rq_w"] = _blocks_T(np.asarray(ins["L1_attn_rq_w"]))
    d["rq_b"] = np.ascontiguousarray(ins["L1_attn_rq_b"]).astype(f32)
    d["rkv_w"] = np.ascontiguousarray(ins["L1_attn_rkv_w"]).astype(f32)
    d["rkv_b"] = np.ascontiguousarray(ins["L1_attn_rkv_b"]).astype(f32)
    d["rout_w"] = _blocks_T(np.asarray(ins["L1_attn_rout_w"]))
    d["rout_wT"] = _blocks_T(np.ascontiguousarray(np.asarray(ins["L1_attn_rout_w"]).T))
    d["rout_b"] = np.ascontiguousarray(ins["L1_attn_rout_b"]).astype(f32)
    d["out_w"] = np.ascontiguousarray(ins["L1_attn_out_w"]).astype(f32)
    d["out_b"] = np.ascontiguousarray(ins["L1_attn_out_b"]).astype(f32)
    d["gl_w"] = np.ascontiguousarray(ins["L1_attn_gl_w"]).astype(f32)
    d["gr_w"] = np.ascontiguousarray(ins["L1_attn_gr_w"]).astype(f32)
    d["ln_g"] = np.ascontiguousarray(ins["L1_ffn_ln_g"]).astype(f32)
    d["ln_b"] = np.ascontiguousarray(ins["L1_ffn_ln_b"]).astype(f32)
    d["bn_g"] = np.ascontiguousarray(ins["L1_ffn_bn_g"]).astype(f32)
    d["bn_b"] = np.ascontiguousarray(ins["L1_ffn_bn_b"]).astype(f32)
    d["bn_mean"] = np.ascontiguousarray(ins["L1_ffn_bn_mean"]).astype(f32)
    d["bn_var"] = np.ascontiguousarray(ins["L1_ffn_bn_var"]).astype(f32)
    d["wr"] = np.ascontiguousarray(ins["L1_ffn_Wr"]).astype(f32)
    d["br"] = np.ascontiguousarray(ins["L1_ffn_br"]).astype(f32)
    d["wd"] = np.ascontiguousarray(
        np.asarray(ins["L1_ffn_W_down"]).transpose(1, 0, 2).reshape(D, E * H1)).astype(f32)
    d["bd"] = np.ascontiguousarray(
        np.asarray(ins["L1_ffn_b_down"]).reshape(E * H1)).astype(f32)
    d["wu"] = np.ascontiguousarray(
        np.asarray(ins["L1_ffn_W_up"]).reshape(E * H1, D)).astype(f32)
    d["bu"] = np.ascontiguousarray(ins["L1_ffn_b_up"]).astype(f32)
    return d


_WSHAPES = {
    "wqT": [NDC, P, D], "wkT": [NDC, P, D],
    "wvT": [D, D], "woT": [D, D],
    "w_red": [D, G], "wg": [E, G],
    "w1": [E, NHC0, P, D], "w2": [E, NDC, P, H0],
    "b1": [E, H0], "b2": [E, D],
    "qkv_w": [24, P, D], "qkv_n": [D, 3 * D], "qkv_b": [3 * D],
    "rq_w": [2, P, D], "rq_b": [LD],
    "rkv_w": [D, 2 * LD], "rkv_b": [2 * LD],
    "rout_w": [NDC, P, 2 * P], "rout_wT": [2, P, D], "rout_b": [D],
    "out_w": [D, D], "out_b": [D],
    "gl_w": [HD, 1], "gr_w": [HD, 1],
    "ln_g": [D], "ln_b": [D], "bn_g": [D], "bn_b": [D],
    "bn_mean": [D], "bn_var": [D],
    "wr": [D, E], "br": [E],
    "wd": [D, E * H1], "bd": [E * H1], "wu": [E * H1, D], "bu": [E, D],
}

_DBG_NAMES = ["d_att0", "d_x1", "d_x2", "d_att1", "d_x3", "d_x4"]


class Prog:
    def __init__(self, consts, debug=False):
        nc = bass.Bass("TRN2", target_bir_lowering=False)
        self.nc = nc
        self.debug = debug
        self.x_h = nc.dram_tensor("x", [S, D], F32, kind="ExternalInput").ap()
        self.out_h = nc.dram_tensor("out", [S, D], F32, kind="ExternalOutput").ap()
        self.W = {n: nc.dram_tensor(n, shp, F32, kind="ExternalInput").ap()
                  for n, shp in _WSHAPES.items()}
        self.C = {n: nc.inline_tensor(np.asarray(v), name="c_" + n).ap()
                  for n, v in consts.items()}
        self.dbg = {}
        if debug:
            for n in _DBG_NAMES:
                self.dbg[n] = nc.dram_tensor(n, [S, D], F32,
                                             kind="ExternalOutput").ap()

        A = nc.alloc_sbuf_tensor
        self.xs = A("xs", [P, NSC * D], F32)
        self.xT = A("xT", [P, NDC * S], BF16)
        self.bufB = A("bufB", [P, NDC * S], BF16)
        self.bufC = A("bufC", [P, 8448], BF16)
        self.bufD = A("bufD", [P, 4096], BF16)
        self.qga = A("qga", [65, S], BF16)
        self.qgb = A("qgb", [65, S], BF16)
        self.kga = A("kga", [65, S], BF16)
        self.kgb = A("kgb", [65, S], BF16)
        self.wst = A("wst", [P, 2048], F32)
        self.wbf = A("wbf", [P, 4096], BF16)
        self.smf = A("smf", [P, 1024], F32)
        self.smb = A("smb", [P, 2048], BF16)
        self.stat = A("stat", [P, 96], F32)
        self.rows = A("rows", [8, S], BF16)
        self.kc = {}
        for n, v in consts.items():
            arr = np.asarray(v)
            if n in ("m_lat", "sel8", "sel16"):
                continue
            self.kc[n] = A("k_" + n, list(arr.shape),
                           F32 if n == "eye32" else BF16)
        self.mlat = self.kc["rope_cos"]
        self.PS = [nc.alloc_psum_tensor("ps%d" % i, [P, 512], F32)
                   for i in range(8)]

        self.sem = {}
        self.ctr = {}
        names = ["t", "v", "s", "g", "dq", "dg", "o"]
        names += ["dq%d" % i for i in range(6)]
        names += ["dg%d" % i for i in range(3)]
        for n in names:
            self.sem[n] = nc.alloc_semaphore("sem_" + n)
            self.ctr[n] = 0
        self._dqrr = 0
        self._dgrr = 0
        self.marks = {}
        self.blk = None

    def inc(self, instr, sem):
        step = 1 if sem in ("t", "v", "s", "g") else 16
        self.ctr[sem] += step
        instr.then_inc(self.sem[sem], step)
        return self.ctr[sem]

    def mark(self, label, sem=None):
        if sem is None:
            self.marks[label] = {n: self.ctr[n] for n in self.ctr}
        else:
            self.marks[label] = {sem: self.ctr[sem]}

    def wait(self, eng, label):
        for sem, val in self.marks[label].items():
            if val > 0:
                eng.wait_ge(self.sem[sem], val)

    def wv(self, eng, sem, val):
        if val > 0:
            eng.wait_ge(self.sem[sem], val)

    def psb(self, i, n=P):
        return self.PS[i][:, :].bitcast(BF16)[:, 0:n]

    def dma_barrier(self, eng, sem):
        if self.ctr[sem] > 0:
            eng.wait_ge(self.sem[sem], self.ctr[sem])

    def body(self, engine, fn):
        getattr(self.blk, engine)(fn)

    def load_cast(self, hbm_ap, n, wst_off, wbf_off, label, wait=None,
                  wait_cast=None, blocks=None, queue="sync", stage=None,
                  split=False):
        k = self
        ch2 = None
        if queue == "sync":
            ch = "dq%d" % (self._dqrr % 6)
            self._dqrr += 1
            if split:
                ch2 = "dq%d" % (self._dqrr % 6)
                self._dqrr += 1
        else:
            ch = "dg%d" % (self._dgrr % 3)
            self._dgrr += 1

        stg = k.wst if stage is None else stage

        def qb(q):
            k.dma_barrier(q, ch)
            if ch2 is not None:
                k.dma_barrier(q, ch2)
            if wait is not None:
                k.wait(q, wait)
            if ch2 is not None and blocks is None:
                hf = n // 2
                k.inc(q.dma_start(out=stg[:, wst_off:wst_off + hf],
                                  in_=hbm_ap[:, 0:hf]), ch)
                k.inc(q.dma_start(out=stg[:, wst_off + hf:wst_off + n],
                                  in_=hbm_ap[:, hf:n]), ch2)
            else:
                dst = stg[:, wst_off:wst_off + n]
                if blocks is not None:
                    dst = dst.rearrange("p (o j) -> p o j", o=blocks)
                k.inc(q.dma_start(out=dst, in_=hbm_ap), ch)
        self.body("sync" if queue == "sync" else "gpsimd", qb)
        self.mark(label + "_dma", ch)
        if ch2 is not None:
            self.mark(label + "_dmb", ch2)

        def vb(v):
            k.wait(v, label + "_dma")
            if ch2 is not None:
                k.wait(v, label + "_dmb")
            if wait_cast is not None:
                k.wait(v, wait_cast)
            k.inc(v.tensor_copy(out=k.wbf[:, wbf_off:wbf_off + n],
                                in_=stg[:, wst_off:wst_off + n]), "v")
        self.body("vector", vb)
        self.mark(label, "v")
        return label

    def dump(self, name, region_fn, wait_label):
        if not self.debug:
            return
        k = self
        dst = self.dbg[name]

        def gb(g):
            k.dma_barrier(g, "dg")
            k.wait(g, wait_label)
            for t in range(NSC):
                k.inc(g.dma_start(out=dst[t * P:(t + 1) * P, :],
                                  in_=region_fn(t)), "dg")
        self.body("gpsimd", gb)



def ph_input(p):
    k = p

    def qb(q):
        for t in range(NSC):
            k.inc(q.dma_start(out=p.xs[:, t * D:(t + 1) * D],
                              in_=p.x_h[t * P:(t + 1) * P, :]), "dq")
        for n, ap in p.C.items():
            if n in ("m_lat", "sel8", "sel16"):
                continue
            k.inc(q.dma_start(out=p.kc[n][:, :], in_=ap), "dq")
    p.body("sync", qb)
    k.mark("input", "dq")

    def vb(v):
        k.wait(v, "input")
        k.inc(v.memset(p.kga[64:65, :], 1.0), "v")
        k.inc(v.memset(p.kgb[64:65, :], 1.0), "v")
    p.body("vector", vb)
    k.mark("aug_ones", "v")


def ph_transpose(p, tag, do_cast=True, tile_marks=None):
    k = p
    tbase = k.ctr["t"]
    vbase = k.ctr["v"]

    def tb(t):
        if tile_marks is None:
            k.wait(t, tag + "_src")
        i = 0
        for st in range(NSC):
            if tile_marks is not None:
                k.wait(t, tile_marks(st))
            for dc in range(NDC):
                if i >= 4:
                    k.wv(t, "v", vbase + i - 3)
                if do_cast:
                    k.inc(t.transpose(
                        p.PS[4 + i % 4][:, 0:P],
                        p.xs[:, st * D + dc * P: st * D + dc * P + P],
                        p.kc["eye32"][:, :]), "t")
                else:
                    k.inc(t.transpose(
                        p.psb(4 + i % 4, P),
                        p.bufB[:, st * D + dc * P: st * D + dc * P + P],
                        p.kc["eye"][:, :]), "t")
                i += 1
    p.body("tensor", tb)

    def vb(v):
        i = 0
        for st in range(NSC):
            for dc in range(NDC):
                k.wv(v, "t", tbase + i + 1)
                srcap = (p.PS[4 + i % 4][:, 0:P] if do_cast
                         else p.psb(4 + i % 4, P))
                k.inc(v.tensor_copy(
                    out=p.xT[:, dc * S + st * P: dc * S + st * P + P],
                    in_=srcap), "v")
                i += 1
    p.body("vector", vb)
    k.mark(tag, "v")


def ph_l0_attn(p):
    k = p
    KC = p.kc

    for grp in range(2):
        for dc in range(NDC):
            src = p.W["wvT"][dc * P:(dc + 1) * P, 512 * grp:512 * grp + 512]
            if dc >= 4:
                gw = "g%d_wv%d" % (grp, dc - 4)
            elif grp == 1:
                gw = "qk_done3"
            else:
                gw = None
            gwc = ("qk_done3" if grp == 1 else None)
            p.load_cast(src, 512, (dc % 4) * 512, dc * 512,
                        "g%d_wv%d" % (grp, dc), wait=gw, wait_cast=gwc)

        def vones(v, grp=grp):
            if grp == 1:
                k.wait(v, "l0a_vaug_used0")
            for st in range(NSC):
                k.inc(v.memset(p.bufC[:, st * 528: st * 528 + 528], 1.0), "v")
        p.body("vector", vones)
        k.mark("g%d_vones" % grp, "v")

        tb = k.ctr["t"]
        vb0 = k.ctr["v"]

        def tmm(t, grp=grp, tb=tb, vb0=vb0):
            k.wait(t, "g%d_wv%d" % (grp, NDC - 1))
            k.wait(t, "g%d_vones" % grp)
            k.wait(t, "l0a_xT")
            for st in range(NSC):
                if st >= 2:
                    k.wv(t, "v", vb0 + st - 1)
                for dc in range(NDC):
                    k.inc(t.matmul(
                        p.PS[st % 2][:, 0:512],
                        p.xT[:, dc * S + st * P: dc * S + st * P + P],
                        p.wbf[:, dc * 512:(dc + 1) * 512],
                        start=(dc == 0), stop=(dc == NDC - 1)), "t")
        p.body("tensor", tmm)

        def vevac(v, grp=grp, tb=tb):
            for st in range(NSC):
                k.wv(v, "t", tb + (st + 1) * NDC)
                dst = p.bufC[:, st * 528: st * 528 + 528]
                dst3 = dst.rearrange("p (h c) -> p h c", c=66)[:, 0:8, 0:64]
                src3 = p.PS[st % 2][:, 0:512].rearrange("p (h c) -> p h c", c=64)
                k.inc(v.tensor_copy(out=dst3, in_=src3), "v")
        p.body("vector", vevac)
        k.mark("g%d_v" % grp, "v")

        for cpi in range(4):
            cp = grp * 4 + cpi
            srcq = p.W["wqT"][cp]
            srck = p.W["wkT"][cp]
            guard = ("qk_done%d" % (cp - 1)) if cpi > 0 else ("g%d_v" % grp)
            wq = p.load_cast(srcq, 1024, 0, 0, "wq%d" % cp,
                             wait=guard, wait_cast=guard)
            wk = p.load_cast(srck, 1024, 1024, 1024, "wk%d" % cp,
                             wait=guard, wait_cast=guard)

            for which, wboff in (("k", 1024), ("q", 0)):
                scale = 0.125 if which == "q" else 1.0
                wlab = wq if which == "q" else wk
                s_base = k.ctr["s"]

                def tproj(t, wboff=wboff, wlab=wlab, grp=grp, which=which,
                          cp=cp, s_base=s_base):
                    k.wait(t, wlab)
                    k.wait(t, "g%d_v" % grp)
                    if "normmm_prev" in k.marks:
                        k.wait(t, "normmm_prev")
                    for sb in range(NSB):
                        if sb >= 2:
                            k.wv(t, "s", s_base + sb - 1)
                        for dc in range(NDC):
                            k.inc(t.matmul(
                                p.PS[4 + (sb % 2)][:, 0:512],
                                p.wbf[:, wboff + dc * P: wboff + dc * P + P],
                                p.xT[:, dc * S + sb * 512: dc * S + (sb + 1) * 512],
                                start=(dc == 0), stop=(dc == NDC - 1)), "t")
                p.body("tensor", tproj)
                tb2 = k.ctr["t"]

                def sevac(s, tb2=tb2, scale=scale, cp=cp):
                    if "normmm_prev" in k.marks:
                        k.wait(s, "normmm_prev")
                    if cp > 0:
                        k.wait(s, "heads%d" % (cp - 1))
                    if "bufD_free" in k.marks:
                        k.wait(s, "bufD_free")
                    for sb in range(NSB):
                        k.wv(s, "t", tb2 - NSB * NDC + (sb + 1) * NDC)
                        k.inc(s.activation(
                            out=p.smb[:, sb * 512:(sb + 1) * 512],
                            in_=p.PS[4 + (sb % 2)][:, 0:512],
                            func=ACTF.Copy, scale=scale), "s")
                p.body("scalar", sevac)
                k.mark("raw_%s%d" % (which, cp), "s")

                tb3 = k.ctr["t"]
                v_base = k.ctr["v"]

                def trot(t, which=which, cp=cp, s_base=s_base, v_base=v_base):
                    for sb in range(NSB):
                        k.wv(t, "s", s_base + sb + 1)
                        if sb >= 2:
                            k.wv(t, "v", v_base + 4 * (sb - 1))
                        k.inc(t.matmul(p.PS[6 + (sb % 2)][:, 0:512],
                                       KC["rotm"][:, :],
                                       p.smb[:, sb * 512:(sb + 1) * 512],
                                       start=True, stop=True), "t")
                p.body("tensor", trot)

                def vrope(v, tb3=tb3, which=which, cp=cp):
                    if "bufD_free" in k.marks:
                        k.wait(v, "bufD_free")
                    for sb in range(NSB):
                        k.wv(v, "t", tb3 + sb + 1)
                        sl = slice(sb * 512, (sb + 1) * 512)
                        sl2 = slice(2048 + sb * 512, 2048 + (sb + 1) * 512)
                        k.inc(v.tensor_tensor(
                            out=p.bufD[:, sl2], in0=p.smb[:, sl],
                            in1=KC["rope_cos"][:, sl], op=ALU.mult), "v")
                        k.inc(v.tensor_tensor(
                            out=p.bufD[:, sl], in0=p.PS[6 + (sb % 2)][:, 0:512],
                            in1=KC["rope_sin"][:, sl], op=ALU.mult), "v")
                        c1 = k.ctr["v"]
                        v.wait_ge(p.sem["v"], c1)
                        k.inc(v.tensor_tensor(
                            out=p.bufD[:, sl], in0=p.bufD[:, sl],
                            in1=p.bufD[:, sl2], op=ALU.add), "v")
                        c2 = k.ctr["v"]
                        v.wait_ge(p.sem["v"], c2)
                        k.inc(v.tensor_tensor(
                            out=p.smb[:, sl], in0=p.bufD[:, sl],
                            in1=p.bufD[:, sl], op=ALU.mult), "v")
                p.body("vector", vrope)
                k.mark("rot_%s%d" % (which, cp), "v")
                k.mark("sq_%s%d" % (which, cp), "v")

                for sb in range(NSB):
                    tb4 = k.ctr["t"]

                    def tns(t, sb=sb, which=which, cp=cp, v_base=v_base):
                        k.wv(t, "v", v_base + 4 * sb + 4)
                        if "ns_evac" in k.marks:
                            k.wait(t, "ns_evac")
                        if cp > 0 and sb == 0 and which == "k":
                            k.wait(t, "hbc%d_3" % (2 * cp - 1))
                        k.inc(t.matmul(p.PS[3][0:2, 0:512],
                                       KC["blockones"][:, :],
                                       p.smb[:, sb * 512:(sb + 1) * 512],
                                       start=True, stop=True), "t")
                    p.body("tensor", tns)

                    def vns(v, tb4=tb4, sb=sb, which=which, cp=cp):
                        k.wv(v, "t", tb4 + 1)
                        if which == "q" and sb == 0:
                            k.wait(v, "kmax%d" % cp)
                        with p.nc.allow_low_precision(
                                reason="norm bound feeds bf16 stabilizer"):
                            k.inc(v.tensor_copy(
                                out=p.rows[0:2, sb * 512:(sb + 1) * 512],
                                in_=p.PS[3][0:2, 0:512]), "v")
                    p.body("vector", vns)
                    k.mark("ns_evac", "v")
                k.mark("norm_%s%d" % (which, cp), "v")
                k.mark("normmm_prev", "t")

                if which == "k":
                    def vkm(v, cp=cp):
                        k.wait(v, "norm_k%d" % cp)
                        k.inc(v.reduce_max(out=p.stat[0:2, 0:1],
                                           in_=p.rows[0:2, :], axis=AX), "v")
                    p.body("vector", vkm)
                    k.mark("kmax%d" % cp, "v")

                def vaug(v, which=which, cp=cp):
                    k.wait(v, "rot_%s%d" % (which, cp))
                    if cp > 0:
                        k.wait(v, "heads%d" % (cp - 1))
                    dst = p.qga if which == "q" else p.kga
                    k.inc(v.tensor_copy(out=dst[0:64, :], in_=p.bufD[0:64, 0:2048]), "v")
                p.body("vector", vaug)
                k.mark("augA_%s%d" % (which, cp), "v")

                def gaug(g, which=which, cp=cp):
                    k.dma_barrier(g, "dg")
                    k.wait(g, "rot_%s%d" % (which, cp))
                    if cp > 0:
                        k.wait(g, "heads%d" % (cp - 1))
                    dst = p.qgb if which == "q" else p.kgb
                    k.inc(g.dma_start(out=dst[0:64, :], in_=p.bufD[64:128, 0:2048]), "dg")
                p.body("gpsimd", gaug)
                k.mark("augB_%s%d" % (which, cp), "dg")
                k.mark("bufD_free")

            def vcrow1(v, cp=cp):
                k.wait(v, "kmax%d" % cp)
                k.wait(v, "norm_q%d" % cp)
                with p.nc.allow_low_precision(reason="stabilizer bound"):
                    k.inc(v.tensor_scalar(
                        out=p.rows[0:2, :], in0=p.rows[0:2, :],
                        scalar1=p.stat[0:2, 0:1], scalar2=None,
                        op0=ALU.mult), "v")
            p.body("vector", vcrow1)
            k.mark("crow1_%d" % cp, "v")

            def scrow(s, cp=cp):
                k.wait(s, "crow1_%d" % cp)
                k.inc(s.activation(out=p.rows[0:2, :], in_=p.rows[0:2, :],
                                   func=ACTF.Sqrt), "s")
            p.body("scalar", scrow)
            k.mark("crow2_%d" % cp, "s")

            def vcrow(v, cp=cp):
                k.wait(v, "crow2_%d" % cp)
                with p.nc.allow_low_precision(reason="stabilizer bound"):
                    k.inc(v.tensor_scalar(
                        out=p.rows[0:2, :], in0=p.rows[0:2, :],
                        scalar1=-1.0, scalar2=-1.0,
                        op0=ALU.mult, op1=ALU.add), "v")
            p.body("vector", vcrow)
            k.mark("crow%d" % cp, "v")

            def gcrow(g, cp=cp):
                k.dma_barrier(g, "dg")
                k.wait(g, "crow%d" % cp)
                k.wait(g, "augA_q%d" % cp)
                k.wait(g, "augB_q%d" % cp)
                k.inc(g.dma_start(out=p.qga[64:65, :], in_=p.rows[0:1, :]), "dg")
                k.inc(g.dma_start(out=p.qgb[64:65, :], in_=p.rows[1:2, :]), "dg")
            p.body("gpsimd", gcrow)
            k.mark("aug%d" % cp, "dg")
            k.mark("qk_done%d" % cp)

            for hh in range(2):
                h = 2 * cp + hh
                qa = p.qga if hh == 0 else p.qgb
                ka = p.kga if hh == 0 else p.kgb
                hslot = h % 8
                tb5 = k.ctr["t"]
                sb5 = k.ctr["s"]
                vb5 = k.ctr["v"]

                st_sched = []
                for kc in range(NSC):
                    for qb in range(kc // 4, NSB):
                        st_sched.append((kc, qb))
                n_t = len(st_sched)
                silast = {}
                for si, (kc, qb) in enumerate(st_sched):
                    silast[qb] = si
                tops = []
                for si in range(n_t):
                    tops.append(("st", si))
                    if si >= 2:
                        av = si - 2
                        tops.append(("av", av))
                        if av == silast[1]:
                            tops.append(("bc", 0))
                        elif av == silast[2]:
                            tops.append(("bc", 1))
                tops.append(("av", n_t - 2))
                tops.append(("av", n_t - 1))
                tops.append(("bc", 2))
                tops.append(("bc", 3))
                tc = {}
                for pos, opx in enumerate(tops):
                    tc[opx] = tb5 + pos + 1
                vcum = [0] * n_t
                vrec = {}
                vbcp = {}
                vc = vb5
                for si, (kc, qb) in enumerate(st_sched):
                    if kc >= 4 * qb:
                        vc += 2 if kc == 4 * qb + 3 else 1
                    vcum[si] = vc
                    if si == silast[0]:
                        vc += 1; vrec[0] = vc
                    elif si == silast[1]:
                        vc += 1; vrec[1] = vc
                        vc += 2; vbcp[0] = vc
                    elif si == silast[2]:
                        vc += 1; vrec[2] = vc
                        vc += 2; vbcp[1] = vc
                vc += 1; vrec[3] = vc
                vc += 2; vbcp[2] = vc
                vc += 2; vbcp[3] = vc

                def thead(t, tb5=tb5, sb5=sb5, qa=qa, ka=ka, hslot=hslot,
                          st_sched=st_sched, tops=tops, tc=tc, vcum=vcum,
                          vrec=vrec, vbcp=vbcp, cp=cp, h=h):
                    k.wait(t, "aug%d" % cp)
                    k.wait(t, "augA_k%d" % cp)
                    k.wait(t, "augB_k%d" % cp)
                    for opx in tops:
                        kind, xi = opx
                        if kind == "st":
                            si = xi
                            kc, qb = st_sched[si]
                            lo = max(0, kc - 4 * qb) * P
                            if si >= 3:
                                k.wv(t, "s", sb5 + si - 2)
                            k.inc(t.matmul(
                                p.PS[si % 3][:, lo:512],
                                ka[0:65, kc * P:(kc + 1) * P],
                                qa[0:65, qb * 512 + lo:(qb + 1) * 512],
                                start=True, stop=True), "t")
                        elif kind == "av":
                            si = xi
                            kc, qb = st_sched[si]
                            avlo = max(0, kc - 4 * qb) * P
                            if avlo >= 384:
                                avlo = 0
                            if kc == 0 and h > 0:
                                k.wait(t, "hbc%d_%d" % (h - 1, qb))
                            k.wv(t, "s", sb5 + si + 1)
                            k.wv(t, "v", vcum[si])
                            k.inc(t.matmul(
                                p.PS[4 + qb][0:65, avlo:512],
                                p.bufC[:, kc * 528 + hslot * 66:
                                       kc * 528 + hslot * 66 + 65],
                                p.smb[:, (si % 3) * 512 + avlo:
                                       (si % 3) * 512 + 512],
                                start=(kc == 0), stop=(kc == 4 * qb + 3)), "t")
                        else:
                            qb = xi
                            k.wv(t, "v", vrec[qb])
                            if qb > 0:
                                k.wv(t, "v", vbcp[qb - 1] - 1)
                            elif h > 0:
                                k.wait(t, "hbc%d_3" % (h - 1))
                            k.inc(t.matmul(
                                p.PS[3][0:64, 0:512],
                                p.kc["m_causal"][0:1, 0:64],
                                p.rows[0:1, qb * 512:(qb + 1) * 512],
                                start=True, stop=True), "t")
                p.body("tensor", thead)

                def sexp(s, sb5=sb5, st_sched=st_sched, tc=tc, vcum=vcum):
                    for si in range(len(st_sched)):
                        kc, qb = st_sched[si]
                        lo = max(0, kc - 4 * qb) * P
                        tw = tc[("st", si)]
                        if si >= 3:
                            tw = max(tw, tc[("av", si - 3)])
                            k.wv(s, "v", vcum[si - 3])
                        k.wv(s, "t", tw)
                        k.inc(s.activation(
                            out=p.smb[:, (si % 3) * 512 + lo:
                                     (si % 3) * 512 + 512],
                            in_=p.PS[si % 3][:, lo:512], func=ACTF.Exp), "s")
                p.body("scalar", sexp)

                scr = p.smf[0:64, 512:768].bitcast(BF16)

                def vnorm(v, qb, tc=tc, silast=silast, cp=cp):
                    k.wv(v, "t", tc[("av", silast[qb])])
                    if qb == 0:
                        k.wait(v, "aug%d" % cp)
                    with p.nc.allow_low_precision(
                            reason="1/l feeds a bf16 matmul operand; rounding"
                                   " happens at the cast either way"):
                        k.inc(v.reciprocal(
                            out=p.rows[0:1, qb * 512:(qb + 1) * 512],
                            in_=p.PS[4 + qb][64:65, 0:512]), "v")

                def vbc(v, qb, tc=tc, hh=hh, h=h, scr=scr):
                    k.wv(v, "t", tc[("bc", qb)])
                    if hh == 1 and qb > 0:
                        k.wait(v, "hodd%d_%d" % (h, qb - 1))
                    c0 = k.ctr["v"]
                    v.wait_ge(p.sem["v"], c0)
                    with p.nc.allow_low_precision(
                            reason="1/l broadcast feeds bf16 scale"):
                        k.inc(v.tensor_copy(out=scr,
                                            in_=p.PS[3][0:64, 0:512]), "v")
                    c1 = k.ctr["v"]
                    v.wait_ge(p.sem["v"], c1)
                    if hh == 0:
                        dst = p.bufB[0:64, (h // 2) * S + qb * 512:
                                     (h // 2) * S + (qb + 1) * 512]
                    else:
                        dst = p.smb[0:64, 1536:2048]
                    k.inc(v.tensor_tensor(
                        out=dst, in0=p.PS[4 + qb][0:64, 0:512],
                        in1=scr, op=ALU.mult), "v")

                def vseg(v, los, his, recq, bcq, sb5=sb5, st_sched=st_sched):
                    for si in range(los, his):
                        kc, qb = st_sched[si]
                        base = (si % 3) * 512
                        npre = kc - 4 * qb
                        if npre < 0:
                            continue
                        k.wv(v, "s", sb5 + si + 1)
                        if npre == 3:
                            k.inc(v.memset(p.smb[:, base:base + npre * P],
                                           0.0), "v")
                        off = base + npre * P
                        k.inc(v.tensor_tensor(
                            out=p.smb[:, off:off + P],
                            in0=p.smb[:, off:off + P],
                            in1=p.kc["m_causal"][:, :], op=ALU.mult), "v")
                    if recq is not None:
                        vnorm(v, recq)
                        assert k.ctr["v"] == vrec[recq], \
                            ("vrec", recq, k.ctr["v"], vrec[recq])
                    if bcq is not None:
                        vbc(v, bcq)
                        assert k.ctr["v"] == vbcp[bcq], \
                            ("vbcp", bcq, k.ctr["v"], vbcp[bcq])

                segs = [(0, silast[0] + 1, 0, None),
                        (silast[0] + 1, silast[1] + 1, 1, 0),
                        (silast[1] + 1, silast[2] + 1, 2, 1),
                        (silast[2] + 1, n_t, 3, 2),
                        (n_t, n_t, None, 3)]
                for (los, his, recq, bcq) in segs:
                    p.body("vector",
                           lambda v, los=los, his=his, recq=recq, bcq=bcq:
                           vseg(v, los, his, recq, bcq))
                    if bcq is not None:
                        k.mark("hbc%d_%d" % (h, bcq), "v")
                        if hh == 1:
                            def gsc(g, qb=bcq, h=h):
                                k.dma_barrier(g, "dg")
                                k.wait(g, "hbc%d_%d" % (h, qb))
                                k.inc(g.dma_start(
                                    out=p.bufB[64:128,
                                               (h // 2) * S + qb * 512:
                                               (h // 2) * S + (qb + 1) * 512],
                                    in_=p.smb[0:64, 1536:2048]), "dg")
                            p.body("gpsimd", gsc)
                            k.mark("hodd%d_%d" % (h, bcq), "dg")
                k.mark("hnorm%d" % h)
            k.mark("heads%d" % cp)
        if grp == 0:
            k.mark("l0a_vaug_used0")
    k.mark("l0a_attnT")

    if p.debug:
        dbg_bufB = p.nc.dram_tensor("d_bufB", [128, 16384], BF16,
                                    kind="ExternalOutput").ap()

        def gdb(g):
            k.dma_barrier(g, "dg")
            k.wait(g, "l0a_attnT")
            k.inc(g.dma_start(out=dbg_bufB, in_=p.bufB[:, :]), "dg")
        p.body("gpsimd", gdb)
        k.mark("l0a_dbgB", "dg")

    for ic in range(NDC):
        src = p.W["woT"][ic * P:(ic + 1) * P, :]
        p.load_cast(src, 1024, (ic % 2) * 1024, 0, "wo_st%d" % ic,
                    wait=("wo_mv%d" % (ic - 2)) if ic >= 2 else "l0a_attnT",
                    wait_cast=("wo_mv%d" % (ic - 1)) if ic >= 1 else "l0a_attnT")
        def vmv(v, ic=ic):
            k.wait(v, "wo_st%d" % ic)
            k.inc(v.tensor_copy(out=p.bufC[:, ic * 1024:(ic + 1) * 1024],
                                in_=p.wbf[:, 0:1024]), "v")
        p.body("vector", vmv)
        k.mark("wo_mv%d" % ic, "v")
    k.mark("wo_res", "v")

    ph_proj_res_ln(p, "l0a", lambda ic: p.bufB[:, ic * S: (ic + 1) * S],
                   lambda ic: p.bufC[:, ic * 1024:(ic + 1) * 1024],
                   "wo_res", "l0a_attnT", dbg_pre="d_att0")



def emit_ln_phase(p, tag, apply_out=None, radd_mark=None, src_mark=None,
                  dbg_name=None):
    k = p

    def vsum(v):
        for sc in range(NSC):
            if radd_mark is not None:
                k.wait(v, radd_mark + "_radd%d" % sc)
            elif sc == 0 and src_mark is not None:
                k.wait(v, src_mark)
            k.inc(v.reduce_sum(out=p.stat[:, 32 + sc:33 + sc],
                               in_=p.xs[:, sc * D:(sc + 1) * D], axis=AX), "v")
    p.body("vector", vsum)
    k.mark(tag + "_sums", "v")

    def ssq(s):
        for sc in range(NSC):
            if radd_mark is not None:
                k.wait(s, radd_mark + "_radd%d" % sc)
            elif sc == 0 and src_mark is not None:
                k.wait(s, src_mark)
            if sc > 0:
                s.wait_ge(p.sem["s"], k.ctr["s"])
            k.inc(s.activation(out=p.smf[:, 0:1024],
                               in_=p.xs[:, sc * D:(sc + 1) * D],
                               func=ACTF.Square, accum_out=p.stat[:, 48 + sc:49 + sc]),
                  "s")
    p.body("scalar", ssq)
    k.mark(tag + "_sqs", "s")

    def vstats(v):
        k.wait(v, tag + "_sqs")
        k.wait(v, tag + "_sums")
        k.inc(v.tensor_scalar(out=p.stat[:, 32:48], in0=p.stat[:, 32:48],
                              scalar1=1.0 / D, scalar2=None, op0=ALU.mult), "v")
        k.inc(v.tensor_scalar(out=p.stat[:, 48:64], in0=p.stat[:, 48:64],
                              scalar1=1.0 / D, scalar2=None, op0=ALU.mult), "v")
        c1 = k.ctr["v"]
        v.wait_ge(p.sem["v"], c1)
        k.inc(v.tensor_tensor(out=p.stat[:, 64:80], in0=p.stat[:, 32:48],
                              in1=p.stat[:, 32:48], op=ALU.mult), "v")
        c2 = k.ctr["v"]
        v.wait_ge(p.sem["v"], c2)
        k.inc(v.tensor_tensor(out=p.stat[:, 48:64], in0=p.stat[:, 48:64],
                              in1=p.stat[:, 64:80], op=ALU.subtract), "v")
        c3 = k.ctr["v"]
        v.wait_ge(p.sem["v"], c3)
        k.inc(v.tensor_scalar(out=p.stat[:, 48:64], in0=p.stat[:, 48:64],
                              scalar1=EPS, scalar2=None, op0=ALU.add), "v")
    p.body("vector", vstats)
    k.mark(tag + "_vst", "v")

    def ssqrt(s):
        k.wait(s, tag + "_vst")
        k.inc(s.activation(out=p.stat[:, 64:80], in_=p.stat[:, 48:64],
                           func=ACTF.Sqrt), "s")
    p.body("scalar", ssqrt)
    k.mark(tag + "_sqrt", "s")

    def vapply(v):
        k.wait(v, tag + "_sqrt")
        k.inc(v.reciprocal(out=p.stat[:, 64:80], in_=p.stat[:, 64:80]), "v")
        c1 = k.ctr["v"]
        v.wait_ge(p.sem["v"], c1)
        for sc in range(NSC):
            if dbg_name is not None and p.debug:
                k.wait(v, tag + "_dbg%d" % sc)
            dst = (p.xs[:, sc * D:(sc + 1) * D] if apply_out is None
                   else apply_out(sc))
            if apply_out is None:
                k.inc(v.tensor_scalar(
                    out=dst, in0=p.xs[:, sc * D:(sc + 1) * D],
                    scalar1=p.stat[:, 32 + sc:33 + sc],
                    scalar2=p.stat[:, 64 + sc:65 + sc],
                    op0=ALU.subtract, op1=ALU.mult), "v")
            else:
                with p.nc.allow_low_precision(reason="bf16 matmul operand"):
                    k.inc(v.tensor_scalar(
                        out=dst, in0=p.xs[:, sc * D:(sc + 1) * D],
                        scalar1=p.stat[:, 32 + sc:33 + sc],
                        scalar2=p.stat[:, 64 + sc:65 + sc],
                        op0=ALU.subtract, op1=ALU.mult), "v")
            k.mark(tag + "_ln%d" % sc, "v")
    p.body("vector", vapply)
    k.mark(tag + "_ln", "v")


def emit_dbg_dumps(p, tag, dbg_name):
    if not p.debug:
        return
    k = p

    for sc in range(NSC):
        def gd(g, sc=sc):
            k.dma_barrier(g, "dg")
            k.wait(g, tag + "_radd%d" % sc)
            k.inc(g.dma_start(out=p.dbg[dbg_name][sc * P:(sc + 1) * P, :],
                              in_=p.xs[:, sc * D:(sc + 1) * D]), "dg")
        p.body("gpsimd", gd)
        k.mark(tag + "_dbg%d" % sc, "dg")


def ph_proj_res_ln(p, tag, lhsT_fn, rhs_fn, w_ready, act_ready, dbg_pre=None,
                   extra_bias_row=None, scale_col=None):
    k = p
    for sc in range(NSC):
        def tmm(t, sc=sc):
            k.wait(t, w_ready)
            k.wait(t, act_ready)
            if sc >= 1:
                k.wait(t, tag + "_radd%d" % (sc - 1))
            for half in range(2):
                last = extra_bias_row is None
                for ic in range(NDC):
                    k.inc(t.matmul(
                        p.PS[half][:, 0:512],
                        lhsT_fn(ic)[:, sc * P:(sc + 1) * P],
                        rhs_fn(ic)[:, half * 512:(half + 1) * 512],
                        start=(ic == 0),
                        stop=(last and ic == NDC - 1)), "t")
                if extra_bias_row is not None:
                    k.inc(t.matmul(
                        p.PS[half][:, 0:512],
                        p.kc["m_causal"][0:1, 0:128],
                        extra_bias_row[0:1, half * 512:(half + 1) * 512],
                        start=False, stop=True), "t")
        p.body("tensor", tmm)
        mm_done = k.ctr["t"]

        def vradd(v, sc=sc, mm_done=mm_done):
            k.wv(v, "t", mm_done)
            for half in range(2):
                sl = slice(sc * D + half * 512, sc * D + (half + 1) * 512)
                if scale_col is None:
                    k.inc(v.tensor_tensor(
                        out=p.xs[:, sl], in0=p.PS[half][:, 0:512],
                        in1=p.xs[:, sl], op=ALU.add), "v")
                else:
                    k.inc(v.scalar_tensor_tensor(
                        out=p.xs[:, sl], in0=p.PS[half][:, 0:512],
                        scalar=scale_col, in1=p.xs[:, sl],
                        op0=ALU.mult, op1=ALU.add), "v")
        p.body("vector", vradd)
        k.mark(tag + "_radd%d" % sc, "v")
    if dbg_pre is not None:
        emit_dbg_dumps(p, tag, dbg_pre)
    emit_ln_phase(p, tag, radd_mark=tag, dbg_name=dbg_pre)


def ph_l0_ffn(p):
    k = p
    KC = p.kc

    def qsel(q):
        k.dma_barrier(q, "dq")
        k.wait(q, "l0a_attnT")
        k.inc(q.dma_start(out=p.kc["rope_sin"][0:E, 0:E * P], in_=p.C["sel8"]),
              "dq")
    p.body("sync", qsel)
    k.mark("sel8", "dq")

    src_wr = p.W["w_red"].rearrange("(c p) g -> p c g", p=P)
    p.load_cast(src_wr, NDC * G, 0, 0, "wred", wait="l0a_ln",
                wait_cast="l0a_ln", blocks=NDC)

    tb = k.ctr["t"]

    def txr(t):
        k.wait(t, "wred")
        k.wait(t, "l0f_xT")
        for sb in range(NSB):
            if sb >= 2:
                k.wv(t, "v", xr_vb + sb - 1)
            for dc in range(NDC):
                k.inc(t.matmul(p.PS[sb % 2][0:G, 0:512],
                               p.wbf[:, dc * G:(dc + 1) * G],
                               p.xT[:, dc * S + sb * 512: dc * S + (sb + 1) * 512],
                               start=(dc == 0), stop=(dc == NDC - 1)), "t")
    xr_vb = k.ctr["v"]
    p.body("tensor", txr)

    def vxr(v, tb=tb):
        for sb in range(NSB):
            k.wv(v, "t", tb + (sb + 1) * NDC)
            k.inc(v.tensor_copy(out=p.bufD[0:G, sb * 512:(sb + 1) * 512],
                                in_=p.PS[sb % 2][0:G, 0:512]), "v")
    p.body("vector", vxr)
    k.mark("xrT", "v")

    def qwg(q):
        k.dma_barrier(q, "dq")
        k.inc(q.dma_start(out=p.stat[0:E, 48:64], in_=p.W["wg"]), "dq")
    p.body("sync", qwg)
    k.mark("wg_dma", "dq")

    def swg(s):
        k.wait(s, "wg_dma")
        k.inc(s.activation(out=p.stat[0:E, 64:80], in_=p.stat[0:E, 48:64],
                           func=ACTF.Square), "s")
    p.body("scalar", swg)
    k.mark("wg_sq", "s")

    def vwg(v):
        k.wait(v, "wg_sq")
        k.inc(v.reduce_sum(out=p.stat[0:E, 80:81], in_=p.stat[0:E, 64:80],
                           axis=AX), "v")
    p.body("vector", vwg)
    k.mark("wg_ss", "v")

    def swg2(s):
        k.wait(s, "wg_ss")
        k.inc(s.activation(out=p.stat[0:E, 81:82], in_=p.stat[0:E, 80:81],
                           func=ACTF.Sqrt), "s")
    p.body("scalar", swg2)
    k.mark("wg_sqrt", "s")

    def vwg2(v):
        k.wait(v, "wg_sqrt")
        k.inc(v.tensor_scalar_max(out=p.stat[0:E, 82:83],
                                  in0=p.stat[0:E, 81:82], scalar1=1e-4), "v")
        c1 = k.ctr["v"]
        v.wait_ge(p.sem["v"], c1)
        k.inc(v.reciprocal(out=p.stat[0:E, 83:84], in_=p.stat[0:E, 82:83]), "v")
        c2 = k.ctr["v"]
        v.wait_ge(p.sem["v"], c2)
        k.inc(v.tensor_scalar(out=p.smb[0:E, 0:G], in0=p.stat[0:E, 48:64],
                              scalar1=p.stat[0:E, 83:84], scalar2=None,
                              op0=ALU.mult), "v")
    p.body("vector", vwg2)
    k.mark("wgn", "v")

    def twgn(t):
        k.wait(t, "wgn")
        k.inc(t.transpose(p.psb(2, G)[0:G, 0:E], p.smb[0:E, 0:G],
                          KC["eye"][0:E, 0:E]), "t")
    p.body("tensor", twgn)
    k.mark("wgnT_mm", "t")

    def vwgn(v):
        k.wait(v, "wgnT_mm")
        k.inc(v.tensor_copy(out=p.smb[0:G, 16:24], in_=p.psb(2, G)[0:G, 0:E]), "v")
    p.body("vector", vwgn)
    k.mark("wgnT", "v")

    for sc in range(NSC):
        tb2 = k.ctr["t"]

        def tlg(t, sc=sc):
            k.wait(t, "wgnT")
            k.wait(t, "xrT")
            if sc >= 1:
                k.wait(t, "gate%d" % (sc - 1))
            k.inc(t.matmul(p.PS[3][:, 0:E],
                           p.bufD[0:G, sc * P:(sc + 1) * P],
                           p.smb[0:G, 16:24], start=True, stop=True), "t")
        p.body("tensor", tlg)
        lg_done = k.ctr["t"]

        def vlg(v, sc=sc, lg_done=lg_done):
            k.wv(v, "t", lg_done)
            k.inc(v.reduce_max(out=p.stat[:, 0:1], in_=p.PS[3][:, 0:E],
                               axis=AX, negate=True), "v")
            k.inc(v.reduce_max(out=p.stat[:, 1:2], in_=p.PS[3][:, 0:E],
                               axis=AX), "v")
        p.body("vector", vlg)
        k.mark("lgm%d" % sc, "v")

        def slg(s, sc=sc):
            k.wait(s, "lgm%d" % sc)
            k.inc(s.activation(out=p.smf[:, 0:E], in_=p.PS[3][:, 0:E],
                               func=ACTF.Exp, bias=p.stat[:, 0:1],
                               accum_out=p.stat[:, 2:3]), "s")
        p.body("scalar", slg)
        k.mark("lge%d" % sc, "s")

        def vlg2(v, sc=sc):
            k.wait(v, "lge%d" % sc)
            k.inc(v.reciprocal(out=p.stat[:, 16 + sc:17 + sc],
                               in_=p.stat[:, 2:3]), "v")
            k.inc(v.tensor_scalar(out=p.smb[:, 128 + E * sc:128 + E * sc + E],
                                  in0=p.PS[3][:, 0:E],
                                  scalar1=p.stat[:, 1:2], scalar2=None,
                                  op0=ALU.is_equal), "v")
        p.body("vector", vlg2)
        k.mark("gate%d" % sc, "v")
    k.mark("gates")

    for sc in range(NSC):
        tb3 = k.ctr["t"]

        def tmt(t, sc=sc):
            k.wait(t, "gates")
            if sc >= 2:
                k.wait(t, "mt%d" % (sc - 2))
            k.inc(t.transpose(p.psb(4 + sc % 2, P)[0:E, 0:P],
                              p.smb[:, 128 + E * sc:128 + E * sc + E],
                              KC["eye"][:, :]), "t")
        p.body("tensor", tmt)
        mt_done = k.ctr["t"]

        def vmt(v, sc=sc, mt_done=mt_done):
            k.wv(v, "t", mt_done)
            k.inc(v.tensor_copy(out=p.bufD[0:E, sc * P:(sc + 1) * P],
                                in_=p.psb(4 + sc % 2, P)[0:E, 0:P]), "v")
        p.body("vector", vmt)
        k.mark("mt%d" % sc, "v")
    k.mark("maskT")

    def qb2(q):
        k.dma_barrier(q, "dq")
        k.wait(q, "wred")
        k.inc(q.dma_start(out=p.wst[0:E, 0:D], in_=p.W["b2"]), "dq")
    p.body("sync", qb2)
    k.mark("b2_dma", "dq")

    def vb2(v):
        k.wait(v, "b2_dma")
        k.inc(v.tensor_copy(out=p.smb[0:E, 1024:2048], in_=p.wst[0:E, 0:D]), "v")
    p.body("vector", vb2)
    k.mark("b2b", "v")

    def vz(v):
        k.wait(v, "l0f_xT")
        if "l0a_dbgB" in k.marks:
            k.wait(v, "l0a_dbgB")
        for i in range(4):
            k.inc(v.memset(p.bufB[:, i * 4096:(i + 1) * 4096], 0.0), "v")
    p.body("vector", vz)
    k.mark("oacc0", "v")

    for e in range(E):
        def qb1(q, e=e):
            k.dma_barrier(q, "dq")
            if e > 0:
                k.wait(q, "eW1_%d_done" % (e - 1))
            with p.nc.allow_non_contiguous_dma(reason="tiny 8KB bias gather"):
                k.inc(q.dma_start(out=p.stat[:, 32:48],
                                  in_=p.W["b1"][e].rearrange("(hc p) -> p hc", p=P)),
                      "dq")
        p.body("sync", qb1)
        k.mark("b1_%d" % e, "dq")

        tmb_b = k.ctr["t"]
        vmb_b = k.ctr["v"]

        def tmb(t, e=e, vmb_b=vmb_b):
            k.wait(t, "maskT")
            if e > 0:
                k.wait(t, "emask_%d_done" % (e - 1))
            for j in range(4):
                if j >= 2:
                    k.wv(t, "v", vmb_b + j - 1)
                if e == 0:
                    k.wait(t, "sel8")
                k.inc(t.matmul(
                    p.PS[6 + j % 2][:, 0:512],
                    KC["rope_sin"][0:E, e * P:(e + 1) * P],
                    p.bufD[0:E, j * 512:(j + 1) * 512],
                    start=True, stop=True), "t")
        p.body("tensor", tmb)

        def vmb(v, e=e, tmb_b=tmb_b):
            for j in range(4):
                k.wv(v, "t", tmb_b + j + 1)
                k.inc(v.tensor_copy(
                    out=p.bufD[:, 2048 + j * 512:2048 + (j + 1) * 512],
                    in_=p.PS[6 + j % 2][:, 0:512]), "v")
        p.body("vector", vmb)
        k.mark("mb_%d" % e, "v")

        for hh in range(2):
            for sbh in range(2):
                W1BANKS = [4, 5, 0, 1]
                s_base = k.ctr["s"]
                tb4s = []
                for hci in range(8):
                    hc = hh * 8 + hci
                    srcw = p.W["w1"][e, hc]
                    lab = "w1_%d_%d_%d_%d" % (e, hh, sbh, hci)
                    slot3 = hci % 3
                    slot = hci % 2
                    stg_off1 = 0 if slot3 == 2 else slot3 * 1024
                    stg1 = p.smf if slot3 == 2 else None
                    gw1 = "wst_slot%d" % slot3
                    if gw1 not in k.marks:
                        gw1 = "gates" if slot3 == 2 else "b2b"
                    p.load_cast(srcw, 1024, stg_off1, slot * 1024,
                                lab, wait=gw1,
                                wait_cast=("w1mm_prev%d" % slot)
                                if ("w1mm_prev%d" % slot) in k.marks else "xrT",
                                stage=stg1, split=True)
                    k.mark("wst_slot%d" % slot3, "v")
                    tb4s.append(k.ctr["t"])

                    def tw1(t, e=e, hh=hh, sbh=sbh, hci=hci, lab=lab,
                            s_base=s_base):
                        k.wait(t, lab)
                        k.wait(t, "l0f_xT")
                        if (e, hh, sbh, hci) == (0, 0, 0, 0):
                            k.wait(t, "maskT")
                            k.wait(t, "gates")
                        for half in range(2):
                            idx = 2 * hci + half
                            if idx >= 4:
                                k.wv(t, "s", s_base + idx - 4 + 1)
                            elif (e, hh, sbh) != (0, 0, 0):
                                pass
                            bank = W1BANKS[idx % 4]
                            soff = sbh * 1024 + half * 512
                            for dc in range(NDC):
                                k.inc(t.matmul(
                                    p.PS[bank][:, 0:512],
                                    p.wbf[:, (hci % 2) * 1024 + dc * P:
                                           (hci % 2) * 1024 + dc * P + P],
                                    p.xT[:, dc * S + soff: dc * S + soff + 512],
                                    start=(dc == 0), stop=(dc == NDC - 1)), "t")
                    p.body("tensor", tw1)
                    k.mark("w1mm_prev%d" % (hci % 2), "t")

                def sglall(s, e=e, hh=hh, sbh=sbh, tb4s=list(tb4s)):
                    k.wait(s, "b1_%d" % e)
                    if "w2round" in k.marks:
                        k.wait(s, "w2round")
                    for hci in range(8):
                        hc = hh * 8 + hci
                        for half in range(2):
                            bank = W1BANKS[(2 * hci + half) % 4]
                            k.wv(s, "t", tb4s[hci] + (half + 1) * NDC)
                            k.inc(s.activation(
                                out=p.bufC[:, hci * 1024 + half * 512:
                                          hci * 1024 + half * 512 + 512],
                                in_=p.PS[bank][:, 0:512],
                                func=ACTF.Gelu, bias=p.stat[:, 32 + hc:33 + hc]),
                                "s")
                p.body("scalar", sglall)
                k.mark("gelus_%d_%d_%d" % (e, hh, sbh), "s")
                gel_base = s_base

                def vgmall(v, e=e, sbh=sbh, gel_base=gel_base):
                    k.wait(v, "mb_%d" % e)
                    for hci in range(8):
                        for half in range(2):
                            k.wv(v, "s", gel_base + 2 * hci + half + 1)
                            off = hci * 1024 + half * 512
                            moff = 2048 + sbh * 1024 + half * 512
                            k.inc(v.tensor_tensor(
                                out=p.bufC[:, off:off + 512],
                                in0=p.bufC[:, off:off + 512],
                                in1=p.bufD[:, moff:moff + 512], op=ALU.mult), "v")
                p.body("vector", vgmall)
                k.mark("gready_%d_%d_%d" % (e, hh, sbh))
                if sbh == 1:
                    k.mark("eW1_%d_done" % e)

                OTBANKS = [6, 7, 2, 3]
                for cc in range(NDC):
                    srcw2 = p.W["w2"][e, cc][:, hh * 1024:(hh + 1) * 1024]
                    lab2 = "w2_%d_%d_%d_%d" % (e, hh, sbh, cc)
                    slot2 = cc % 3
                    wslot = cc % 2
                    stg_off = 0 if slot2 == 2 else slot2 * 1024
                    stg = p.smf if slot2 == 2 else None
                    gw2 = "wst_slot%d" % slot2
                    if gw2 not in k.marks:
                        gw2 = "gates"
                    p.load_cast(srcw2, 1024, stg_off,
                                2048 + wslot * 1024, lab2,
                                wait=gw2,
                                wait_cast=("w2mm_prev%d" % wslot)
                                if ("w2mm_prev%d" % wslot) in k.marks else None,
                                stage=stg, split=True)
                    k.mark("wst_slot%d" % slot2, "v")

                    for half in range(2):
                        bank = OTBANKS[(2 * cc + half) % 4]
                        tb5 = k.ctr["t"]

                        def tw2(t, e=e, hh=hh, sbh=sbh, cc=cc, half=half,
                                bank=bank, lab2=lab2):
                            k.wait(t, lab2)
                            if (e, hh, sbh) == (0, 0, 0):
                                k.wait(t, "gates")
                            k.wait(t, "gready_%d_%d_%d" % (e, hh, sbh))
                            ob = "oadd_%d" % bank
                            if ob in k.marks:
                                k.wait(t, ob)
                            nmm = 8 + (1 if (e == 0 and hh == 0) else 0)
                            for hci in range(8):
                                k.inc(t.matmul(
                                    p.PS[bank][:, 0:512],
                                    p.wbf[:, 2048 + (cc % 2) * 1024 + hci * P:
                                           2048 + (cc % 2) * 1024 + hci * P + P],
                                    p.bufC[:, hci * 1024 + half * 512:
                                           hci * 1024 + half * 512 + 512],
                                    start=(hci == 0),
                                    stop=(hci == 7 and nmm == 8)), "t")
                            if nmm == 9:
                                soff = sbh * 1024 + half * 512
                                k.inc(t.matmul(
                                    p.PS[bank][:, 0:512],
                                    p.smb[0:E, 1024 + cc * P:1024 + cc * P + P],
                                    p.bufD[0:E, soff:soff + 512],
                                    start=False, stop=True), "t")
                        p.body("tensor", tw2)
                        k.mark("w2mm_prev%d" % (cc % 2), "t")
                        w2_done = k.ctr["t"]

                        def voadd(v, e=e, hh=hh, sbh=sbh, cc=cc, half=half,
                                  bank=bank, w2_done=w2_done):
                            k.wv(v, "t", w2_done)
                            if e == 0 and hh == 0:
                                k.wait(v, "b2b")
                                k.wait(v, "oacc0")
                            off = cc * S + sbh * 1024 + half * 512
                            k.inc(v.tensor_tensor(
                                out=p.bufB[:, off:off + 512],
                                in0=p.PS[bank][:, 0:512],
                                in1=p.bufB[:, off:off + 512], op=ALU.add), "v")
                        p.body("vector", voadd)
                        k.mark("oadd_%d" % bank, "v")
                k.mark("w2round", "t")
        k.mark("emask_%d_done" % e)
    k.mark("moe_oT")

    for sc in range(NSC):
        tbt = k.ctr["t"]
        vbt = k.ctr["v"]

        def ttr(t, sc=sc, vbt=vbt):
            k.wait(t, "moe_oT")
            if sc >= 1:
                k.wait(t, "l0f_radd%d" % (sc - 1))
            for cc in range(NDC):
                if cc >= 4:
                    k.wv(t, "v", vbt + cc - 3)
                k.inc(t.transpose(
                    p.psb(cc % 4, P),
                    p.bufB[:, cc * S + sc * P: cc * S + sc * P + P],
                    KC["eye"][:, :]), "t")
        p.body("tensor", ttr)

        def vtr(v, sc=sc, tbt=tbt):
            if sc == 0:
                k.wait(v, "gates")
            for cc in range(NDC):
                k.wv(v, "t", tbt + cc + 1)
                sl = slice(sc * D + cc * P, sc * D + (cc + 1) * P)
                k.inc(v.scalar_tensor_tensor(
                    out=p.xs[:, sl],
                    in0=p.psb(cc % 4, P),
                    scalar=p.stat[:, 16 + sc:17 + sc],
                    in1=p.xs[:, sl],
                    op0=ALU.mult, op1=ALU.add), "v")
        p.body("vector", vtr)
        k.mark("l0f_radd%d" % sc, "v")
    emit_dbg_dumps(p, "l0f", "d_x2")
    emit_ln_phase(p, "l0f", radd_mark="l0f", dbg_name="d_x2" if p.debug else None)


def ph_l1_attn(p):
    k = p
    KC = p.kc

    def qov(q):
        k.dma_barrier(q, "dq")
        k.wait(q, "l0f_ln")
        k.inc(q.dma_start(out=p.kc["rope_cos"][0:NLAT, :], in_=p.C["m_lat"]), "dq")
        k.inc(q.dma_start(out=p.kc["rope_sin"][0:G, P * E:P * E + G * HD],
                          in_=p.C["sel16"]), "dq")
    p.body("sync", qov)
    k.mark("l1consts", "dq")

    for grp in range(2):
        for dc in range(NDC):
            src_ = p.W["qkv_n"][dc * P:(dc + 1) * P,
                                2 * D + 512 * grp:2 * D + 512 * (grp + 1)]
            if dc >= 4:
                gw = "h%d_wv%d" % (grp, dc - 4)
            else:
                gw = "l1qk_done3" if grp == 1 else "l0f_ln"
            p.load_cast(src_, 512, (dc % 4) * 512, dc * 512,
                        "h%d_wv%d" % (grp, dc), wait=gw,
                        wait_cast=("l1qk_done3" if grp == 1 else "l0f_ln"))
        def qvb(q, grp=grp):
            k.dma_barrier(q, "dq")
            k.wait(q, "l0f_ln" if grp == 0 else "l1vb0")
            k.inc(q.dma_start(out=p.smf[0:1, 0:512],
                              in_=p.W["qkv_b"][2 * D + 512 * grp:
                                               2 * D + 512 * (grp + 1)].unsqueeze(0)),
                  "dq")
        p.body("sync", qvb)
        k.mark("l1vb_dma%d" % grp, "dq")

        def vvb(v, grp=grp):
            k.wait(v, "l1vb_dma%d" % grp)
            if grp == 1:
                k.wait(v, "l1heads3")
            k.inc(v.tensor_copy(out=p.smb[0:1, 1536:2048],
                                in_=p.smf[0:1, 0:512]), "v")
        p.body("vector", vvb)
        k.mark("l1vb%d" % grp, "v")

        def vones(v, grp=grp):
            if grp == 1:
                k.wait(v, "l1a_vaug_used0")
            else:
                k.wait(v, "l0f_ln")
            for st in range(NSC):
                k.inc(v.memset(p.bufC[:, st * 528: st * 528 + 528], 1.0), "v")
        p.body("vector", vones)
        k.mark("h%d_vones" % grp, "v")

        tb = k.ctr["t"]
        vb0 = k.ctr["v"]

        def tmm(t, grp=grp, tb=tb, vb0=vb0):
            k.wait(t, "h%d_wv%d" % (grp, NDC - 1))
            k.wait(t, "h%d_vones" % grp)
            k.wait(t, "l1vb%d" % grp)
            k.wait(t, "l1a_xT")
            for st in range(NSC):
                if st >= 2:
                    k.wv(t, "v", vb0 + st - 1)
                for dc in range(NDC):
                    k.inc(t.matmul(
                        p.PS[st % 2][:, 0:512],
                        p.xT[:, dc * S + st * P: dc * S + st * P + P],
                        p.wbf[:, dc * 512:(dc + 1) * 512],
                        start=(dc == 0), stop=False), "t")
                k.inc(t.matmul(
                    p.PS[st % 2][:, 0:512],
                    KC["m_causal"][0:1, 0:P],
                    p.smb[0:1, 1536:2048],
                    start=False, stop=True), "t")
        p.body("tensor", tmm)

        def vevac(v, grp=grp, tb=tb):
            for st in range(NSC):
                k.wv(v, "t", tb + (st + 1) * (NDC + 1))
                dst = p.bufC[:, st * 528: st * 528 + 528]
                dst3 = dst.rearrange("p (h c) -> p h c", c=66)[:, 0:8, 0:64]
                src3 = p.PS[st % 2][:, 0:512].rearrange("p (h c) -> p h c", c=64)
                k.inc(v.tensor_copy(out=dst3, in_=src3), "v")
        p.body("vector", vevac)
        k.mark("h%d_v" % grp, "v")
        if grp == 0:
            k.mark("l1vb_used0")

        for cpi in range(4):
            cp = grp * 4 + cpi
            srcq = p.W["qkv_w"][cp]
            srck = p.W["qkv_w"][8 + cp]
            guard = ("l1qk_done%d" % (cp - 1)) if cpi > 0 else ("h%d_v" % grp)
            wq = p.load_cast(srcq, 1024, 0, 0, "l1wq%d" % cp,
                             wait=guard, wait_cast=guard)
            wk = p.load_cast(srck, 1024, 1024, 1024, "l1wk%d" % cp,
                             wait=guard, wait_cast=guard)
            k.mark("l1_qk_free", "v")
            def qbc(q, cp=cp):
                k.dma_barrier(q, "dq")
                if "l1bc_used" in k.marks:
                    k.wait(q, "l1bc_used")
                k.inc(q.dma_start(out=p.stat[:, 84:85],
                                  in_=p.W["qkv_b"][128 * cp:128 * cp + 128].unsqueeze(1)),
                      "dq")
                k.inc(q.dma_start(out=p.stat[:, 85:86],
                                  in_=p.W["qkv_b"][D + 128 * cp:D + 128 * cp + 128].unsqueeze(1)),
                      "dq")
            p.body("sync", qbc)
            k.mark("l1bc_dma%d" % cp, "dq")

            def vbc0(v, cp=cp):
                k.wait(v, "l1bc_dma%d" % cp)
                k.inc(v.tensor_scalar(out=p.stat[:, 86:87], in0=p.stat[:, 84:85],
                                      scalar1=0.125, scalar2=None,
                                      op0=ALU.mult), "v")
            p.body("vector", vbc0)
            k.mark("l1bc%d" % cp, "v")
            k.mark("l1bc_used", "v")

            for which, wboff in (("k", 1024), ("q", 0)):
                scale = 0.125 if which == "q" else 1.0
                bcol = p.stat[:, 86:87] if which == "q" else p.stat[:, 85:86]
                wlab = ("l1wq%d" % cp) if which == "q" else ("l1wk%d" % cp)
                s_base = k.ctr["s"]

                def tproj(t, wboff=wboff, wlab=wlab, grp=grp, which=which,
                          cp=cp, s_base=s_base):
                    k.wait(t, wlab)
                    k.wait(t, "h%d_v" % grp)
                    k.wait(t, "l1bc%d" % cp)
                    for sb in range(NSB):
                        if sb >= 2:
                            k.wv(t, "s", s_base + sb - 1)
                        for dc in range(NDC):
                            k.inc(t.matmul(
                                p.PS[4 + (sb % 2)][:, 0:512],
                                p.wbf[:, wboff + dc * P: wboff + dc * P + P],
                                p.xT[:, dc * S + sb * 512: dc * S + (sb + 1) * 512],
                                start=(dc == 0), stop=(dc == NDC - 1)), "t")
                p.body("tensor", tproj)
                tb2 = k.ctr["t"]

                def sevac(s, tb2=tb2, scale=scale, cp=cp, bcol=bcol):
                    if cp > 0:
                        k.wait(s, "l1heads%d" % (cp - 1))
                    if "l1bufD_free" in k.marks:
                        k.wait(s, "l1bufD_free")
                    for sb in range(NSB):
                        k.wv(s, "t", tb2 - NSB * NDC + (sb + 1) * NDC)
                        k.inc(s.activation(
                            out=p.bufD[:, sb * 512:(sb + 1) * 512],
                            in_=p.PS[4 + (sb % 2)][:, 0:512],
                            func=ACTF.Identity, scale=scale, bias=bcol), "s")
                p.body("scalar", sevac)
                k.mark("l1raw_%s%d" % (which, cp), "s")

                def vsq(v, which=which, cp=cp):
                    k.wait(v, "l1raw_%s%d" % (which, cp))
                    for sb in range(NSB):
                        sl = slice(sb * 512, (sb + 1) * 512)
                        k.inc(v.tensor_tensor(out=p.smb[:, sl], in0=p.bufD[:, sl],
                                              in1=p.bufD[:, sl], op=ALU.mult), "v")
                p.body("vector", vsq)
                k.mark("l1sq_%s%d" % (which, cp), "v")

                for sb in range(NSB):
                    tb4 = k.ctr["t"]

                    def tns(t, sb=sb, which=which, cp=cp):
                        k.wait(t, "l1sq_%s%d" % (which, cp))
                        if "l1ns_evac" in k.marks:
                            k.wait(t, "l1ns_evac")
                        if cp > 0 and sb == 0 and which == "k":
                            k.wait(t, "l1hbc%d_3" % (2 * cp - 1))
                        k.inc(t.matmul(p.PS[3][0:2, 0:512],
                                       KC["blockones"][:, :],
                                       p.smb[:, sb * 512:(sb + 1) * 512],
                                       start=True, stop=True), "t")
                    p.body("tensor", tns)

                    def vns(v, tb4=tb4, sb=sb, which=which, cp=cp):
                        k.wv(v, "t", tb4 + 1)
                        if which == "q" and sb == 0:
                            k.wait(v, "l1kmax%d" % cp)
                        with p.nc.allow_low_precision(
                                reason="norm bound feeds bf16 stabilizer"):
                            k.inc(v.tensor_copy(
                                out=p.rows[0:2, sb * 512:(sb + 1) * 512],
                                in_=p.PS[3][0:2, 0:512]), "v")
                    p.body("vector", vns)
                    k.mark("l1ns_evac", "v")
                k.mark("l1norm_%s%d" % (which, cp), "v")
                k.mark("l1normmm_prev", "t")

                if which == "k":
                    def vkm(v, cp=cp):
                        k.wait(v, "l1norm_k%d" % cp)
                        k.inc(v.reduce_max(out=p.stat[0:2, 0:1],
                                           in_=p.rows[0:2, :], axis=AX), "v")
                    p.body("vector", vkm)
                    k.mark("l1kmax%d" % cp, "v")

                def vaug(v, which=which, cp=cp):
                    k.wait(v, "l1sq_%s%d" % (which, cp))
                    if cp > 0:
                        k.wait(v, "l1heads%d" % (cp - 1))
                    dst = p.qga if which == "q" else p.kga
                    k.inc(v.tensor_copy(out=dst[0:64, :], in_=p.bufD[0:64, 0:2048]), "v")
                p.body("vector", vaug)
                k.mark("l1augA_%s%d" % (which, cp), "v")

                def gaug(g, which=which, cp=cp):
                    k.dma_barrier(g, "dg")
                    k.wait(g, "l1sq_%s%d" % (which, cp))
                    if cp > 0:
                        k.wait(g, "l1heads%d" % (cp - 1))
                    dst = p.qgb if which == "q" else p.kgb
                    k.inc(g.dma_start(out=dst[0:64, :], in_=p.bufD[64:128, 0:2048]), "dg")
                p.body("gpsimd", gaug)
                k.mark("l1augB_%s%d" % (which, cp), "dg")
                k.mark("l1bufD_free")

            def vcrow1(v, cp=cp):
                k.wait(v, "l1kmax%d" % cp)
                k.wait(v, "l1norm_q%d" % cp)
                with p.nc.allow_low_precision(reason="stabilizer bound"):
                    k.inc(v.tensor_scalar(
                        out=p.rows[0:2, :], in0=p.rows[0:2, :],
                        scalar1=p.stat[0:2, 0:1], scalar2=None,
                        op0=ALU.mult), "v")
            p.body("vector", vcrow1)
            k.mark("l1crow1_%d" % cp, "v")

            def scrow(s, cp=cp):
                k.wait(s, "l1crow1_%d" % cp)
                k.inc(s.activation(out=p.rows[0:2, :], in_=p.rows[0:2, :],
                                   func=ACTF.Sqrt), "s")
            p.body("scalar", scrow)
            k.mark("l1crow2_%d" % cp, "s")

            def vcrow(v, cp=cp):
                k.wait(v, "l1crow2_%d" % cp)
                with p.nc.allow_low_precision(reason="stabilizer bound"):
                    k.inc(v.tensor_scalar(
                        out=p.rows[0:2, :], in0=p.rows[0:2, :],
                        scalar1=-1.0, scalar2=-1.0,
                        op0=ALU.mult, op1=ALU.add), "v")
            p.body("vector", vcrow)
            k.mark("l1crow%d" % cp, "v")

            def gcrow(g, cp=cp):
                k.dma_barrier(g, "dg")
                k.wait(g, "l1crow%d" % cp)
                k.wait(g, "l1augA_q%d" % cp)
                k.wait(g, "l1augB_q%d" % cp)
                k.inc(g.dma_start(out=p.qga[64:65, :], in_=p.rows[0:1, :]), "dg")
                k.inc(g.dma_start(out=p.qgb[64:65, :], in_=p.rows[1:2, :]), "dg")
            p.body("gpsimd", gcrow)
            k.mark("l1aug%d" % cp, "dg")
            k.mark("l1qk_done%d" % cp)

            for hh in range(2):
                h = 2 * cp + hh
                qa = p.qga if hh == 0 else p.qgb
                ka = p.kga if hh == 0 else p.kgb
                hslot = h % 8
                tb5 = k.ctr["t"]
                sb5 = k.ctr["s"]
                vb5 = k.ctr["v"]

                st_sched = []
                for kc in range(NSC):
                    for qb in range(NSB):
                        if 4 * qb - 1 <= kc <= 4 * qb + 3:
                            st_sched.append((kc, qb))
                n_t = len(st_sched)
                silast = {}
                for si, (kc, qb) in enumerate(st_sched):
                    silast[qb] = si

                def live_qcs(kc, qb):
                    out = []
                    for qc in (kc, kc + 1):
                        if 4 * qb <= qc < 4 * qb + 4 and qc < NSC:
                            out.append(qc)
                    return out

                tops = []
                for si in range(n_t):
                    tops.append(("st", si))
                    if si >= 2:
                        av = si - 2
                        tops.append(("av", av))
                        if av == silast[1]:
                            tops.append(("bc", 0))
                        elif av == silast[2]:
                            tops.append(("bc", 1))
                tops.append(("av", n_t - 2))
                tops.append(("av", n_t - 1))
                tops.append(("bc", 2))
                tops.append(("bc", 3))
                tc = {}
                tcnt = tb5
                for opx in tops:
                    kind, xi = opx
                    if kind == "av":
                        tcnt += len(live_qcs(*st_sched[xi]))
                    else:
                        tcnt += 1
                    tc[opx] = tcnt
                vcum = [0] * n_t
                vrec = {}
                vbcp = {}
                vc = vb5
                for si, (kc, qb) in enumerate(st_sched):
                    vc += len(live_qcs(kc, qb))
                    vcum[si] = vc
                    if si == silast[0]:
                        vc += 2; vrec[0] = vc
                    elif si == silast[1]:
                        vc += 2; vrec[1] = vc
                        vc += 2; vbcp[0] = vc
                    elif si == silast[2]:
                        vc += 2; vrec[2] = vc
                        vc += 2; vbcp[1] = vc
                vc += 2; vrec[3] = vc
                vc += 2; vbcp[2] = vc
                vc += 2; vbcp[3] = vc

                def thead(t, tb5=tb5, sb5=sb5, qa=qa, ka=ka, hslot=hslot,
                          st_sched=st_sched, tops=tops, tc=tc, vcum=vcum,
                          vrec=vrec, vbcp=vbcp, cp=cp, h=h,
                          live_qcs=live_qcs):
                    k.wait(t, "l1aug%d" % cp)
                    k.wait(t, "l1augA_k%d" % cp)
                    k.wait(t, "l1augB_k%d" % cp)
                    for opx in tops:
                        kind, xi = opx
                        if kind == "st":
                            si = xi
                            kc, qb = st_sched[si]
                            qcs = live_qcs(kc, qb)
                            lo = (qcs[0] - 4 * qb) * P
                            hi = (qcs[-1] - 4 * qb + 1) * P
                            if si >= 3:
                                k.wv(t, "s", sb5 + si - 2)
                            k.inc(t.matmul(
                                p.PS[si % 3][:, lo:hi],
                                ka[0:65, kc * P:(kc + 1) * P],
                                qa[0:65, qb * 512 + lo:qb * 512 + hi],
                                start=True, stop=True), "t")
                        elif kind == "av":
                            si = xi
                            kc, qb = st_sched[si]
                            first_kc = max(0, 4 * qb - 1)
                            if kc == first_kc and h > 0:
                                k.wait(t, "l1hbc%d_%d" % (h - 1, qb))
                            k.wv(t, "s", sb5 + si + 1)
                            k.wv(t, "v", vcum[si])
                            for qc in live_qcs(kc, qb):
                                colo = (qc - 4 * qb) * P
                                stf = (qc == kc + 1) or (kc == 0 and qc == 0)
                                spf = (qc == kc)
                                k.inc(t.matmul(
                                    p.PS[4 + qb][0:65, colo:colo + P],
                                    p.bufC[:, kc * 528 + hslot * 66:
                                           kc * 528 + hslot * 66 + 65],
                                    p.smb[:, (si % 3) * 512 + colo:
                                           (si % 3) * 512 + colo + P],
                                    start=stf, stop=spf), "t")
                        else:
                            qb = xi
                            k.wv(t, "v", vrec[qb])
                            if qb > 0:
                                k.wv(t, "v", vbcp[qb - 1] - 1)
                            elif h > 0:
                                k.wait(t, "l1hbc%d_3" % (h - 1))
                            k.inc(t.matmul(
                                p.PS[3][0:64, 0:512],
                                p.kc["m_causal"][0:1, 0:64],
                                p.rows[0:1, qb * 512:(qb + 1) * 512],
                                start=True, stop=True), "t")
                p.body("tensor", thead)

                def sexp(s, sb5=sb5, st_sched=st_sched, tc=tc, vcum=vcum,
                         live_qcs=live_qcs):
                    for si in range(len(st_sched)):
                        kc, qb = st_sched[si]
                        qcs = live_qcs(kc, qb)
                        lo = (qcs[0] - 4 * qb) * P
                        hi = (qcs[-1] - 4 * qb + 1) * P
                        tw = tc[("st", si)]
                        if si >= 3:
                            tw = max(tw, tc[("av", si - 3)])
                            k.wv(s, "v", vcum[si - 3])
                        k.wv(s, "t", tw)
                        k.inc(s.activation(
                            out=p.smb[:, (si % 3) * 512 + lo:
                                     (si % 3) * 512 + hi],
                            in_=p.PS[si % 3][:, lo:hi], func=ACTF.Exp), "s")
                p.body("scalar", sexp)

                scr = p.smf[0:64, 512:768].bitcast(BF16)

                def vnorm(v, qb, tc=tc, silast=silast, cp=cp):
                    k.wv(v, "t", tc[("av", silast[qb])])
                    if qb == 0:
                        k.wait(v, "l1aug%d" % cp)
                    with p.nc.allow_low_precision(
                            reason="1/l feeds a bf16 matmul operand"):
                        k.inc(v.tensor_scalar(
                            out=p.rows[0:1, qb * 512:(qb + 1) * 512],
                            in0=p.PS[4 + qb][64:65, 0:512],
                            scalar1=1e-30, scalar2=None, op0=ALU.add), "v")
                        c1 = k.ctr["v"]
                        v.wait_ge(p.sem["v"], c1)
                        k.inc(v.reciprocal(
                            out=p.rows[0:1, qb * 512:(qb + 1) * 512],
                            in_=p.rows[0:1, qb * 512:(qb + 1) * 512]), "v")

                def vbc(v, qb, tc=tc, hh=hh, h=h, scr=scr):
                    k.wv(v, "t", tc[("bc", qb)])
                    if hh == 1 and qb > 0:
                        k.wait(v, "l1hodd%d_%d" % (h, qb - 1))
                    c0 = k.ctr["v"]
                    v.wait_ge(p.sem["v"], c0)
                    with p.nc.allow_low_precision(
                            reason="1/l broadcast feeds bf16 scale"):
                        k.inc(v.tensor_copy(out=scr,
                                            in_=p.PS[3][0:64, 0:512]), "v")
                    c1 = k.ctr["v"]
                    v.wait_ge(p.sem["v"], c1)
                    if hh == 0:
                        dst = p.bufB[0:64, (h // 2) * S + qb * 512:
                                     (h // 2) * S + (qb + 1) * 512]
                    else:
                        dst = p.smb[0:64, 1536:2048]
                    k.inc(v.tensor_tensor(
                        out=dst, in0=p.PS[4 + qb][0:64, 0:512],
                        in1=scr, op=ALU.mult), "v")

                def vseg(v, los, his, recq, bcq, sb5=sb5, st_sched=st_sched,
                         live_qcs=live_qcs):
                    for si in range(los, his):
                        kc, qb = st_sched[si]
                        base = (si % 3) * 512
                        qcs = live_qcs(kc, qb)
                        if qcs:
                            k.wv(v, "s", sb5 + si + 1)
                        for qc in qcs:
                            rel = qc - kc
                            off = base + (qc - 4 * qb) * P
                            mt = p.kc["m_upper"] if rel == 0 else \
                                p.kc["m_lowereq"]
                            k.inc(v.tensor_tensor(
                                out=p.smb[:, off:off + P],
                                in0=p.smb[:, off:off + P],
                                in1=mt[:, :], op=ALU.mult), "v")
                    if recq is not None:
                        vnorm(v, recq)
                        assert k.ctr["v"] == vrec[recq], \
                            ("l1vrec", recq, k.ctr["v"], vrec[recq])
                    if bcq is not None:
                        vbc(v, bcq)
                        assert k.ctr["v"] == vbcp[bcq], \
                            ("l1vbcp", bcq, k.ctr["v"], vbcp[bcq])

                segs = [(0, silast[0] + 1, 0, None),
                        (silast[0] + 1, silast[1] + 1, 1, 0),
                        (silast[1] + 1, silast[2] + 1, 2, 1),
                        (silast[2] + 1, n_t, 3, 2),
                        (n_t, n_t, None, 3)]
                for (los, his, recq, bcq) in segs:
                    p.body("vector",
                           lambda v, los=los, his=his, recq=recq, bcq=bcq:
                           vseg(v, los, his, recq, bcq))
                    if bcq is not None:
                        k.mark("l1hbc%d_%d" % (h, bcq), "v")
                        if hh == 1:
                            def gsc(g, qb=bcq, h=h):
                                k.dma_barrier(g, "dg")
                                k.wait(g, "l1hbc%d_%d" % (h, qb))
                                k.inc(g.dma_start(
                                    out=p.bufB[64:128,
                                               (h // 2) * S + qb * 512:
                                               (h // 2) * S + (qb + 1) * 512],
                                    in_=p.smb[0:64, 1536:2048]), "dg")
                            p.body("gpsimd", gsc)
                            k.mark("l1hodd%d_%d" % (h, bcq), "dg")
                k.mark("l1hnorm%d" % h)
            k.mark("l1heads%d" % cp)
        if grp == 0:
            k.mark("l1a_vaug_used0")
    k.mark("l1a_localT")




def ph_l1_latmix(p):
    k = p
    KC = p.kc
    MLAT = p.kc["rope_cos"]

    tpl = k.ctr["t"]
    for st in range(NSC):
        def vc(v, st=st):
            if st == 0:
                k.wait(v, "l1a_localT")
            if st >= 2:
                k.wait(v, "poolmm%d" % (st - 2))
            k.inc(v.tensor_copy(
                out=p.smb[:, (st % 2) * 1024:(st % 2) * 1024 + 1024],
                in_=p.xs[:, st * D:(st + 1) * D]), "v")
        p.body("vector", vc)
        k.mark("poolc%d" % st, "v")

        def tp(t, st=st):
            k.wait(t, "poolc%d" % st)
            if st == 0:
                k.wait(t, "l1a_localT")
            for dc in range(NDC):
                k.inc(t.matmul(
                    p.PS[6][:, 0:512][:, dc * 64 + 4 * st: dc * 64 + 4 * st + 4],
                    p.smb[:, (st % 2) * 1024 + dc * P:(st % 2) * 1024 + dc * P + P],
                    KC["poolavg"][:, :],
                    start=True, stop=True), "t")
        p.body("tensor", tp)
        k.mark("poolmm%d" % st, "t")
    k.mark("pool_mm", "t")

    def vpool(v):
        k.wait(v, "pool_mm")
        k.inc(v.tensor_copy(out=p.bufD[:, 2048:2560], in_=p.PS[6][:, 0:512]), "v")
    p.body("vector", vpool)
    k.mark("poolT", "v")

    def qrb(q):
        k.dma_barrier(q, "dq")
        k.wait(q, "l1a_localT")
        with p.nc.allow_non_contiguous_dma(reason="tiny bias gathers"):
            k.inc(q.dma_start(out=p.stat[:, 88:92],
                              in_=p.W["rkv_b"].rearrange("(c p) -> p c", p=P)), "dq")
            k.inc(q.dma_start(out=p.stat[:, 92:94],
                              in_=p.W["rq_b"].rearrange("(c p) -> p c", p=P)), "dq")
        k.inc(q.dma_start(out=p.smf[0:1, 0:512], in_=p.W["rkv_b"].unsqueeze(0)),
              "dq")
    p.body("sync", qrb)
    k.mark("l1rb_dma", "dq")

    def vrb(v):
        k.wait(v, "l1rb_dma")
        k.inc(v.tensor_copy(out=p.rows[0:1, 0:512], in_=p.smf[0:1, 0:512]), "v")
        k.inc(v.tensor_scalar(out=p.stat[:, 94:96], in0=p.stat[:, 92:94],
                              scalar1=0.125, scalar2=None, op0=ALU.mult), "v")
    p.body("vector", vrb)
    k.mark("l1rb", "v")

    for fc in range(2):
        src_ = p.W["rkv_w"].rearrange("(c p) f -> p c f", p=P)[:, :, fc * P:(fc + 1) * P]
        p.load_cast(src_, 1024, 0, 0, "rkw%d" % fc,
                    wait=("rkmm%d" % (fc - 1)) if fc else "l1a_localT",
                    wait_cast=("rkmm%d" % (fc - 1)) if fc else "l1heads7",
                    blocks=NDC)
        tb = k.ctr["t"]

        def trk(t, fc=fc):
            k.wait(t, "rkw%d" % fc)
            k.wait(t, "poolT")
            if fc > 0:
                k.wait(t, "rkT%d" % (fc - 1))
            for dc in range(NDC):
                k.inc(t.matmul(p.PS[0][:, 0:64],
                               p.wbf[:, dc * P:(dc + 1) * P],
                               p.bufD[:, 2048 + dc * 64:2048 + dc * 64 + 64],
                               start=(dc == 0), stop=(dc == NDC - 1)), "t")
        p.body("tensor", trk)
        k.mark("rkmm%d" % fc, "t")

        def srk(s, fc=fc):
            k.wait(s, "rkmm%d" % fc)
            k.wait(s, "l1rb")
            k.inc(s.activation(out=p.bufD[:, 2560 + fc * 64:2560 + fc * 64 + 64],
                               in_=p.PS[0][:, 0:64], func=ACTF.Identity,
                               bias=p.stat[:, 88 + fc:89 + fc]), "s")
        p.body("scalar", srk)
        k.mark("rkT%d" % fc, "s")
    k.mark("rkT", "s")

    src_ = p.W["rkv_w"].rearrange("(c p) f -> p c f", p=P)[:, :, 2 * P:4 * P]
    p.load_cast(src_, 2048, 0, 2048, "rvw", wait="rkmm1", wait_cast="rkmm1",
                blocks=NDC)
    tbv = k.ctr["t"]

    def trv(t):
        k.wait(t, "rvw")
        k.wait(t, "rkT")
        k.wait(t, "l1rb")
        for dc in range(NDC):
            k.inc(t.matmul(p.PS[1][0:64, 0:256],
                           p.bufD[:, 2048 + dc * 64:2048 + dc * 64 + 64],
                           p.wbf[:, 2048 + dc * 256:2048 + (dc + 1) * 256],
                           start=(dc == 0), stop=False), "t")
        k.inc(t.matmul(p.PS[1][0:64, 0:256],
                       KC["m_causal"][0:1, 0:64],
                       p.rows[0:1, 256:512],
                       start=False, stop=True), "t")
    p.body("tensor", trv)
    k.mark("rvmm", "t")

    def vrv(v):
        k.wait(v, "rvmm")
        k.inc(v.tensor_copy(out=p.bufD[0:64, 2688:2944], in_=p.PS[1][0:64, 0:256]), "v")
    p.body("vector", vrv)
    k.mark("rvtm", "v")

    def vrva(v):
        k.wait(v, "rvtm")
        k.wait(v, "l1a_localT")
        k.inc(v.memset(p.bufC[0:64, 4096:4360], 1.0), "v")
        c1 = k.ctr["v"]
        v.wait_ge(p.sem["v"], c1)
        dst = p.bufC[0:64, 4096:4360].rearrange("p (l c) -> p l c", c=66)[:, 0:4, 0:64]
        src3 = p.bufD[0:64, 2688:2944].rearrange("p (l c) -> p l c", c=64)
        k.inc(v.tensor_copy(out=dst, in_=src3), "v")
    p.body("vector", vrva)
    k.mark("rvaug", "v")

    for fc in range(2):
        src_ = p.W["rq_w"][fc]
        p.load_cast(src_, 1024, 0, 0, "rqw%d" % fc,
                    wait="rvw" if fc == 0 else "lheads%d" % (fc - 1),
                    wait_cast="rvw" if fc == 0 else "lheads%d" % (fc - 1))
        tbq = k.ctr["t"]
        sq_base = k.ctr["s"]

        def trq(t, fc=fc, sq_base=sq_base):
            k.wait(t, "rqw%d" % fc)
            k.wait(t, "l1a_localT")
            if fc > 0:
                k.wait(t, "lheads%d" % (fc - 1))
            for sb in range(NSB):
                if sb >= 1:
                    k.wv(t, "s", sq_base + sb)
                for dc in range(NDC):
                    k.inc(t.matmul(
                        p.PS[2][:, 0:512],
                        p.wbf[:, dc * P:(dc + 1) * P],
                        p.xT[:, dc * S + sb * 512: dc * S + (sb + 1) * 512],
                        start=(dc == 0), stop=(dc == NDC - 1)), "t")
        p.body("tensor", trq)

        def srq(s, fc=fc, tbq=tbq):
            k.wait(s, "l1rb")
            k.wait(s, "l1a_localT" if fc == 0 else "lheads%d" % (fc - 1))
            for sb in range(NSB):
                k.wv(s, "t", tbq + (sb + 1) * NDC)
                k.inc(s.activation(
                    out=p.bufD[:, sb * 512:(sb + 1) * 512],
                    in_=p.PS[2][:, 0:512], func=ACTF.Identity, scale=0.125,
                    bias=p.stat[:, 94 + fc:95 + fc]), "s")
        p.body("scalar", srq)
        k.mark("rqT%d" % fc, "s")

        def vsq(v, fc=fc):
            k.wait(v, "rqT%d" % fc)
            for sb in range(NSB):
                sl = slice(sb * 512, (sb + 1) * 512)
                k.inc(v.tensor_tensor(out=p.smb[:, sl], in0=p.bufD[:, sl],
                                      in1=p.bufD[:, sl], op=ALU.mult), "v")
        p.body("vector", vsq)
        k.mark("lsq%d" % fc, "v")

        def vsqk(v, fc=fc):
            k.wait(v, "rkT")
            if fc > 0:
                k.wait(v, "lheads%d" % (fc - 1))
            k.inc(v.tensor_tensor(
                out=p.bufD[:, 2944:3008],
                in0=p.bufD[:, 2560 + fc * 64:2560 + fc * 64 + 64],
                in1=p.bufD[:, 2560 + fc * 64:2560 + fc * 64 + 64],
                op=ALU.mult), "v")
        p.body("vector", vsqk)
        k.mark("lsqk%d" % fc, "v")

        tkn = k.ctr["t"]

        def tkn_f(t, fc=fc):
            k.wait(t, "lsqk%d" % fc)
            if "lkn_ev" in k.marks:
                k.wait(t, "lkn_ev")
            k.inc(t.matmul(p.PS[3][0:2, 0:64], KC["blockones"][:, :],
                           p.bufD[:, 2944:3008], start=True, stop=True), "t")
        p.body("tensor", tkn_f)

        def vknev(v, fc=fc, tkn=tkn):
            k.wv(v, "t", tkn + 1)
            with p.nc.allow_low_precision(reason="norm bound feeds stabilizer"):
                k.inc(v.tensor_copy(out=p.rows[0:2, 0:64],
                                    in_=p.PS[3][0:2, 0:64]), "v")
        p.body("vector", vknev)
        k.mark("lkn_ev", "v")
        k.mark("lkn%d" % fc, "v")

        def vkm(v, fc=fc):
            k.wait(v, "lkn%d" % fc)
            k.inc(v.reduce_max(out=p.stat[0:2, 0:1], in_=p.rows[0:2, 0:64],
                               axis=AX), "v")
        p.body("vector", vkm)
        k.mark("lkmax%d" % fc, "v")

        for sb in range(NSB):
            tqn = k.ctr["t"]

            def tqn_f(t, fc=fc, sb=sb):
                k.wait(t, "lsq%d" % fc)
                k.wait(t, "lkn_ev")
                if "lqn_ev" in k.marks:
                    k.wait(t, "lqn_ev")
                k.inc(t.matmul(p.PS[3][0:2, 0:512], KC["blockones"][:, :],
                               p.smb[:, sb * 512:(sb + 1) * 512],
                               start=True, stop=True), "t")
            p.body("tensor", tqn_f)

            def vqn(v, fc=fc, sb=sb, tqn=tqn):
                k.wv(v, "t", tqn + 1)
                if sb == 0:
                    k.wait(v, "lkmax%d" % fc)
                with p.nc.allow_low_precision(
                        reason="norm bound feeds stabilizer"):
                    k.inc(v.tensor_copy(
                        out=p.rows[0:2, sb * 512:(sb + 1) * 512],
                        in_=p.PS[3][0:2, 0:512]), "v")
            p.body("vector", vqn)
            k.mark("lqn_ev", "v")
        k.mark("lqn%d" % fc, "v")

        def vcrow1(v, fc=fc):
            k.wait(v, "lkmax%d" % fc)
            k.wait(v, "lqn%d" % fc)
            with p.nc.allow_low_precision(reason="stabilizer bound"):
                k.inc(v.tensor_scalar(
                    out=p.rows[0:2, :], in0=p.rows[0:2, :],
                    scalar1=p.stat[0:2, 0:1], scalar2=None,
                    op0=ALU.mult), "v")
        p.body("vector", vcrow1)
        k.mark("lcrow1_%d" % fc, "v")

        def scrowl(s, fc=fc):
            k.wait(s, "lcrow1_%d" % fc)
            k.inc(s.activation(out=p.rows[0:2, :], in_=p.rows[0:2, :],
                               func=ACTF.Sqrt), "s")
        p.body("scalar", scrowl)
        k.mark("lcrow2_%d" % fc, "s")

        def vcrow(v, fc=fc):
            k.wait(v, "lcrow2_%d" % fc)
            with p.nc.allow_low_precision(reason="stabilizer bound"):
                k.inc(v.tensor_scalar(
                    out=p.rows[0:2, :], in0=p.rows[0:2, :],
                    scalar1=-1.0, scalar2=-1.0,
                    op0=ALU.mult, op1=ALU.add), "v")
        p.body("vector", vcrow)
        k.mark("lcrow%d" % fc, "v")

        def vaugl(v, fc=fc):
            k.wait(v, "lsq%d" % fc)
            if fc > 0:
                k.wait(v, "lheads%d" % (fc - 1))
            k.inc(v.tensor_copy(out=p.qga[0:64, :], in_=p.bufD[0:64, 0:2048]), "v")
            k.inc(v.tensor_copy(out=p.kga[0:64, 0:64],
                                in_=p.bufD[0:64, 2560 + fc * 64:2560 + fc * 64 + 64]),
                  "v")
        p.body("vector", vaugl)
        k.mark("laugA%d" % fc, "v")

        def gaugl(g, fc=fc):
            k.dma_barrier(g, "dg")
            k.wait(g, "lsq%d" % fc)
            k.wait(g, "rkT")
            if fc > 0:
                k.wait(g, "lheads%d" % (fc - 1))
            k.inc(g.dma_start(out=p.qgb[0:64, :], in_=p.bufD[64:128, 0:2048]), "dg")
            k.inc(g.dma_start(out=p.kgb[0:64, 0:64],
                              in_=p.bufD[64:128, 2560 + fc * 64:2560 + fc * 64 + 64]),
                  "dg")
        p.body("gpsimd", gaugl)
        k.mark("laugB%d" % fc, "dg")

        def gcrowl(g, fc=fc):
            k.dma_barrier(g, "dg")
            k.wait(g, "lcrow%d" % fc)
            k.wait(g, "laugA%d" % fc)
            k.wait(g, "laugB%d" % fc)
            k.inc(g.dma_start(out=p.qga[64:65, :], in_=p.rows[0:1, :]), "dg")
            k.inc(g.dma_start(out=p.qgb[64:65, :], in_=p.rows[1:2, :]), "dg")
        p.body("gpsimd", gcrowl)
        k.mark("laug%d" % fc, "dg")

        for lhh in range(2):
            lh = 2 * fc + lhh
            qa = p.qga if lhh == 0 else p.qgb
            ka = p.kga if lhh == 0 else p.kgb
            tb5 = k.ctr["t"]
            sb5 = k.ctr["s"]
            vb5 = k.ctr["v"]

            def tlath(t, fc=fc, lh=lh, lhh=lhh, qa=qa, ka=ka, sb5=sb5, vb5=vb5):
                k.wait(t, "laug%d" % fc)
                k.wait(t, "laugB%d" % fc)
                if lh > 0:
                    k.wait(t, "lhnorm%d" % (lh - 1))
                k.wait(t, "rvaug")
                for sb in range(NSB):
                    if sb >= 3:
                        k.wv(t, "s", sb5 + sb - 2)
                    k.inc(t.matmul(
                        p.PS[sb % 3][0:64, 0:512],
                        ka[0:65, 0:64],
                        qa[0:65, sb * 512:(sb + 1) * 512],
                        start=True, stop=True), "t")
                for sb in range(NSB):
                    k.wv(t, "s", sb5 + sb + 1)
                    k.wv(t, "v", vb5 + sb + 1)
                    k.inc(t.matmul(
                        p.PS[4 + sb][0:65, 0:512],
                        p.bufC[0:64, 4096 + lh * 66:4096 + lh * 66 + 65],
                        p.smb[:, (sb % 3) * 512:(sb % 3) * 512 + 512][0:64, :],
                        start=True, stop=True), "t")
            p.body("tensor", tlath)
            tlat_end = k.ctr["t"]

            def slexp(s, fc=fc, lh=lh, tb5=tb5, vb5=vb5):
                for sb in range(NSB):
                    tw = tb5 + sb + 1
                    if sb >= 3:
                        tw = max(tw, tb5 + NSB + (sb - 3) + 1)
                        k.wv(s, "v", vb5 + sb - 2)
                    k.wv(s, "t", tw)
                    k.inc(s.activation(
                        out=p.smb[:, (sb % 3) * 512:(sb % 3) * 512 + 512][0:64, :],
                        in_=p.PS[sb % 3][0:64, 0:512], func=ACTF.Exp), "s")
            p.body("scalar", slexp)

            def vlmask(v, fc=fc, lh=lh, sb5=sb5):
                for sb in range(NSB):
                    k.wv(v, "s", sb5 + sb + 1)
                    k.inc(v.tensor_tensor(
                        out=p.smb[:, (sb % 3) * 512:(sb % 3) * 512 + 512][0:64, :],
                        in0=p.smb[:, (sb % 3) * 512:(sb % 3) * 512 + 512][0:64, :],
                        in1=MLAT[0:64, sb * 512:(sb + 1) * 512], op=ALU.mult), "v")
            p.body("vector", vlmask)
            k.mark("lav%d" % lh, "t")

            def vlnorm(v, lh=lh, fc=fc, tlat_end=tlat_end):
                k.wv(v, "t", tlat_end)
                k.wait(v, "laug%d" % fc)
                with p.nc.allow_low_precision(reason="1/l bf16 operand"):
                    for sb in range(NSB):
                        k.inc(v.tensor_scalar(
                            out=p.rows[0:1, sb * 512:(sb + 1) * 512],
                            in0=p.PS[4 + sb][64:65, 0:512],
                            scalar1=1e-30, scalar2=None, op0=ALU.add), "v")
                    c1 = k.ctr["v"]
                    v.wait_ge(p.sem["v"], c1)
                    for sb in range(NSB):
                        k.inc(v.reciprocal(
                            out=p.rows[0:1, sb * 512:(sb + 1) * 512],
                            in_=p.rows[0:1, sb * 512:(sb + 1) * 512]), "v")
            p.body("vector", vlnorm)
            k.mark("lhrec%d" % lh, "v")
            for sb in range(NSB):
                tb6 = k.ctr["t"]

                def tlbc(t, lh=lh, sb=sb):
                    k.wait(t, "lhrec%d" % lh)
                    if sb > 0:
                        k.wait(t, "lhbc%d_%d" % (lh, sb - 1))
                    k.inc(t.matmul(p.PS[3][0:64, 0:512],
                                   p.kc["m_causal"][0:1, 0:64],
                                   p.rows[0:1, sb * 512:(sb + 1) * 512],
                                   start=True, stop=True), "t")
                p.body("tensor", tlbc)
                tbc_done = k.ctr["t"]

                def vlbc(v, tbc_done=tbc_done, lh=lh, sb=sb, lhh=lhh, fc=fc):
                    k.wv(v, "t", tbc_done)
                    if lhh == 1 and sb > 0:
                        k.wait(v, "lhodd%d_%d" % (lh, sb - 1))
                    k.inc(v.tensor_copy(out=p.smb[0:64, 1024:1536],
                                        in_=p.PS[3][0:64, 0:512]), "v")
                    c1 = k.ctr["v"]
                    v.wait_ge(p.sem["v"], c1)
                    if lhh == 0:
                        dst = p.bufC[0:64, fc * S + sb * 512:fc * S + (sb + 1) * 512]
                    else:
                        dst = p.smb[0:64, 1536:2048]
                    k.inc(v.tensor_tensor(
                        out=dst, in0=p.PS[4 + sb][0:64, 0:512],
                        in1=p.smb[0:64, 1024:1536], op=ALU.mult), "v")
                p.body("vector", vlbc)
                k.mark("lhbc%d_%d" % (lh, sb), "v")

                if lhh == 1:
                    def glsc(g, sb=sb, lh=lh, fc=fc):
                        k.dma_barrier(g, "dg")
                        k.wait(g, "lhbc%d_%d" % (lh, sb))
                        k.inc(g.dma_start(
                            out=p.bufC[64:128, fc * S + sb * 512:
                                       fc * S + (sb + 1) * 512],
                            in_=p.smb[0:64, 1536:2048]), "dg")
                    p.body("gpsimd", glsc)
                    k.mark("lhodd%d_%d" % (lh, sb), "dg")
            k.mark("lhnorm%d" % lh)
        k.mark("lheads%d" % fc)
    k.mark("lat_outT")


    def qgw(q):
        k.dma_barrier(q, "dq")
        k.wait(q, "lat_outT")
        k.inc(q.dma_start(out=p.stat[0:64, 84:85], in_=p.W["gl_w"]), "dq")
        k.inc(q.dma_start(out=p.stat[64:128, 84:85], in_=p.W["gl_w"]), "dq")
        k.inc(q.dma_start(out=p.stat[0:64, 85:86], in_=p.W["gr_w"]), "dq")
        k.inc(q.dma_start(out=p.stat[64:128, 85:86], in_=p.W["gr_w"]), "dq")
        with p.nc.allow_non_contiguous_dma(reason="tiny rout_b gather"):
            k.inc(q.dma_start(out=p.stat[0:64, 0:16],
                              in_=p.W["rout_b"].rearrange("(h j) -> j h", j=HD)),
                  "dq")
        with p.nc.allow_non_contiguous_dma(reason="tiny rout_b col gather"):
            k.inc(q.dma_start(out=p.stat[:, 16:24],
                              in_=p.W["rout_b"].rearrange("(c p) -> p c", p=P)),
                  "dq")
        k.inc(q.dma_start(out=p.smf[0:1, 0:D], in_=p.W["out_b"].unsqueeze(0)), "dq")
    p.body("sync", qgw)
    k.mark("gw_dma", "dq")

    def vgw(v):
        k.wait(v, "gw_dma")
        k.inc(v.tensor_copy(out=p.smb[:, 1568:1569], in_=p.stat[:, 84:85]), "v")
        k.inc(v.tensor_copy(out=p.smb[:, 1569:1570], in_=p.stat[:, 85:86]), "v")
        k.inc(v.memset(p.smb[:, 1586:1590], 0.0), "v")
        c1 = k.ctr["v"]
        v.wait_ge(p.sem["v"], c1)
        k.inc(v.tensor_copy(out=p.smb[0:64, 1586:1587], in_=p.smb[0:64, 1568:1569]), "v")
        k.inc(v.tensor_copy(out=p.smb[64:128, 1587:1588], in_=p.smb[64:128, 1568:1569]), "v")
        k.inc(v.tensor_copy(out=p.smb[0:64, 1588:1589], in_=p.smb[0:64, 1569:1570]), "v")
        k.inc(v.tensor_copy(out=p.smb[64:128, 1589:1590], in_=p.smb[64:128, 1569:1570]), "v")
        k.inc(v.tensor_scalar(out=p.smb[0:64, 1570:1586], in0=p.stat[0:64, 0:16],
                              scalar1=p.stat[0:64, 85:86], scalar2=None,
                              op0=ALU.mult), "v")
        k.inc(v.tensor_copy(out=p.rows[0:1, 0:1024], in_=p.smf[0:1, 0:D]), "v")
    p.body("vector", vgw)
    k.mark("gw", "v")

    def trbg(t):
        k.wait(t, "gw")
        k.inc(t.matmul(p.PS[0][0:1, 0:16], KC["m_causal"][0:64, 127:128],
                       p.smb[0:64, 1570:1586], start=True, stop=True), "t")
    p.body("tensor", trbg)
    k.mark("rbg_mm", "t")

    def vrbg(v):
        k.wait(v, "rbg_mm")
        k.inc(v.tensor_copy(out=p.rows[0:1, 1024:1040], in_=p.PS[0][0:1, 0:16]), "v")
    p.body("vector", vrbg)
    k.mark("rbg", "v")

    for fc in range(2):
        src_ = p.W["rout_wT"][fc]
        p.load_cast(src_, 1024, 0, 0, "rwt%d" % fc,
                    wait="lat_outT" if fc == 0 else "rwt0",
                    wait_cast="lat_outT" if fc == 0 else "wrmm0")
        tbw = k.ctr["t"]

        def twr(t, fc=fc):
            k.wait(t, "rwt%d" % fc)
            k.wait(t, "gw")
            for h in range(NH):
                cc = h // 2
                off = (h % 2) * 64
                k.inc(t.matmul(
                    p.PS[1][:, fc * NH + h: fc * NH + h + 1],
                    p.wbf[off:off + 64, cc * P:(cc + 1) * P],
                    p.smb[off:off + 64, 1569:1570],
                    start=True, stop=True), "t")
        p.body("tensor", twr)
        k.mark("wrmm%d" % fc, "t")

    def vwr(v):
        k.wait(v, "wrmm1")
        k.inc(v.tensor_copy(out=p.smb[:, 1536:1568], in_=p.PS[1][:, 0:32]), "v")
    p.body("vector", vwr)
    k.mark("WR", "v")

    for sc in range(NSC):
        tbg = k.ctr["t"]

        def tgp(t, sc=sc):
            k.wait(t, "WR")
            k.wait(t, "rbg")
            k.wait(t, "l1a_localT")
            if sc >= 2:
                k.wait(t, "gsig%d" % (sc - 2))
            bank = 2 + sc % 2
            for c in range(NDC):
                k.inc(t.matmul(p.PS[bank][:, 2 * c:2 * c + 2],
                               p.bufB[:, c * S + sc * P: c * S + sc * P + P],
                               p.smb[:, 1586:1588],
                               start=(c == 0), stop=False), "t")
            for fc in range(2):
                k.inc(t.matmul(p.PS[bank][:, 0:16],
                               p.bufC[:, fc * S + sc * P: fc * S + sc * P + P],
                               p.smb[:, 1536 + fc * NH:1536 + fc * NH + NH],
                               start=False, stop=False), "t")
            k.inc(t.matmul(p.PS[bank][:, 0:16],
                           KC["m_causal"][0:1, 0:P],
                           p.rows[0:1, 1024:1040],
                           start=False, stop=True), "t")
        p.body("tensor", tgp)
        k.mark("gpre%d" % sc, "t")

        def sgs(s, sc=sc):
            k.wait(s, "gpre%d" % sc)
            bank = 2 + sc % 2
            k.inc(s.activation(out=p.smb[:, 128 + sc * NH:128 + sc * NH + NH],
                               in_=p.PS[bank][:, 0:16], func=ACTF.Sigmoid), "s")
        p.body("scalar", sgs)
        k.mark("gsig%d" % sc, "s")
    k.mark("gates_tm")


    for sc in range(NSC):
        tbt = k.ctr["t"]

        def tgt(t, sc=sc):
            k.wait(t, "gsig%d" % sc)
            if sc >= 2:
                k.wait(t, "gtT%d" % (sc - 2))
            k.inc(t.transpose(p.psb(4 + sc % 2, P)[0:NH, 0:P],
                              p.smb[:, 128 + sc * NH:128 + sc * NH + NH],
                              KC["eye"][:, :]), "t")
        p.body("tensor", tgt)
        gt_done = k.ctr["t"]

        def vgt(v, sc=sc, gt_done=gt_done):
            k.wv(v, "t", gt_done)
            k.inc(v.tensor_copy(out=p.bufD[0:NH, sc * P:(sc + 1) * P],
                                in_=p.psb(4 + sc % 2, P)[0:NH, 0:P]), "v")
        p.body("vector", vgt)
        k.mark("gtT%d" % sc, "v")
    k.mark("gateT")

    for c in range(NDC):
        src_ = p.W["rout_w"][c]
        p.load_cast(src_, 256, 0, 0, "row%d" % c,
                    wait=("mix%d" % (c - 1)) if c else "gateT",
                    wait_cast=("mix%d" % (c - 1)) if c else "gateT")
        for sb in range(NSB):
            tbr = k.ctr["t"]

            def tre(t, c=c, sb=sb):
                k.wait(t, "row%d" % c)
                if sb == 0 and c == 0:
                    k.wait(t, "gateT")
                if "mixv%d_%d" % (c, sb - 1) in k.marks:
                    k.wait(t, "mixv%d_%d" % (c, sb - 1))
                elif c > 0:
                    k.wait(t, "mixv%d_%d" % (c - 1, NSB - 1))
                for fc in range(2):
                    k.inc(t.matmul(p.PS[0][:, 0:512],
                                   p.wbf[:, fc * P:(fc + 1) * P],
                                   p.bufC[:, fc * S + sb * 512: fc * S + (sb + 1) * 512],
                                   start=(fc == 0), stop=(fc == 1)), "t")
                SEL = p.kc["rope_sin"]
                k.inc(t.matmul(p.PS[1][0:64, 0:512],
                               SEL[0:G, 1024 + (2 * c) * 64:1024 + (2 * c) * 64 + 64],
                               p.bufD[0:NH, sb * 512:(sb + 1) * 512],
                               start=True, stop=True), "t")
                k.inc(t.matmul(p.PS[1][64:128, 0:512],
                               SEL[0:G, 1024 + (2 * c + 1) * 64:1024 + (2 * c + 1) * 64 + 64],
                               p.bufD[0:NH, sb * 512:(sb + 1) * 512],
                               start=True, stop=True, tile_position=(0, 64)), "t")
            p.body("tensor", tre)
            k.mark("rem%d_%d" % (c, sb), "t")

            def sre(s, c=c, sb=sb):
                k.wait(s, "rem%d_%d" % (c, sb))
                k.inc(s.activation(out=p.smb[:, 0:512], in_=p.PS[0][:, 0:512],
                                   func=ACTF.Identity, bias=p.stat[:, 16 + c:17 + c]),
                      "s")
            p.body("scalar", sre)
            k.mark("remb%d_%d" % (c, sb), "s")

            def vmix(v, c=c, sb=sb):
                k.wait(v, "remb%d_%d" % (c, sb))
                k.inc(v.tensor_tensor(
                    out=p.smb[:, 512:1024],
                    in0=p.bufB[:, c * S + sb * 512: c * S + (sb + 1) * 512],
                    in1=p.smb[:, 0:512], op=ALU.subtract), "v")
                c1 = k.ctr["v"]
                v.wait_ge(p.sem["v"], c1)
                k.inc(v.tensor_tensor(
                    out=p.smb[:, 1024:1536], in0=p.PS[1][:, 0:512],
                    in1=p.smb[:, 512:1024], op=ALU.mult), "v")
                c2 = k.ctr["v"]
                v.wait_ge(p.sem["v"], c2)
                k.inc(v.tensor_tensor(
                    out=p.xT[:, c * S + sb * 512: c * S + (sb + 1) * 512],
                    in0=p.smb[:, 0:512], in1=p.smb[:, 1024:1536], op=ALU.add), "v")
            p.body("vector", vmix)
            k.mark("mixv%d_%d" % (c, sb), "v")
        k.mark("mix%d" % c, "v")
    k.mark("mixedT")

    for ic in range(NDC):
        src_ = p.W["out_w"][ic * P:(ic + 1) * P, :]
        p.load_cast(src_, 1024, (ic % 2) * 1024, 0, "ow_st%d" % ic,
                    wait=("ow_mv%d" % (ic - 2)) if ic >= 2 else "mixedT",
                    wait_cast=("ow_mv%d" % (ic - 1)) if ic >= 1 else "mixedT")

        def vmv(v, ic=ic):
            k.wait(v, "ow_st%d" % ic)
            if ic == 0:
                k.wait(v, "lat_outT")
                k.wait(v, "mixedT")
            k.inc(v.tensor_copy(out=p.bufC[:, ic * 1024:(ic + 1) * 1024],
                                in_=p.wbf[:, 0:1024]), "v")
        p.body("vector", vmv)
        k.mark("ow_mv%d" % ic, "v")
    k.mark("ow_res", "v")

    ph_proj_res_ln(p, "l1a", lambda ic: p.xT[:, ic * S:(ic + 1) * S],
                   lambda ic: p.bufC[:, ic * 1024:(ic + 1) * 1024],
                   "ow_res", "mixedT", dbg_pre="d_x3",
                   extra_bias_row=p.rows[0:1, 0:1024])




def ph_l1_ffn(p):
    k = p
    KC = p.kc

    emit_ln_phase(p, "zn",
                  apply_out=lambda sc: p.bufB[:, sc * D:(sc + 1) * D],
                  src_mark="l1a_ln")
    k.mark("zn_done", "v")

    p.marks["l1f_xT_src"] = dict(p.marks["zn_done"])
    ph_transpose(p, "l1f_xT", do_cast=False)

    def vzs(v):
        k.wait(v, "l1f_xT")
        for dc in range(NDC):
            k.inc(v.reduce_sum(out=p.stat[:, 32 + dc:33 + dc],
                               in_=p.xT[:, dc * S:(dc + 1) * S], axis=AX), "v")
    p.body("vector", vzs)
    k.mark("zsent", "v")

    def qpar(q):
        k.dma_barrier(q, "dq")
        k.wait(q, "zn_done")
        with p.nc.allow_non_contiguous_dma(reason="tiny param gathers"):
            k.inc(q.dma_start(out=p.stat[:, 40:48],
                              in_=p.W["ln_g"].rearrange("(c p) -> p c", p=P)), "dq")
            k.inc(q.dma_start(out=p.stat[:, 48:56],
                              in_=p.W["ln_b"].rearrange("(c p) -> p c", p=P)), "dq")
            k.inc(q.dma_start(out=p.stat[:, 56:64],
                              in_=p.W["bn_g"].rearrange("(c p) -> p c", p=P)), "dq")
            k.inc(q.dma_start(out=p.stat[:, 64:72],
                              in_=p.W["bn_b"].rearrange("(c p) -> p c", p=P)), "dq")
            k.inc(q.dma_start(out=p.stat[:, 72:80],
                              in_=p.W["bn_mean"].rearrange("(c p) -> p c", p=P)), "dq")
            k.inc(q.dma_start(out=p.stat[:, 80:88],
                              in_=p.W["bn_var"].rearrange("(c p) -> p c", p=P)), "dq")
            k.inc(q.dma_start(out=p.stat[:, 16:20],
                              in_=p.W["bd"].rearrange("(c p) -> p c", p=P)), "dq")
        k.inc(q.dma_start(out=p.smf[0:1, 0:E], in_=p.W["br"].unsqueeze(0)), "dq")
        k.inc(q.dma_start(out=p.wst[0:E, 0:D], in_=p.W["bu"]), "dq")
    p.body("sync", qpar)
    k.mark("l1f_par", "dq")

    def vpar(v):
        k.wait(v, "l1f_par")
        k.inc(v.tensor_copy(out=p.smb[:, 16:24], in_=p.stat[:, 48:56]), "v")
        k.inc(v.tensor_copy(out=p.smb[:, 24:32], in_=p.stat[:, 40:48]), "v")
        k.inc(v.tensor_copy(out=p.rows[0:1, 1040:1048], in_=p.smf[0:1, 0:E]), "v")
        k.inc(v.tensor_copy(out=p.smb[0:E, 1024:2048], in_=p.wst[0:E, 0:D]), "v")
    p.body("vector", vpar)
    k.mark("l1f_parb", "v")

    def vzb(v):
        k.wait(v, "zsent")
        k.wait(v, "l1f_parb")
        k.inc(v.tensor_scalar(out=p.stat[:, 0:8], in0=p.stat[:, 32:40],
                              scalar1=1.0 / S, scalar2=None, op0=ALU.mult), "v")
        c1 = k.ctr["v"]
        v.wait_ge(p.sem["v"], c1)
        k.inc(v.tensor_tensor(out=p.stat[:, 0:8], in0=p.stat[:, 0:8],
                              in1=p.stat[:, 40:48], op=ALU.mult), "v")
        c2 = k.ctr["v"]
        v.wait_ge(p.sem["v"], c2)
        k.inc(v.tensor_tensor(out=p.stat[:, 0:8], in0=p.stat[:, 0:8],
                              in1=p.stat[:, 48:56], op=ALU.add), "v")
        k.inc(v.tensor_scalar(out=p.stat[:, 8:16], in0=p.stat[:, 80:88],
                              scalar1=EPS, scalar2=None, op0=ALU.add), "v")
    p.body("vector", vzb)
    k.mark("zb1", "v")

    def szb(s):
        k.wait(s, "zb1")
        k.inc(s.activation(out=p.stat[:, 8:16], in_=p.stat[:, 8:16],
                           func=ACTF.Sqrt), "s")
    p.body("scalar", szb)
    k.mark("zb2", "s")

    def vzb2(v):
        k.wait(v, "zb2")
        k.inc(v.reciprocal(out=p.stat[:, 8:16], in_=p.stat[:, 8:16]), "v")
        c1 = k.ctr["v"]
        v.wait_ge(p.sem["v"], c1)
        k.inc(v.tensor_tensor(out=p.stat[:, 0:8], in0=p.stat[:, 0:8],
                              in1=p.stat[:, 72:80], op=ALU.subtract), "v")
        c2 = k.ctr["v"]
        v.wait_ge(p.sem["v"], c2)
        k.inc(v.tensor_tensor(out=p.stat[:, 0:8], in0=p.stat[:, 0:8],
                              in1=p.stat[:, 8:16], op=ALU.mult), "v")
        c3 = k.ctr["v"]
        v.wait_ge(p.sem["v"], c3)
        k.inc(v.tensor_tensor(out=p.stat[:, 0:8], in0=p.stat[:, 0:8],
                              in1=p.stat[:, 56:64], op=ALU.mult), "v")
        c4 = k.ctr["v"]
        v.wait_ge(p.sem["v"], c4)
        k.inc(v.tensor_tensor(out=p.stat[:, 0:8], in0=p.stat[:, 0:8],
                              in1=p.stat[:, 64:72], op=ALU.add), "v")
        c5 = k.ctr["v"]
        v.wait_ge(p.sem["v"], c5)
        with p.nc.allow_low_precision(reason="bf16 matmul operand"):
            k.inc(v.tensor_copy(out=p.smb[:, 32:40], in_=p.stat[:, 0:8]), "v")
    p.body("vector", vzb2)
    k.mark("zbn", "v")

    src_ = p.W["wr"].rearrange("(c p) e -> p c e", p=P)
    p.load_cast(src_, NDC * E, 0, 1024, "wrw", wait="l1f_parb",
                wait_cast="l1f_parb", blocks=NDC)

    def tlg(t):
        k.wait(t, "wrw")
        k.wait(t, "zbn")
        k.wait(t, "l1f_xT")
        for dc in range(NDC):
            k.inc(t.matmul(p.PS[0][0:1, 0:E],
                           p.smb[:, 32 + dc:33 + dc],
                           p.wbf[:, 1024 + dc * E:1024 + (dc + 1) * E],
                           start=(dc == 0), stop=False), "t")
        k.inc(t.matmul(p.PS[0][0:1, 0:E],
                       KC["m_causal"][0:1, 0:1],
                       p.rows[0:1, 1040:1048],
                       start=False, stop=True), "t")
    p.body("tensor", tlg)
    k.mark("l1f_lg", "t")

    def vlg(v):
        k.wait(v, "l1f_lg")
        k.inc(v.reduce_max(out=p.stat[0:1, 88:89], in_=p.PS[0][0:1, 0:E],
                           axis=AX, negate=True), "v")
        k.inc(v.reduce_max(out=p.stat[0:1, 89:90], in_=p.PS[0][0:1, 0:E],
                           axis=AX), "v")
    p.body("vector", vlg)
    k.mark("l1f_lgm", "v")

    def slg(s):
        k.wait(s, "l1f_lgm")
        k.inc(s.activation(out=p.smf[0:1, 0:E], in_=p.PS[0][0:1, 0:E],
                           func=ACTF.Exp, bias=p.stat[0:1, 88:89],
                           accum_out=p.stat[0:1, 90:91]), "s")
    p.body("scalar", slg)
    k.mark("l1f_lge", "s")

    def vlg2(v):
        k.wait(v, "l1f_lge")
        k.inc(v.reciprocal(out=p.stat[0:1, 91:92], in_=p.stat[0:1, 90:91]), "v")
        k.inc(v.tensor_scalar(out=p.smb[0:1, 0:E], in0=p.PS[0][0:1, 0:E],
                              scalar1=p.stat[0:1, 89:90], scalar2=None,
                              op0=ALU.is_equal), "v")
        c1 = k.ctr["v"]
        v.wait_ge(p.sem["v"], c1)
        with p.nc.allow_low_precision(reason="gate bf16 bcast operand"):
            k.inc(v.tensor_copy(out=p.smb[0:1, 40:41], in_=p.stat[0:1, 91:92]), "v")
    p.body("vector", vlg2)
    k.mark("l1f_gate", "v")

    def tmt(t):
        k.wait(t, "l1f_gate")
        k.inc(t.transpose(p.psb(1, 1)[0:E, 0:1], p.smb[0:1, 0:E],
                          KC["eye"][0:1, 0:1]), "t")
    p.body("tensor", tmt)
    k.mark("l1f_mtT", "t")

    def vmt(v):
        k.wait(v, "l1f_mtT")
        k.inc(v.tensor_copy(out=p.smb[0:E, 8:9], in_=p.psb(1, 1)[0:E, 0:1]), "v")
    p.body("vector", vmt)
    k.mark("l1f_mt", "v")

    tmc_b = k.ctr["t"]

    def tmc(t):
        k.wait(t, "l1f_mt")
        for j in range(4):
            k.inc(t.matmul(p.PS[2][:, j:j + 1],
                           KC["exp8"][0:E, j * P:(j + 1) * P],
                           p.smb[0:E, 8:9], start=True, stop=True), "t")
    p.body("tensor", tmc)
    k.mark("l1f_mc_mm", "t")

    def vmc(v):
        k.wait(v, "l1f_mc_mm")
        k.inc(v.tensor_copy(out=p.stat[:, 32:36], in_=p.PS[2][:, 0:4]), "v")
    p.body("vector", vmc)
    k.mark("l1f_mc", "v")

    def tgb(t):
        k.wait(t, "l1f_gate")
        k.wait(t, "l1f_mc_mm")
        k.inc(t.matmul(p.PS[3][:, 0:1], KC["m_causal"][0:1, 0:P],
                       p.smb[0:1, 40:41], start=True, stop=True), "t")
    p.body("tensor", tgb)
    k.mark("l1f_gb_mm", "t")

    def vgb(v):
        k.wait(v, "l1f_gb_mm")
        k.inc(v.tensor_copy(out=p.stat[:, 95:96], in_=p.PS[3][:, 0:1]), "v")
    p.body("vector", vgb)
    k.mark("l1f_gb", "v")

    for dc in range(NDC):
        src_ = p.W["wd"][dc * P:(dc + 1) * P, :]
        p.load_cast(src_, 512, (dc % 4) * 512, dc * 512, "wdl%d" % dc,
                    wait=("wdl%d" % (dc - 4)) if dc >= 4 else "wrw",
                    wait_cast="wrw")
    for hj in range(4):
        tbx = k.ctr["t"]

        def tbx_f(t, hj=hj):
            k.wait(t, "wdl%d" % (NDC - 1))
            k.wait(t, "l1f_parb")
            k.wait(t, "l1f_mc")
            for dc in range(NDC):
                k.inc(t.matmul(p.PS[2][:, hj:hj + 1],
                               p.wbf[:, dc * 512 + hj * P: dc * 512 + hj * P + P],
                               p.smb[:, 16 + dc:17 + dc],
                               start=(dc == 0), stop=(dc == NDC - 1)), "t")
        p.body("tensor", tbx_f)
    k.mark("l1f_bdx_mm", "t")

    def vbdx(v):
        k.wait(v, "l1f_bdx_mm")
        k.inc(v.tensor_tensor(out=p.stat[:, 16:20], in0=p.PS[2][:, 0:4],
                              in1=p.stat[:, 16:20], op=ALU.add), "v")
    p.body("vector", vbdx)
    k.mark("l1f_bdtot", "v")

    def vwds(v):
        k.wait(v, "wdl%d" % (NDC - 1))
        k.wait(v, "l1f_bdx_mm")
        k.wait(v, "l1f_parb")
        for dc in range(NDC):
            k.inc(v.tensor_scalar(out=p.wbf[:, dc * 512:(dc + 1) * 512],
                                  in0=p.wbf[:, dc * 512:(dc + 1) * 512],
                                  scalar1=p.stat[:, 40 + dc:41 + dc],
                                  scalar2=None, op0=ALU.mult), "v")
    p.body("vector", vwds)
    k.mark("l1f_wdp", "v")

    for hj in range(4):
        for sb in range(NSB):
            tbh = k.ctr["t"]

            def th(t, hj=hj, sb=sb):
                k.wait(t, "l1f_wdp")
                k.wait(t, "l1f_xT")
                ob = "l1f_hm%d_%d" % (hj, sb - 1) if sb > 0 else (
                    "l1f_hm%d_%d" % (hj - 1, NSB - 1) if hj > 0 else None)
                if ob is not None and ob in k.marks:
                    pass
                bank = 4 + (sb % 2)
                if "l1f_hrl%d" % bank in k.marks:
                    k.wait(t, "l1f_hrl%d" % bank)
                for dc in range(NDC):
                    k.inc(t.matmul(p.PS[bank][:, 0:512],
                                   p.wbf[:, dc * 512 + hj * P: dc * 512 + hj * P + P],
                                   p.xT[:, dc * S + sb * 512: dc * S + (sb + 1) * 512],
                                   start=(dc == 0), stop=(dc == NDC - 1)), "t")
            p.body("tensor", th)
            th_done = k.ctr["t"]

            def sh(s, hj=hj, sb=sb, th_done=th_done):
                k.wv(s, "t", th_done)
                k.wait(s, "l1f_bdtot")
                bank = 4 + (sb % 2)
                k.inc(s.activation(
                    out=p.bufC[:, hj * S + sb * 512: hj * S + (sb + 1) * 512],
                    in_=p.PS[bank][:, 0:512], func=ACTF.Relu,
                    bias=p.stat[:, 16 + hj:17 + hj]), "s")
            p.body("scalar", sh)
            k.mark("l1f_hr%d_%d" % (hj, sb), "s")
            k.mark("l1f_hrl%d" % (4 + sb % 2), "s")

            def vh(v, hj=hj, sb=sb):
                k.wait(v, "l1f_hr%d_%d" % (hj, sb))
                k.wait(v, "l1f_mc")
                k.inc(v.tensor_scalar(
                    out=p.bufC[:, hj * S + sb * 512: hj * S + (sb + 1) * 512],
                    in0=p.bufC[:, hj * S + sb * 512: hj * S + (sb + 1) * 512],
                    scalar1=p.stat[:, 32 + hj:33 + hj], scalar2=None,
                    op0=ALU.mult), "v")
            p.body("vector", vh)
            k.mark("l1f_hm%d_%d" % (hj, sb), "v")
    k.mark("l1f_hm")

    for hj in range(4):
        src_ = p.W["wu"][hj * P:(hj + 1) * P, :]
        p.load_cast(src_, 1024, (hj % 2) * 1024, hj * 1024, "wul%d" % hj,
                    wait=("wul%d" % (hj - 2)) if hj >= 2 else "l1f_hm",
                    wait_cast="l1f_hm" if hj == 0 else None)
    tbu = k.ctr["t"]

    def tbu_f(t):
        k.wait(t, "l1f_mt")
        k.wait(t, "l1f_parb")
        k.wait(t, "l1f_gb_mm")
        for half in range(2):
            k.inc(t.matmul(p.PS[6 + half][0:1, 0:512],
                           p.smb[0:E, 8:9],
                           p.smb[0:E, 1024 + half * 512:1024 + (half + 1) * 512],
                           start=True, stop=True), "t")
    p.body("tensor", tbu_f)
    k.mark("l1f_busel_mm", "t")

    def vbu(v, tbu=tbu):
        k.wait(v, "l1f_busel_mm")
        k.inc(v.tensor_copy(out=p.rows[0:1, 1024:1536], in_=p.PS[6][0:1, 0:512]), "v")
        k.inc(v.tensor_copy(out=p.rows[0:1, 1536:2048], in_=p.PS[7][0:1, 0:512]), "v")
    p.body("vector", vbu)
    k.mark("l1f_busel", "v")

    for sc in range(NSC):
        tbo = k.ctr["t"]

        def to_f(t, sc=sc):
            k.wait(t, "wul3")
            k.wait(t, "l1f_hm")
            k.wait(t, "l1f_busel")
            if sc >= 1:
                k.wait(t, "l1f_radd%d" % (sc - 1))
            for half in range(2):
                for hj in range(4):
                    k.inc(t.matmul(
                        p.PS[half][:, 0:512],
                        p.bufC[:, hj * S + sc * P: hj * S + sc * P + P],
                        p.wbf[:, hj * 1024 + half * 512: hj * 1024 + (half + 1) * 512],
                        start=(hj == 0), stop=False), "t")
                k.inc(t.matmul(
                    p.PS[half][:, 0:512],
                    KC["m_causal"][0:1, 0:P],
                    p.rows[0:1, 1024 + half * 512:1024 + (half + 1) * 512],
                    start=False, stop=True), "t")
        p.body("tensor", to_f)
        mm_done = k.ctr["t"]

        def vro(v, sc=sc, mm_done=mm_done):
            k.wv(v, "t", mm_done)
            k.wait(v, "l1f_gb")
            for half in range(2):
                sl = slice(sc * D + half * 512, sc * D + (half + 1) * 512)
                k.inc(v.scalar_tensor_tensor(
                    out=p.xs[:, sl],
                    in0=p.PS[half][:, 0:512],
                    scalar=p.stat[:, 95:96],
                    in1=p.xs[:, sl],
                    op0=ALU.mult, op1=ALU.add), "v")
        p.body("vector", vro)
        k.mark("l1f_radd%d" % sc, "v")
    emit_dbg_dumps(p, "l1f", "d_x4")
    emit_ln_phase(p, "l1f", radd_mark="l1f", dbg_name="d_x4" if p.debug else None)


def build_program(consts, debug=False):
    p = Prog(consts, debug=debug)
    import contextlib

    def scope(name):
        return p.nc.named_scope(name)

    with p.nc.Block() as blk:
        p.blk = blk
        with scope("ph_input"):
            ph_input(p)
            p.mark("l0a_xT_src", "dq")
            p.marks["l0a_xT_src"] = p.marks["input"]
            ph_transpose(p, "l0a_xT")
        with scope("ph_l0_attn"):
            ph_l0_attn(p)
        if debug:
            p.dump("d_x1", lambda t: p.xs[:, t * D:(t + 1) * D], "l0a_ln")
        with scope("ph_l0_ffn"):
            p.marks["l0f_xT_src"] = dict(p.marks["l0a_ln"])
            if debug:
                p.marks["l0f_xT_src"]["dg"] = p.ctr["dg"]
            if debug:
                ph_transpose(p, "l0f_xT")
            else:
                ph_transpose(p, "l0f_xT",
                             tile_marks=lambda t: "l0a_ln%d" % t)
            ph_l0_ffn(p)
        with scope("ph_l1_attn"):
            p.marks["l1a_xT_src"] = dict(p.marks["l0f_ln"])
            p.marks["l1a_xT_src"]["dg"] = p.ctr["dg"]
            if debug:
                ph_transpose(p, "l1a_xT")
            else:
                ph_transpose(p, "l1a_xT",
                             tile_marks=lambda t: "l0f_ln%d" % t)
            ph_l1_attn(p)
        with scope("ph_l1_latmix"):
            ph_l1_latmix(p)
        with scope("ph_l1_ffn"):
            ph_l1_ffn(p)

        def gout(g):
            p.dma_barrier(g, "dg")
            p.wait(g, "l1f_ln")
            for t in range(NSC):
                p.inc(g.dma_start(out=p.out_h[t * P:(t + 1) * P, :],
                                  in_=p.xs[:, t * D:(t + 1) * D]), "o")
            g.wait_ge(p.sem["o"], p.ctr["o"])
        p.body("gpsimd", gout)
    return p


def kernel(**inputs):
    from concourse.bass_utils import run_bass_kernel_spmd
    consts = _host_consts()
    w = _prep_inputs(inputs)
    p = build_program(consts, debug=False)
    x = np.ascontiguousarray(np.asarray(inputs["L0_attn_x"], dtype=np.float32))
    in_maps = []
    for b in range(8):
        m = {"x": np.ascontiguousarray(x[b])}
        m.update(w)
        in_maps.append(m)
    res = run_bass_kernel_spmd(p.nc, in_maps, core_ids=list(range(8)))
    return np.stack([r["out"] for r in res.results]).astype(np.float32)
